# version 01
# baseline (speedup 1.0000x reference)
import sys
import contextlib
import numpy as np
import ml_dtypes

sys.path.insert(0, "/opt/trn_rl_repo")
import concourse.bass as bass
import concourse.mybir as mybir
from concourse.bass_utils import run_bass_kernel_spmd

F32 = mybir.dt.float32
BF16 = mybir.dt.bfloat16
AF = mybir.ActivationFunctionType
ALU = mybir.AluOpType
AX = mybir.AxisListType

D = 1024
T = 2048
C = 8
NT = 16
H = 16
DK = 64
E = 8
HID = 4096
SEQ = 4096
EPS_LN = 1e-5
EPS_LA = 1e-6
INV_D = 1.0 / D
LN_EIGHTH = float(np.log(0.125))


class Ctr:
    def __init__(self, sem):
        self.sem = sem
        self.n = 0

    def inc(self, instr, k=1):
        instr.then_inc(self.sem, k)
        self.n += k
        return self.n


class B:

    def __init__(self, nc, upto):
        self.nc = nc
        self.upto = upto
        self.ev = {}

    def mark(self, key, instr, ctr, k=1):
        ctr.inc(instr, k)
        self.ev[key] = (ctr, ctr.n)

    def wait(self, eng, *keys):
        for key in keys:
            if key is None:
                continue
            e = self.ev.get(key) if isinstance(key, str) else key
            if e is None:
                continue
            ctr, n = e
            if n > 0:
                eng.wait_ge(ctr.sem, n)

    def declare(self):
        nc = self.nc

        def din(name, shape, dt=F32):
            return nc.dram_tensor(name, shape, dt, kind="ExternalInput")

        self.x_in = din("x_in", [D, T])
        self.awq = din("awq", [D, D]); self.awk = din("awk", [D, D]); self.awv = din("awv", [D, D])
        self.awo = din("awo", [D, D])
        self.wqkv = din("wqkv", [D, 3 * D])
        self.wproj = din("wproj", [D, D])
        self.biascols = din("biascols", [D, 8])
        self.gate_w = din("gate_w", [D, E])
        self.gate_b = din("gate_b", [1, E])
        self.w1 = din("w1", [E, D, HID])
        self.w2 = din("w2", [E, HID, D])
        self.ones_f = din("ones_f", [1, 128])
        self.negones_f = din("negones_f", [1, 128])
        self.ones_bf = din("ones_bf", [1, 128], BF16)
        self.ones65_f = din("ones65_f", [65, 128])
        self.onescol_bf = din("onescol_bf", [128, 1], BF16)
        self.onesm_bf = din("onesm_bf", [128, 128], BF16)
        self.ident = din("ident", [128, 128])
        self.ind8 = din("ind8", [E, E * 128])

        self.x_out = nc.dram_tensor("x_out", [D, T], F32, kind="ExternalOutput")
        if self.upto != "full":
            self.dbg = nc.dram_tensor("dbg", [D, T], F32, kind="ExternalOutput")

        self.xdram = nc.dram_tensor("xdram", [D, T], F32)
        self.agk_in = nc.dram_tensor("agk_in", [D, T], BF16)
        self.agk_out = nc.dram_tensor("agk_out", [2 * D, T], BF16)
        self.agv_in = nc.dram_tensor("agv_in", [T, D], BF16)
        self.agv_out = nc.dram_tensor("agv_out", [2 * T, D], BF16)
        self.agm_in = nc.dram_tensor("agm_in", [1, H], F32)
        self.agm_out = nc.dram_tensor("agm_out", [2, H], F32)
        self.arl_in = nc.dram_tensor("arl_in", [128, 16], F32)
        self.arl_out = nc.dram_tensor("arl_out", [128, 16], F32)

    def alloc(self, st):
        nc = self.nc

        def sb(name, shape, dt):
            return st.enter_context(nc.sbuf_tensor(name, shape, dt))

        def psum(name, shape, dt=F32):
            return st.enter_context(nc.psum_tensor(name, shape, dt))

        def sem(name):
            return Ctr(st.enter_context(nc.semaphore(name)))

        self.sb_ = sb
        self.stream2 = [sb(f"stream2_{c}", [128, T], BF16) for c in range(C)]
        self.mm_in = [sb(f"mm_in_{c}", [128, T], BF16) for c in range(C)]
        self.scr = [sb(f"scr_{i}", [128, T], F32) for i in range(2)]
        self.xb = [sb(f"xb_{i}", [128, T], BF16) for i in range(2)]
        self.sqb = [sb(f"sqb_{i}", [128, T], BF16) for i in range(2)]
        self.rows = sb("rows", [65, T], F32)
        self.maskfm = sb("maskfm", [E, T], F32)
        self.maske = sb("maske", [128, T], F32)
        self.w1b = [sb(f"w1b_{i}", [128, 8 * 128], BF16) for i in range(2)]
        self.wsm = [sb(f"wsm_{i}", [128, D], BF16) for i in range(2)]
        self.cols8 = sb("cols8", [128, 64], F32)
        self.cols8n = sb("cols8n", [128, 24], F32)
        self.lincols = sb("lincols", [128, 32], F32)
        self.onesc = sb("onesc", [128, 1], BF16)
        self.onesf = sb("onesf", [1, 128], F32)
        self.negonesf = sb("negonesf", [1, 128], F32)
        self.onesbf = sb("onesbf", [1, 128], BF16)
        self.ones65 = sb("ones65", [65, 128], F32)
        self.onesm = sb("onesm", [128, 128], BF16)
        self.identt = sb("identt", [128, 128], F32)
        self.ind8t = sb("ind8t", [E, E * 128], F32)
        self.gwb = sb("gwb", [128, C * E], BF16)
        self.gbb = sb("gbb", [1, E], BF16)
        self.hrow = sb("hrow", [1, 256], F32)
        self.ebias = sb("ebias", [128, H], F32)

        self.pA = psum("pA", [128, T])
        self.pB = psum("pB", [128, T])

        self.s_sy = sem("s_sy")
        self.s_g = sem("s_g")
        self.s_pe = sem("s_pe")
        self.s_ac = sem("s_ac")
        self.s_ve = sem("s_ve")
        self.d_cs = sem("d_cs")
        self.d_cg = sem("d_cg")
        self.d_xb = [sem(f"d_xb{i}") for i in range(2)]
        self.d_w1b = [sem(f"d_w1b{i}") for i in range(2)]
        self.d_wsm = [sem(f"d_wsm{i}") for i in range(2)]
        self.d_kt = [sem(f"d_kt{i}") for i in range(2)]
        self.d_vt = [sem(f"d_vt{i}") for i in range(4)]
        self.d_kd = [sem(f"d_kd{i}") for i in range(2)]
        self.d_vd = [sem(f"d_vd{i}") for i in range(2)]
        self.d_msc = [sem(f"d_msc{i}") for i in range(2)]
        self.d_rr = [sem(f"d_rr{i}") for i in range(2)]
        self.d_rw = [sem(f"d_rw{i}") for i in range(2)]
        self.d_vsl = sem("d_vsl")
        self.d_w2 = sem("d_w2")
        self.c_k = sem("c_k")
        self.c_m = sem("c_m")
        self.c_v = sem("c_v")
        self.c_l = sem("c_l")

        self.colmap = dict(abq=0, abk=1, abv=2, abo=3, ln1w=4, ln1b=5, ln2w=6, ln2b=7)

    def col(self, name, c):
        o = c * 8 + self.colmap[name]
        return self.cols8[:, o:o + 1]

    def ncol(self, name, c):
        o = c * 3 + dict(abq=0, abk=1, abv=2)[name]
        return self.cols8n[:, o:o + 1]

    def p_consts(self, block):
        b = self

        @block.sync
        def _(s):
            for out, in_ in [
                (b.xdram.ap(), b.x_in.ap()),
                (b.onesf[:], b.ones_f.ap()),
                (b.negonesf[:], b.negones_f.ap()),
                (b.onesbf[:], b.ones_bf.ap()),
                (b.ones65[:], b.ones65_f.ap()),
                (b.onesc[:], b.onescol_bf.ap()),
                (b.onesm[:], b.onesm_bf.ap()),
                (b.identt[:], b.ident.ap()),
                (b.ind8t[:], b.ind8.ap()),
            ]:
                i = s.dma_start(out=out, in_=in_)
                b.d_cs.inc(i, 16)
            i = s.dma_start(
                out=b.cols8[:].rearrange("p (c x) -> p c x", c=C),
                in_=b.biascols.ap().rearrange("(c p) x -> p c x", p=128),
            )
            b.mark("consts_sy", i, b.d_cs, 16)

        @block.gpsimd
        def _(g):
            i = g.dma_start(
                out=b.gwb[:].rearrange("p (c e) -> p c e", c=C),
                in_=b.gate_w.ap().rearrange("(c p) e -> p c e", p=128),
            )
            b.d_cg.inc(i, 16)
            i = g.dma_start(out=b.gbb[:], in_=b.gate_b.ap())
            b.mark("consts_g", i, b.d_cg, 16)

        @block.vector
        def _(v):
            b.wait(v, "consts_sy")
            for c in range(C):
                for name in ("abq", "abk", "abv"):
                    i = v.tensor_scalar(b.ncol(name, c), b.col(name, c), -1.0, None, ALU.mult)
            b.mark("consts_v", i, b.s_ve)

    def p_ln(self, block, tag, src, dst, affine=None, deps=()):
        b = self
        psfree = b.ev.get("psfree")
        depf = deps if callable(deps) else (lambda c: deps)

        for c in range(C):
            i2 = c % 2
            if src == "x":
                @block.gpsimd
                def _(g, c=c, i2=i2):
                    b.wait(g, f"{tag}_st_{c-2}", f"{tag}_sq_{c-2}", *depf(c))
                    i = g.dma_start(out=b.xb[i2][:], in_=b.xdram.ap()[c * 128:(c + 1) * 128, :])
                    b.mark(f"{tag}_xb_{c}", i, b.d_xb[i2], 16)
                srcb = b.xb[i2]
                srcdep = f"{tag}_xb_{c}"
            else:
                srcb = b.stream2[c]
                srcdep = None

            @block.scalar
            def _(s, c=c, i2=i2, srcb=srcb, srcdep=srcdep):
                b.wait(s, srcdep, f"{tag}_st_{c-2}", *(depf(c) if src != "x" else ()))
                i = s.activation(b.sqb[i2][:], srcb[:], AF.Square)
                b.mark(f"{tag}_sq_{c}", i, b.s_ac)

            @block.tensor
            def _(t, c=c, i2=i2, srcb=srcb, srcdep=srcdep):
                b.wait(t, srcdep, f"{tag}_sq_{c}")
                if c == 0:
                    b.wait(t, psfree)
                for n in range(4):
                    t.matmul(b.pA[:, n * 512:(n + 1) * 512], b.onesm[:],
                             srcb[:, n * 512:(n + 1) * 512], start=(c == 0), stop=(c == C - 1))
                for n in range(4):
                    i = t.matmul(b.pB[:, n * 512:(n + 1) * 512], b.onesm[:],
                                 b.sqb[i2][:, n * 512:(n + 1) * 512], start=(c == 0), stop=(c == C - 1))
                b.mark(f"{tag}_st_{c}", i, b.s_pe)

        @block.vector
        def _(v):
            b.wait(v, f"{tag}_st_{C-1}")
            v.tensor_scalar(b.rows[0:1, :], b.pA[0:1, :], INV_D, None, ALU.mult)
            v.tensor_scalar(b.rows[32:33, :], b.pB[0:1, :], INV_D, EPS_LN, ALU.mult, ALU.add)
            v.tensor_tensor(b.pB[0:1, :], b.rows[0:1, :], b.rows[0:1, :], ALU.mult)
            i = v.tensor_tensor(b.rows[32:33, :], b.rows[32:33, :], b.pB[0:1, :], ALU.subtract)
            b.mark(f"{tag}_rows", i, b.s_ve)

        @block.scalar
        def _(s):
            b.wait(s, f"{tag}_rows")
            s.activation(b.rows[64:65, :], b.rows[32:33, :], AF.Ln)
            i = s.activation(b.rows[32:33, :], b.rows[64:65, :], AF.Exp, scale=-0.5)
            b.mark(f"{tag}_rstd", i, b.s_ac)

        @block.tensor
        def _(t):
            b.wait(t, f"{tag}_rstd")
            for n in range(4):
                i = t.matmul(b.pB[:, n * 512:(n + 1) * 512], b.ones65[32:33, :],
                             b.rows[32:33, n * 512:(n + 1) * 512], start=True, stop=True)
            b.mark(f"{tag}_bc", i, b.s_pe)

        for c in range(C):
            i2 = c % 2
            if src == "x":
                @block.gpsimd
                def _(g, c=c, i2=i2):
                    b.wait(g, f"{tag}_st_{C-1}" if c < 2 else f"{tag}_n_{c-2}")
                    i = g.dma_start(out=b.xb[i2][:], in_=b.xdram.ap()[c * 128:(c + 1) * 128, :])
                    b.mark(f"{tag}_xb2_{c}", i, b.d_xb[i2], 16)
                srcb = b.xb[i2]
                srcdep = f"{tag}_xb2_{c}"
            else:
                srcb = b.stream2[c]
                srcdep = None

            @block.vector
            def _(v, c=c, i2=i2, srcb=srcb, srcdep=srcdep):
                b.wait(v, f"{tag}_bc", srcdep)
                v.scalar_tensor_tensor(b.scr[i2][:], b.pA[:, :], INV_D, srcb[:],
                                       ALU.mult, ALU.subtract)
                if dst == "s2" and not affine:
                    i = v.tensor_tensor(b.stream2[c][:], b.scr[i2][:], b.pB[:, :], ALU.mult)
                elif affine:
                    v.tensor_tensor(b.scr[i2][:], b.scr[i2][:], b.pB[:, :], ALU.mult)
                    i = v.tensor_scalar(b.mm_in[c][:], b.scr[i2][:],
                                        self.col(affine[0], c), self.col(affine[1], c),
                                        ALU.mult, ALU.add)
                else:
                    i = v.tensor_tensor(b.mm_in[c][:], b.scr[i2][:], b.pB[:, :], ALU.mult)
                b.mark(f"{tag}_n_{c}", i, b.s_ve)

        b.ev["psfree"] = b.ev[f"{tag}_n_{C-1}"]
        b.ev[f"{tag}_done"] = b.ev[f"{tag}_n_{C-1}"]
        b.ev[f"{tag}_done_prev"] = b.ev[f"{tag}_n_{C-2}"]

    def p_resid(self, block, tag, scale, final=False, deps=()):
        b = self
        depf = deps if callable(deps) else (lambda c: deps)
        for c in range(C):
            i2 = c % 2

            @block.sync
            def _(s, c=c, i2=i2):
                b.wait(s, f"{tag}_w_{c-2}", *depf(c))
                i = s.dma_start(out=b.scr[i2][:], in_=b.xdram.ap()[c * 128:(c + 1) * 128, :])
                b.mark(f"{tag}_r_{c}", i, b.d_rr[i2], 16)

            @block.vector
            def _(v, c=c, i2=i2):
                b.wait(v, f"{tag}_r_{c}")
                i = v.scalar_tensor_tensor(b.scr[i2][:], b.stream2[c][:], scale,
                                           b.scr[i2][:], ALU.mult, ALU.add)
                b.mark(f"{tag}_v_{c}", i, b.s_ve)

            @block.sync
            def _(s, c=c, i2=i2):
                b.wait(s, f"{tag}_v_{c}")
                tgt = b.x_out.ap() if final else b.xdram.ap()
                i = s.dma_start(out=tgt[c * 128:(c + 1) * 128, :], in_=b.scr[i2][:])
                if not final:
                    b.mark(f"{tag}_w_{c}", i, b.d_rw[i2], 16)
                else:
                    b.d_rw[i2].inc(i, 16)
                    i2_ = s.dma_start(out=b.xdram.ap()[c * 128:(c + 1) * 128, :], in_=b.scr[i2][:])
                    b.mark(f"{tag}_w_{c}", i2_, b.d_rw[i2], 16)
        b.ev[f"{tag}_done"] = b.ev[f"{tag}_w_{C-1}"]

    def p_kv(self, block, tag, st, partial=None, deps=()):
        b = self
        wq = self.wqkv.ap()
        groups = [[0, 1], [2, 3], [4, 5], [6, 7]]

        for ck in range(C):
            j = ck % 2
            i2 = ck % 2

            @block.gpsimd
            def _(g, ck=ck, j=j):
                b.wait(g, f"{tag}_kmm_{ck-2}")
                if ck == 0:
                    b.wait(g, "consts_g", *deps)
                i = g.dma_start(
                    out=b.w1b[j][:].rearrange("p (kc m) -> p kc m", kc=C),
                    in_=wq[:, D + ck * 128: D + (ck + 1) * 128].rearrange("(kc p) m -> p kc m", p=128),
                )
                b.mark(f"{tag}_kw_{ck}", i, b.d_w1b[j], 16)

            @block.tensor
            def _(t, ck=ck, j=j):
                b.wait(t, f"{tag}_kw_{ck}", f"{tag}_ks_{ck-1}")
                if ck == 0:
                    b.wait(t, "psfree")
                for kc in range(C):
                    for n in range(4):
                        i = t.matmul(b.pA[:, n * 512:(n + 1) * 512],
                                     b.w1b[j][:, kc * 128:(kc + 1) * 128],
                                     b.mm_in[kc][:, n * 512:(n + 1) * 512],
                                     start=(kc == 0), stop=(kc == C - 1))
                b.mark(f"{tag}_kmm_{ck}", i, b.s_pe)

            @block.scalar
            def _(s, ck=ck, i2=i2):
                b.wait(s, f"{tag}_kmm_{ck}", f"{tag}_kd_{ck-2}", f"{tag}_ks2_{ck-2}")
                i = s.activation(b.xb[i2][:], b.pA[:, :], AF.Copy)
                b.mark(f"{tag}_kc_{ck}", i, b.s_ac)
                i = s.activation(b.sqb[i2][:], b.pA[:, :], AF.Square)
                b.mark(f"{tag}_ks_{ck}", i, b.s_ac)

            for h2 in range(2):
                @block.tensor
                def _(t, ck=ck, i2=i2, h2=h2):
                    b.wait(t, f"{tag}_ks_{ck}" if h2 == 0 else f"{tag}_kmh_{ck}_0",
                           f"{tag}_km_{ck-1}" if h2 == 0 else None)
                    for n in range(4):
                        i = t.matmul(b.pB[:, n * 512:(n + 1) * 512],
                                     b.onesm[h2 * 64:(h2 + 1) * 64, :],
                                     b.sqb[i2][h2 * 64:(h2 + 1) * 64, n * 512:(n + 1) * 512],
                                     start=True, stop=True)
                    b.mark(f"{tag}_ksum_{ck}_{h2}", i, b.s_pe)

                @block.vector
                def _(v, ck=ck, h2=h2):
                    b.wait(v, f"{tag}_ksum_{ck}_{h2}")
                    hh = ck * 2 + h2
                    i = v.reduce_max(b.hrow[0:1, hh:hh + 1], b.pB[0:1, :], axis=AX.X)
                    b.mark(f"{tag}_kmh_{ck}_{h2}", i, b.s_ve)
            b.ev[f"{tag}_ks2_{ck}"] = b.ev[f"{tag}_ksum_{ck}_1"]
            b.ev[f"{tag}_km_{ck}"] = b.ev[f"{tag}_kmh_{ck}_1"]

            @block.sync
            def _(s, ck=ck, i2=i2):
                b.wait(s, f"{tag}_kc_{ck}")
                i = s.dma_start(out=b.agk_in.ap()[ck * 128:(ck + 1) * 128, :], in_=b.xb[i2][:])
                b.mark(f"{tag}_kd_{ck}", i, b.d_kd[i2], 16)

        if partial == "k":
            return
        @block.sync
        def _(s):
            b.wait(s, f"{tag}_km_{C-1}")
            i = s.dma_start(out=b.agm_in.ap(), in_=b.hrow[0:1, 0:16])
            b.mark(f"{tag}_md", i, b.d_msc[0], 16)

        @block.gpsimd
        def _(g):
            b.wait(g, f"{tag}_kd_{C-2}", f"{tag}_kd_{C-1}", f"{tag}_md")
            i = g.collective_compute(
                "AllGather", ALU.bypass, replica_groups=groups,
                ins=[b.agk_in.ap().opt()], outs=[b.agk_out.ap().opt()])
            b.mark(f"{tag}_agk", i, b.c_k)
            i = g.collective_compute(
                "AllGather", ALU.bypass, replica_groups=groups,
                ins=[b.agm_in.ap().opt()], outs=[b.agm_out.ap().opt()])
            b.mark(f"{tag}_agm", i, b.c_m)

        if partial == "cc1":
            return
        vsl = [st.enter_context(self.nc.sbuf_tensor(f"{tag}_vsl_{kc}", [128, D], BF16))
               for kc in range(C)]

        @block.gpsimd
        def _(g):
            b.wait(g, *deps)
            for kc in range(C):
                i = g.dma_start(out=vsl[kc][:], in_=wq[kc * 128:(kc + 1) * 128, 2 * D:3 * D])
                b.d_vsl.inc(i, 16)
            b.ev[f"{tag}_vw"] = (b.d_vsl, b.d_vsl.n)

        for tt in range(NT):
            i2 = tt % 2
            pp = [b.pA, b.pB][tt % 2]

            @block.tensor
            def _(t, tt=tt, pp=pp):
                b.wait(t, f"{tag}_vw" if tt == 0 else None, f"{tag}_vev_{tt-2}")
                if tt == 0:
                    b.wait(t, f"{tag}_ks_{C-1}")
                if tt == 1:
                    b.wait(t, f"{tag}_km_{C-1}")
                for kc in range(C):
                    for n in range(2):
                        i = t.matmul(pp[:, n * 512:(n + 1) * 512],
                                     b.mm_in[kc][:, tt * 128:(tt + 1) * 128],
                                     vsl[kc][:, n * 512:(n + 1) * 512],
                                     start=(kc == 0), stop=(kc == C - 1))
                b.mark(f"{tag}_vmm_{tt}", i, b.s_pe)

            @block.scalar
            def _(s, tt=tt, i2=i2, pp=pp):
                b.wait(s, f"{tag}_vmm_{tt}", f"{tag}_vd_{tt-2}")
                if tt < 2:
                    b.wait(s, f"{tag}_ks2_{C-1}")
                i = s.activation(b.sqb[i2][:, 0:1024], pp[:, 0:1024], AF.Copy)
                b.mark(f"{tag}_vev_{tt}", i, b.s_ac)

            @block.sync
            def _(s, tt=tt, i2=i2):
                b.wait(s, f"{tag}_vev_{tt}")
                i = s.dma_start(out=b.agv_in.ap()[tt * 128:(tt + 1) * 128, :], in_=b.sqb[i2][:, 0:1024])
                b.mark(f"{tag}_vd_{tt}", i, b.d_vd[i2], 16)

        @block.gpsimd
        def _(g):
            b.wait(g, f"{tag}_vd_{NT-2}", f"{tag}_vd_{NT-1}")
            i = g.collective_compute(
                "AllGather", ALU.bypass, replica_groups=groups,
                ins=[b.agv_in.ap().opt()], outs=[b.agv_out.ap().opt()])
            b.mark(f"{tag}_agv", i, b.c_v)

        @block.sync
        def _(s):
            b.wait(s, f"{tag}_agm")
            i = s.dma_start(out=b.hrow[0:1, 16:32], in_=b.agm_out.ap()[0:1, :])
            b.d_msc[1].inc(i, 16)
            i = s.dma_start(out=b.hrow[0:1, 32:48], in_=b.agm_out.ap()[1:2, :])
            b.mark(f"{tag}_mg", i, b.d_msc[1], 16)

        @block.vector
        def _(v):
            b.wait(v, f"{tag}_mg")
            i = v.tensor_tensor(b.hrow[0:1, 16:32], b.hrow[0:1, 16:32], b.hrow[0:1, 32:48], ALU.max)
            b.mark(f"{tag}_maxk2", i, b.s_ve)

        b.ev["psfree"] = b.ev[f"{tag}_vev_{NT-1}"]

    def p_msa(self, block, tag, kvtag, st, pairs=8, stage="all", deps=()):
        b = self
        kt = [st.enter_context(self.nc.sbuf_tensor(f"{tag}_kt_{i}", [128, SEQ], BF16)) for i in range(2)]
        qp = [st.enter_context(self.nc.sbuf_tensor(f"{tag}_qp_{i}", [128, T], BF16)) for i in range(2)]
        ap_ = [st.enter_context(self.nc.sbuf_tensor(f"{tag}_ap_{i}", [128, T], BF16)) for i in range(2)]
        vt = [st.enter_context(self.nc.sbuf_tensor(f"{tag}_vt_{i}", [128, 65], BF16)) for i in range(4)]

        @block.vector
        def _(v):
            b.wait(v, "consts_sy", *deps)
            for i_ in range(4):
                i = v.tensor_copy(vt[i_][:, 64:65], b.onesc[:])
            b.mark(f"{tag}_vtones", i, b.s_ve)

        for hp in range(pairs):
            j = hp % 2

            @block.gpsimd
            def _(g, hp=hp, j=j):
                b.wait(g, f"{tag}_qmm_{hp-2}")
                if hp < 2:
                    b.wait(g, f"{kvtag}_kmm_{C-1}", *deps)
                i = g.dma_start(
                    out=b.w1b[j][:].rearrange("p (kc m) -> p kc m", kc=C),
                    in_=b.wqkv.ap()[:, hp * 128:(hp + 1) * 128].rearrange("(kc p) m -> p kc m", p=128),
                )
                b.mark(f"{tag}_qw_{hp}", i, b.d_w1b[j], 16)
                b.wait(g, f"{tag}_pj_{hp-2}")
                i = g.dma_start(out=b.wsm[j][:], in_=b.wproj.ap()[hp * 128:(hp + 1) * 128, :])
                b.mark(f"{tag}_pw_{hp}", i, b.d_wsm[j], 16)

            @block.sync
            def _(s, hp=hp, j=j):
                b.wait(s, f"{kvtag}_agk", f"{tag}_sc_done_{hp-2}")
                if hp < 2:
                    b.wait(s, *deps)
                i = s.dma_start(out=kt[j][:, 0:T], in_=b.agk_out.ap()[hp * 128:(hp + 1) * 128, :])
                b.d_kt[j].inc(i, 16)
                i = s.dma_start(out=kt[j][:, T:SEQ], in_=b.agk_out.ap()[D + hp * 128:D + (hp + 1) * 128, :])
                b.mark(f"{tag}_kt_{hp}", i, b.d_kt[j], 16)

            @block.tensor
            def _(t, hp=hp, j=j):
                b.wait(t, f"{tag}_qw_{hp}", f"{tag}_pj_{hp-1}")
                if hp == 0:
                    b.wait(t, "psfree")
                for kc in range(C):
                    for n in range(4):
                        i = t.matmul(b.pA[:, n * 512:(n + 1) * 512],
                                     b.w1b[j][:, kc * 128:(kc + 1) * 128],
                                     b.mm_in[kc][:, n * 512:(n + 1) * 512],
                                     start=(kc == 0), stop=(kc == C - 1))
                b.mark(f"{tag}_qmm_{hp}", i, b.s_pe)

            @block.scalar
            def _(s, hp=hp, j=j):
                b.wait(s, f"{tag}_qmm_{hp}")
                s.activation(qp[j][:], b.pA[:, :], AF.Copy)
                i = s.activation(b.sqb[0][:], b.pA[:, :], AF.Square)
                b.mark(f"{tag}_qe_{hp}", i, b.s_ac)

            for h2 in range(2):
                @block.tensor
                def _(t, hp=hp, h2=h2):
                    if h2 == 0:
                        b.wait(t, f"{tag}_qe_{hp}", f"{tag}_pvc_{hp-1}_1_1", f"{tag}_rc_{hp-1}_1_1")
                    else:
                        b.wait(t, f"{tag}_qmh_{hp}_0")
                    for n in range(4):
                        i = t.matmul(b.pB[:, n * 512:(n + 1) * 512],
                                     b.onesm[h2 * 64:(h2 + 1) * 64, :],
                                     b.sqb[0][h2 * 64:(h2 + 1) * 64, n * 512:(n + 1) * 512],
                                     start=True, stop=True)
                    b.mark(f"{tag}_qsum_{hp}_{h2}", i, b.s_pe)

                @block.vector
                def _(v, hp=hp, h2=h2):
                    h = hp * 2 + h2
                    b.wait(v, f"{tag}_qsum_{hp}_{h2}")
                    i = v.reduce_max(b.hrow[0:1, 64 + h:65 + h], b.pB[0:1, :], axis=AX.X)
                    b.mark(f"{tag}_qmh_{hp}_{h2}", i, b.s_ve)

            @block.vector
            def _(v, hp=hp):
                b.wait(v, f"{kvtag}_maxk2", f"{tag}_qmh_{hp}_0", f"{tag}_qmh_{hp}_1")
                i = v.tensor_tensor(b.hrow[0:1, 96 + hp * 2:98 + hp * 2],
                                    b.hrow[0:1, 64 + hp * 2:66 + hp * 2],
                                    b.hrow[0:1, 16 + hp * 2:18 + hp * 2], ALU.mult)
                b.mark(f"{tag}_qmt_{hp}", i, b.s_ve)
                b.wait(v, f"{tag}_qmt_{hp}")
                i = v.tensor_scalar(b.hrow[0:1, 96 + hp * 2:98 + hp * 2],
                                    b.hrow[0:1, 96 + hp * 2:98 + hp * 2], 1.0 / 64, None, ALU.mult)
                b.mark(f"{tag}_qm_{hp}", i, b.s_ve)

            @block.scalar
            def _(s, hp=hp):
                b.wait(s, f"{tag}_qm_{hp}")
                i = s.activation(b.hrow[0:1, 128 + hp * 2:130 + hp * 2],
                                 b.hrow[0:1, 96 + hp * 2:98 + hp * 2], AF.Ln)
                b.mark(f"{tag}_mhl_{hp}", i, b.s_ac)
                s.wait_ge(b.s_ac.sem, b.s_ac.n)
                i = s.activation(b.hrow[0:1, 128 + hp * 2:130 + hp * 2],
                                 b.hrow[0:1, 128 + hp * 2:130 + hp * 2], AF.Exp, scale=0.5)
                b.mark(f"{tag}_mh_{hp}", i, b.s_ac)

            @block.tensor
            def _(t, hp=hp):
                b.wait(t, f"{tag}_mh_{hp}", f"{tag}_ebv_{hp-1}")
                for h2 in range(2):
                    h = hp * 2 + h2
                    i = t.matmul(b.pB[:, 1024 + h2 * 512:1025 + h2 * 512],
                                 b.negonesf[:], b.hrow[0:1, 128 + h:129 + h],
                                 start=True, stop=True)
                b.mark(f"{tag}_ebmm_{hp}", i, b.s_pe)

            @block.vector
            def _(v, hp=hp):
                b.wait(v, f"{tag}_ebmm_{hp}")
                for h2 in range(2):
                    h = hp * 2 + h2
                    i = v.tensor_copy(b.ebias[:, h:h + 1], b.pB[:, 1024 + h2 * 512:1025 + h2 * 512])
                b.mark(f"{tag}_ebv_{hp}", i, b.s_ve)

            if stage == "q":
                b.ev[f"{tag}_sc_done_{hp}"] = b.ev[f"{tag}_ebv_{hp}"]
                b.ev[f"{tag}_pj_{hp}"] = b.ev[f"{tag}_ebv_{hp}"]
                b.ev[f"{tag}_pvc_{hp}_1_1"] = b.ev[f"{tag}_ebv_{hp}"]
                b.ev[f"{tag}_rc_{hp}_1_1"] = b.ev[f"{tag}_ebv_{hp}"]
                continue
            prev_grp = f"{tag}_pvev_{hp-1}_1_1"
            for h2 in range(2):
                h = hp * 2 + h2
                for qg in range(2):
                    pvoff = 1024 * qg
                    for kc in range(32):
                        sb_ = (kc % 2) * 1024
                        vb = kc % 4

                        @block.sync
                        def _(s, hp=hp, h=h, h2=h2, qg=qg, kc=kc, vb=vb):
                            b.wait(s, b.ev.get(f"{tag}_vtfree_{vb}"))
                            if kc == 0 and h2 == 0 and qg == 0 and hp == 0:
                                b.wait(s, f"{kvtag}_agv", f"{tag}_vtones")
                            i = s.dma_start(out=vt[vb][:, 0:64],
                                            in_=b.agv_out.ap()[kc * 128:(kc + 1) * 128, h * 64:(h + 1) * 64])
                            b.mark(f"{tag}_vl_{hp}_{h2}_{qg}_{kc}", i, b.d_vt[vb], 16)

                        @block.tensor
                        def _(t, hp=hp, j=j, h2=h2, qg=qg, kc=kc, sb_=sb_, prev_grp=prev_grp):
                            b.wait(t, b.ev.get(f"{tag}_ptbfree_{kc % 2}"))
                            if kc == 0:
                                b.wait(t, prev_grp)
                                if h2 == 0 and qg == 0:
                                    b.wait(t, f"{tag}_kt_{hp}", f"{tag}_ebv_{hp}")
                            for n in range(2):
                                i = t.matmul(b.pA[:, sb_ + n * 512: sb_ + (n + 1) * 512],
                                             kt[j][h2 * 64:(h2 + 1) * 64, kc * 128:(kc + 1) * 128],
                                             qp[j][h2 * 64:(h2 + 1) * 64, qg * 1024 + n * 512: qg * 1024 + (n + 1) * 512],
                                             start=True, stop=True)
                            b.mark(f"{tag}_s_{hp}_{h2}_{qg}_{kc}", i, b.s_pe)

                        @block.scalar
                        def _(s, hp=hp, h=h, h2=h2, qg=qg, kc=kc, sb_=sb_):
                            b.wait(s, f"{tag}_s_{hp}_{h2}_{qg}_{kc}", f"{tag}_pv_{hp}_{h2}_{qg}_{kc-2}")
                            i = s.activation(b.ptb[kc % 2][:], b.pA[:, sb_:sb_ + 1024], AF.Exp,
                                             scale=0.125, bias=b.ebias[:, h:h + 1])
                            b.mark(f"{tag}_exp_{hp}_{h2}_{qg}_{kc}", i, b.s_ac)

                        if stage == "sc":
                            b.ev[f"{tag}_pv_{hp}_{h2}_{qg}_{kc}"] = b.ev[f"{tag}_exp_{hp}_{h2}_{qg}_{kc}"]
                            if kc == 31:
                                b.ev[f"{tag}_pvev_{hp}_{h2}_{qg}"] = b.ev[f"{tag}_exp_{hp}_{h2}_{qg}_31"]
                                prev_grp = f"{tag}_pvev_{hp}_{h2}_{qg}"
                                if h2 == 1 and qg == 1:
                                    b.ev[f"{tag}_sc_done_{hp}"] = b.ev[f"{tag}_exp_{hp}_1_1_31"]
                                    b.ev[f"{tag}_pj_{hp}"] = b.ev[f"{tag}_exp_{hp}_1_1_31"]
                                    b.ev[f"{tag}_pvc_{hp}_1_1"] = b.ev[f"{tag}_exp_{hp}_1_1_31"]
                                    b.ev[f"{tag}_rc_{hp}_1_1"] = b.ev[f"{tag}_exp_{hp}_1_1_31"]
                            continue
                        @block.tensor
                        def _(t, hp=hp, h2=h2, qg=qg, kc=kc, vb=vb, pvoff=pvoff):
                            assert f"{tag}_exp_{hp}_{h2}_{qg}_{kc}" in b.ev
                            b.wait(t, f"{tag}_exp_{hp}_{h2}_{qg}_{kc}", f"{tag}_vl_{hp}_{h2}_{qg}_{kc}")
                            for n in range(2):
                                i = t.matmul(b.pB[0:65, pvoff + n * 512: pvoff + (n + 1) * 512],
                                             vt[vb][:, :],
                                             b.ptb[kc % 2][:, n * 512:(n + 1) * 512],
                                             start=(kc == 0), stop=(kc == 31))
                            b.mark(f"{tag}_pv_{hp}_{h2}_{qg}_{kc}", i, b.s_pe)
                            b.ev[f"{tag}_vtfree_{vb}"] = (b.s_pe, b.s_pe.n)
                            b.ev[f"{tag}_ptbfree_{kc % 2}"] = (b.s_pe, b.s_pe.n)

                    @block.vector
                    def _(v, hp=hp, h2=h2, qg=qg, pvoff=pvoff):
                        b.wait(v, f"{tag}_pv_{hp}_{h2}_{qg}_31")
                        i = v.reciprocal(b.rows[64:65, 0:1024], b.pB[64:65, pvoff:pvoff + 1024])
                        b.mark(f"{tag}_rc_{hp}_{h2}_{qg}", i, b.s_ve)
                        if stage == "pv":
                            b.ev[f"{tag}_pvev_{hp}_{h2}_{qg}"] = b.ev[f"{tag}_rc_{hp}_{h2}_{qg}"]

                    if stage == "pv":
                        prev_grp = f"{tag}_pvev_{hp}_{h2}_{qg}"
                        if h2 == 1 and qg == 1:
                            b.ev[f"{tag}_sc_done_{hp}"] = b.ev[f"{tag}_pvev_{hp}_1_1"]
                            b.ev[f"{tag}_pj_{hp}"] = b.ev[f"{tag}_pvev_{hp}_1_1"]
                            b.ev[f"{tag}_pvc_{hp}_1_1"] = b.ev[f"{tag}_pvev_{hp}_1_1"]
                        continue
                    @block.tensor
                    def _(t, hp=hp, h2=h2, qg=qg):
                        b.wait(t, f"{tag}_rc_{hp}_{h2}_{qg}")
                        for n in range(2):
                            i = t.matmul(b.pA[:, n * 512:(n + 1) * 512], b.ones65[64:65, :],
                                         b.rows[64:65, n * 512:(n + 1) * 512], start=True, stop=True)
                        b.mark(f"{tag}_rcb_{hp}_{h2}_{qg}", i, b.s_pe)

                    @block.scalar
                    def _(s, hp=hp, h2=h2, qg=qg, pvoff=pvoff):
                        b.wait(s, f"{tag}_pv_{hp}_{h2}_{qg}_31")
                        i = s.activation(b.scr[0][0:64, qg * 1024:(qg + 1) * 1024],
                                         b.pB[0:64, pvoff:pvoff + 1024], AF.Copy)
                        b.mark(f"{tag}_pvc_{hp}_{h2}_{qg}", i, b.s_ac)

                    @block.vector
                    def _(v, hp=hp, j=j, h2=h2, qg=qg):
                        b.wait(v, f"{tag}_rcb_{hp}_{h2}_{qg}", f"{tag}_pvc_{hp}_{h2}_{qg}")
                        i = v.tensor_tensor(ap_[j][h2 * 64:(h2 + 1) * 64, qg * 1024:(qg + 1) * 1024],
                                            b.scr[0][0:64, qg * 1024:(qg + 1) * 1024],
                                            b.pA[0:64, 0:1024], ALU.mult)
                        b.mark(f"{tag}_pvev_{hp}_{h2}_{qg}", i, b.s_ve)
                    prev_grp = f"{tag}_pvev_{hp}_{h2}_{qg}"

            b.ev[f"{tag}_sc_done_{hp}"] = b.ev[f"{tag}_pvev_{hp}_1_1"]

            if stage == "ev":
                b.ev[f"{tag}_pj_{hp}"] = b.ev[f"{tag}_pvev_{hp}_1_1"]
            if stage in ("sc", "pv", "ev"):
                continue
            for m in range(C):
                for n in range(4):
                    @block.tensor
                    def _(t, hp=hp, j=j, m=m, n=n):
                        if m == 0 and n == 0:
                            b.wait(t, f"{tag}_pvev_{hp}_1_1", f"{tag}_pw_{hp}")
                        b.wait(t, b.ev.get(f"{tag}_pjfree_{n % 2}"))
                        i = t.matmul(b.pA[:, 1024 + (n % 2) * 512: 1024 + (n % 2 + 1) * 512],
                                     b.wsm[j][:, m * 128:(m + 1) * 128],
                                     ap_[j][:, n * 512:(n + 1) * 512], start=True, stop=True)
                        b.mark(f"{tag}_pjmm_{hp}_{m}_{n}", i, b.s_pe)

                    @block.vector
                    def _(v, hp=hp, m=m, n=n):
                        b.wait(v, f"{tag}_pjmm_{hp}_{m}_{n}")
                        i = v.scalar_tensor_tensor(
                            b.stream2[m][:, n * 512:(n + 1) * 512],
                            b.pA[:, 1024 + (n % 2) * 512: 1024 + (n % 2 + 1) * 512], 1.0,
                            b.stream2[m][:, n * 512:(n + 1) * 512], ALU.mult, ALU.add)
                        b.mark(f"{tag}_pjv_{hp}_{m}_{n}", i, b.s_ve)
                        b.ev[f"{tag}_pjfree_{n % 2}"] = (b.s_ve, b.s_ve.n)
                b.ev[f"{tag}_pjv_{hp}_{m}"] = b.ev[f"{tag}_pjv_{hp}_{m}_3"]
            b.ev[f"{tag}_pj_{hp}"] = b.ev[f"{tag}_pjv_{hp}_{C-1}"]

        b.ev[f"{tag}_done"] = b.ev[f"{tag}_pj_{pairs-1}"]
        b.ev["psfree"] = b.ev[f"{tag}_pj_{pairs-1}"]



    def p_gate(self, block, tag, deps):
        b = self
        for tt in range(NT):
            sl = (tt % 2) * 512

            @block.tensor
            def _(t, tt=tt, sl=sl):
                b.wait(t, f"{tag}_lgv_{tt-2}")
                if tt == 0:
                    b.wait(t, *deps, "psfree", "consts_g")
                for kc in range(C):
                    t.matmul(b.pA[:, sl:sl + 8],
                             b.mm_in[kc][:, tt * 128:(tt + 1) * 128],
                             b.gwb[:, kc * 8:(kc + 1) * 8],
                             start=(kc == 0), stop=False)
                i = t.matmul(b.pA[:, sl:sl + 8], b.onesbf[:], b.gbb[:],
                             start=False, stop=True)
                b.mark(f"{tag}_lg_{tt}", i, b.s_pe)

            @block.vector
            def _(v, tt=tt, sl=sl):
                b.wait(v, f"{tag}_lg_{tt}")
                i = v.tensor_copy(b.scr[0][:, tt * 8:(tt + 1) * 8], b.pA[:, sl:sl + 8])
                b.mark(f"{tag}_lgv_{tt}", i, b.s_ve)
                b.wait(v, f"{tag}_lgv_{tt}")
                i = v.reduce_max(b.scr[1][:, tt:tt + 1], b.scr[0][:, tt * 8:(tt + 1) * 8], axis=AX.X)
                b.mark(f"{tag}_lgm_{tt}", i, b.s_ve)
                b.wait(v, f"{tag}_lgm_{tt}")
                i = v.tensor_scalar(b.scr[1][:, 256 + tt * 8:256 + (tt + 1) * 8],
                                    b.scr[0][:, tt * 8:(tt + 1) * 8],
                                    b.scr[1][:, tt:tt + 1], None, ALU.is_equal)
                b.mark(f"{tag}_msk_{tt}", i, b.s_ve)

            @block.tensor
            def _(t, tt=tt, sl=sl):
                b.wait(t, f"{tag}_msk_{tt}", f"{tag}_mfm_{tt-2}")
                i = t.transpose(b.pB[0:8, sl:sl + 128],
                                b.scr[1][:, 256 + tt * 8:256 + (tt + 1) * 8], b.identt[:])
                b.mark(f"{tag}_tr_{tt}", i, b.s_pe)

            @block.vector
            def _(v, tt=tt, sl=sl):
                b.wait(v, f"{tag}_tr_{tt}")
                i = v.tensor_copy(b.maskfm[0:8, tt * 128:(tt + 1) * 128], b.pB[0:8, sl:sl + 128])
                b.mark(f"{tag}_mfm_{tt}", i, b.s_ve)

        b.ev[f"{tag}_done"] = b.ev[f"{tag}_mfm_{NT-1}"]
        b.ev["psfree"] = b.ev[f"{tag}_mfm_{NT-1}"]

    def p_moe(self, block, tag, st, deps):
        b = self
        hq = [st.enter_context(self.nc.sbuf_tensor(f"{tag}_hq_{i}", [128, T], BF16)) for i in range(8)]
        w2q = st.enter_context(self.nc.sbuf_tensor(f"{tag}_w2q", [128, 8 * 1024], BF16))

        for e in range(E):
            @block.tensor
            def _(t, e=e):
                b.wait(t, f"{tag}_ye_{e-1}_3")
                if e == 0:
                    b.wait(t, *deps, "psfree")
                for n in range(4):
                    i = t.matmul(b.pA[:, n * 512:(n + 1) * 512],
                                 b.ind8t[0:8, e * 128:(e + 1) * 128],
                                 b.maskfm[0:8, n * 512:(n + 1) * 512], start=True, stop=True)
                b.mark(f"{tag}_mb_{e}", i, b.s_pe)

            @block.vector
            def _(v, e=e):
                b.wait(v, f"{tag}_mb_{e}")
                i = v.tensor_copy(b.maske[:], b.pA[:, :])
                b.mark(f"{tag}_me_{e}", i, b.s_ve)

            for q in range(4):
                @block.gpsimd
                def _(g, e=e, q=q):
                    if e == 0 and q == 0:
                        b.wait(g, *deps)
                    b.wait(g, f"{tag}_ye_{e}_{q-1}" if q > 0 else (f"{tag}_ye_{e-1}_3" if e > 0 else None))
                    i = g.dma_start(
                        out=w2q[:].rearrange("p (kc m) -> p kc m", kc=8),
                        in_=b.w2.ap()[e, q * 1024:(q + 1) * 1024, :].rearrange("(kc p) m -> p kc m", p=128),
                    )
                    b.mark(f"{tag}_w2_{e}_{q}", i, b.d_w2, 16)

                for hs in range(8):
                    j = hs % 2
                    pp = [b.pA, b.pB][hs % 2]

                    @block.gpsimd
                    def _(g, e=e, q=q, hs=hs, j=j):
                        b.wait(g, f"{tag}_h_{e}_{q}_{hs-2}")
                        if hs < 2:
                            pe_, pq_ = (e, q - 1) if q > 0 else (e - 1, 3)
                            b.wait(g, f"{tag}_hmm_{pe_}_{pq_}_{hs + 6}")
                            if q == 0 and e == 0:
                                b.wait(g, *deps)
                        i = g.dma_start(
                            out=b.w1b[j][:].rearrange("p (kc m) -> p kc m", kc=C),
                            in_=b.w1.ap()[e, :, q * 1024 + hs * 128: q * 1024 + (hs + 1) * 128]
                                .rearrange("(kc p) m -> p kc m", p=128),
                        )
                        b.mark(f"{tag}_w1_{e}_{q}_{hs}", i, b.d_w1b[j], 16)

                    @block.tensor
                    def _(t, e=e, q=q, hs=hs, j=j, pp=pp):
                        b.wait(t, f"{tag}_w1_{e}_{q}_{hs}", f"{tag}_h_{e}_{q}_{hs-2}")
                        if hs < 2:
                            pe_, pq_ = (e, q - 1) if q > 0 else (e - 1, 3)
                            b.wait(t, f"{tag}_h_{pe_}_{pq_}_{hs + 6}")
                            b.wait(t, f"{tag}_me_{e}" if hs == 0 else None,
                                   f"{tag}_yv_{e}_{q-1}" if q > 0 else (f"{tag}_yv_{e-1}_3" if e > 0 else None))
                        for kc in range(C):
                            for n in range(4):
                                i = t.matmul(pp[:, n * 512:(n + 1) * 512],
                                             b.w1b[j][:, kc * 128:(kc + 1) * 128],
                                             b.mm_in[kc][:, n * 512:(n + 1) * 512],
                                             start=(kc == 0), stop=(kc == C - 1))
                        b.mark(f"{tag}_hmm_{e}_{q}_{hs}", i, b.s_pe)

                    @block.scalar
                    def _(s, e=e, q=q, hs=hs, pp=pp):
                        b.wait(s, f"{tag}_hmm_{e}_{q}_{hs}",
                               f"{tag}_ymm_{e}_{q-1}" if q > 0 else (f"{tag}_ymm_{e-1}_3" if e > 0 else None))
                        i = s.activation(hq[hs][:], pp[:, :], AF.Gelu_apprx_tanh)
                        b.mark(f"{tag}_h_{e}_{q}_{hs}", i, b.s_ac)

                for m in range(C):
                    for tg in range(4):
                        slot = ((m * 4 + tg) % 2) * 512

                        @block.tensor
                        def _(t, e=e, q=q, m=m, tg=tg, slot=slot):
                            if m == 0 and tg == 0:
                                b.wait(t, f"{tag}_h_{e}_{q}_6", f"{tag}_h_{e}_{q}_7", f"{tag}_w2_{e}_{q}")
                            b.wait(t, b.ev.get(f"{tag}_yfree_{slot}"))
                            for hs in range(8):
                                i = t.matmul(b.pA[:, slot:slot + 512],
                                             w2q[:, hs * 1024 + m * 128: hs * 1024 + (m + 1) * 128],
                                             hq[hs][:, tg * 512:(tg + 1) * 512],
                                             start=(hs == 0), stop=(hs == 7))
                            b.mark(f"{tag}_ymm_{e}_{q}_{m}_{tg}", i, b.s_pe)
                            if m == C - 1 and tg == 3:
                                b.ev[f"{tag}_ymm_{e}_{q}"] = (b.s_pe, b.s_pe.n)

                        @block.vector
                        def _(v, e=e, q=q, m=m, tg=tg, slot=slot):
                            b.wait(v, f"{tag}_ymm_{e}_{q}_{m}_{tg}")
                            v.tensor_tensor(b.scr[0][:, slot:slot + 512], b.pA[:, slot:slot + 512],
                                            b.maske[:, tg * 512:(tg + 1) * 512], ALU.mult)
                            i = v.tensor_tensor(b.stream2[m][:, tg * 512:(tg + 1) * 512],
                                                b.scr[0][:, slot:slot + 512],
                                                b.stream2[m][:, tg * 512:(tg + 1) * 512], ALU.add)
                            b.mark(f"{tag}_yv_{e}_{q}_{m}_{tg}", i, b.s_ve)
                            b.ev[f"{tag}_yfree_{slot}"] = (b.s_ve, b.s_ve.n)
                            if m == C - 1 and tg == 3:
                                b.ev[f"{tag}_yv_{e}_{q}"] = (b.s_ve, b.s_ve.n)
                                b.ev[f"{tag}_ye_{e}_{q}"] = (b.s_ve, b.s_ve.n)

            b.ev[f"{tag}_ye_{e}_3"] = b.ev[f"{tag}_ye_{e}_3"]
        b.ev[f"{tag}_done"] = b.ev[f"{tag}_ye_{E-1}_3"]
        b.ev["psfree"] = b.ev[f"{tag}_done"]

    def p_lin(self, block, tag, deps, xdeps=lambda m: ()):
        b = self
        groups = [[0, 1], [2, 3], [4, 5], [6, 7]]

        for ck in range(C):
            j = ck % 2

            @block.gpsimd
            def _(g, ck=ck, j=j):
                b.wait(g, f"{tag}_qmm_{ck-2}")
                if ck == 0:
                    b.wait(g, *deps)
                i = g.dma_start(
                    out=b.w1b[j][:].rearrange("p (kc m) -> p kc m", kc=C),
                    in_=b.awk.ap()[:, ck * 128:(ck + 1) * 128].rearrange("(kc p) m -> p kc m", p=128))
                b.mark(f"{tag}_wk_{ck}", i, b.d_w1b[j], 16)
                i = g.dma_start(
                    out=b.wsm[j][:].rearrange("p (kc m) -> p kc m", kc=C),
                    in_=b.awv.ap()[:, ck * 128:(ck + 1) * 128].rearrange("(kc p) m -> p kc m", p=128))
                b.mark(f"{tag}_wv_{ck}", i, b.d_wsm[j], 16)
                b.wait(g, f"{tag}_qs_{ck-2}")
                i = g.dma_start(
                    out=b.xb[j][:, 0:1024].rearrange("p (kc m) -> p kc m", kc=C),
                    in_=b.awq.ap()[:, ck * 128:(ck + 1) * 128].rearrange("(kc p) m -> p kc m", p=128))
                b.mark(f"{tag}_wq_{ck}", i, b.d_xb[j], 16)

            @block.tensor
            def _(t, ck=ck, j=j):
                b.wait(t, f"{tag}_wk_{ck}", f"{tag}_qS_{ck-1}")
                if ck == 0:
                    b.wait(t, *deps, "psfree")
                for kc in range(C):
                    for n in range(4):
                        i = t.matmul(b.pA[:, n * 512:(n + 1) * 512],
                                     b.w1b[j][:, kc * 128:(kc + 1) * 128],
                                     b.mm_in[kc][:, n * 512:(n + 1) * 512],
                                     start=(kc == 0), stop=(kc == C - 1))
                b.mark(f"{tag}_kmm_{ck}", i, b.s_pe)
                b.wait(t, f"{tag}_wv_{ck}", f"{tag}_vS_{ck-1}")
                for kc in range(C):
                    for n in range(4):
                        i = t.matmul(b.pB[:, n * 512:(n + 1) * 512],
                                     b.wsm[j][:, kc * 128:(kc + 1) * 128],
                                     b.mm_in[kc][:, n * 512:(n + 1) * 512],
                                     start=(kc == 0), stop=(kc == C - 1))
                b.mark(f"{tag}_vmm_{ck}", i, b.s_pe)

            @block.scalar
            def _(s, ck=ck, j=j):
                b.wait(s, f"{tag}_kmm_{ck}", f"{tag}_kv_{ck-1}")
                s.activation(b.scr[0][:], b.pA[:, :], AF.Relu, scale=-1.0, bias=b.ncol("abk", ck))
                s.activation(b.scr[1][:], b.scr[0][:], AF.Exp, scale=-1.0)
                i = s.activation(b.scr[0][:], b.pA[:, :], AF.Relu, bias=self.col("abk", ck))
                b.mark(f"{tag}_kS_{ck}", i, b.s_ac)

            @block.vector
            def _(v, ck=ck):
                b.wait(v, f"{tag}_vmm_{ck}", f"{tag}_kv_{ck-1}")
                i = v.tensor_scalar(b.sqb[1][:], b.pB[:, :], self.col("abv", ck), None, ALU.add)
                b.mark(f"{tag}_vS_{ck}", i, b.s_ve)
                b.wait(v, f"{tag}_kS_{ck}")
                i = v.scalar_tensor_tensor(b.sqb[0][:], b.scr[1][:], 1.0, b.scr[0][:],
                                           ALU.mult, ALU.add,
                                           accum_out=b.lincols[:, 8 + ck:9 + ck])
                b.mark(f"{tag}_kA_{ck}", i, b.s_ve)
                b.wait(v, f"{tag}_kA_{ck}")
                i = v.scalar_tensor_tensor(b.scr[0][:], b.sqb[0][:], 1.0,
                                           b.sqb[1][:], ALU.mult, ALU.mult,
                                           accum_out=b.lincols[:, ck:ck + 1])
                b.mark(f"{tag}_kv0_{ck}", i, b.s_ve)

            @block.tensor
            def _(t, ck=ck, j=j):
                b.wait(t, f"{tag}_wq_{ck}", f"{tag}_kS_{ck}")
                for kc in range(C):
                    for n in range(4):
                        i = t.matmul(b.pA[:, n * 512:(n + 1) * 512],
                                     b.xb[j][:, kc * 128:(kc + 1) * 128],
                                     b.mm_in[kc][:, n * 512:(n + 1) * 512],
                                     start=(kc == 0), stop=(kc == C - 1))
                b.mark(f"{tag}_qmm_{ck}", i, b.s_pe)

            @block.scalar
            def _(s, ck=ck):
                b.wait(s, f"{tag}_qmm_{ck}", f"{tag}_qA_{ck-1}", f"{tag}_kv0_{ck}")
                s.activation(b.scr[0][:], b.pA[:, :], AF.Relu, scale=-1.0, bias=b.ncol("abq", ck))
                s.activation(b.scr[1][:], b.scr[0][:], AF.Exp, scale=-1.0)
                i = s.activation(b.scr[0][:], b.pA[:, :], AF.Relu, bias=self.col("abq", ck))
                b.mark(f"{tag}_qS_{ck}", i, b.s_ac)

            @block.vector
            def _(v, ck=ck):
                b.wait(v, f"{tag}_qS_{ck}")
                i = v.scalar_tensor_tensor(b.stream2[ck][:], b.scr[1][:], 1.0, b.scr[0][:],
                                           ALU.mult, ALU.add)
                b.mark(f"{tag}_qA_{ck}", i, b.s_ve)
                b.ev[f"{tag}_qs_{ck}"] = b.ev[f"{tag}_qA_{ck}"]
                b.ev[f"{tag}_kv_{ck}"] = b.ev[f"{tag}_qA_{ck}"]

        @block.sync
        def _(s):
            b.wait(s, f"{tag}_kv0_{C-1}")
            i = s.dma_start(out=b.arl_in.ap(), in_=b.lincols[:, 0:16])
            b.mark(f"{tag}_ard", i, b.d_msc[0], 16)

        @block.gpsimd
        def _(g):
            b.wait(g, f"{tag}_ard")
            i = g.collective_compute("AllReduce", ALU.add, replica_groups=groups,
                                     ins=[b.arl_in.ap().opt()], outs=[b.arl_out.ap().opt()])
            b.mark(f"{tag}_ar", i, b.c_l)

        @block.sync
        def _(s):
            b.wait(s, f"{tag}_ar")
            i = s.dma_start(out=b.lincols[:, 16:32], in_=b.arl_out.ap())
            b.mark(f"{tag}_art", i, b.d_msc[1], 16)

        for ck in range(C):
            @block.vector
            def _(v, ck=ck):
                b.wait(v, f"{tag}_art" if ck == 0 else None)
                v.tensor_scalar(b.scr[0][:], b.stream2[ck][:],
                                b.lincols[:, 24 + ck:25 + ck], EPS_LA, ALU.mult, ALU.add)
                v.reciprocal(b.scr[1][:], b.scr[0][:])
                v.tensor_tensor(b.scr[0][:], b.stream2[ck][:], b.scr[1][:], ALU.mult)
                i = v.tensor_scalar(b.mm_in[ck][:], b.scr[0][:],
                                    b.lincols[:, 16 + ck:17 + ck], None, ALU.mult)
                b.mark(f"{tag}_cb_{ck}", i, b.s_ve)

        for m in range(C):
            j = m % 2

            @block.gpsimd
            def _(g, m=m, j=j):
                b.wait(g, f"{tag}_pmm_{m-2}")
                i = g.dma_start(
                    out=b.w1b[j][:].rearrange("p (kc m) -> p kc m", kc=C),
                    in_=b.awo.ap()[:, m * 128:(m + 1) * 128].rearrange("(kc p) m -> p kc m", p=128))
                b.mark(f"{tag}_wo_{m}", i, b.d_w1b[j], 16)

            @block.sync
            def _(s, m=m):
                b.wait(s, f"{tag}_xw_{m-1}", f"{tag}_xa_{m-1}", *xdeps(m))
                i = s.dma_start(out=b.maske[:], in_=b.xdram.ap()[m * 128:(m + 1) * 128, :])
                b.mark(f"{tag}_xr_{m}", i, b.d_rr[m % 2], 16)

            @block.tensor
            def _(t, m=m, j=j):
                b.wait(t, f"{tag}_wo_{m}", f"{tag}_pg_{m-2}")
                if m == 0:
                    b.wait(t, f"{tag}_cb_{C-1}")
                pp = [b.pA, b.pB][m % 2]
                for kc in range(C):
                    for n in range(4):
                        i = t.matmul(pp[:, n * 512:(n + 1) * 512],
                                     b.w1b[j][:, kc * 128:(kc + 1) * 128],
                                     b.mm_in[kc][:, n * 512:(n + 1) * 512],
                                     start=(kc == 0), stop=(kc == C - 1))
                b.mark(f"{tag}_pmm_{m}", i, b.s_pe)

            @block.scalar
            def _(s, m=m):
                b.wait(s, f"{tag}_pmm_{m}", f"{tag}_xa_{m-2}")
                pp = [b.pA, b.pB][m % 2]
                i = s.activation(b.scr[m % 2][:], pp[:, :], AF.Gelu_apprx_tanh,
                                 bias=self.col("abo", m))
                b.mark(f"{tag}_pg_{m}", i, b.s_ac)

            @block.vector
            def _(v, m=m):
                b.wait(v, f"{tag}_pg_{m}", f"{tag}_xr_{m}")
                i = v.tensor_tensor(b.maske[:], b.scr[m % 2][:], b.maske[:], ALU.add)
                b.mark(f"{tag}_xa_{m}", i, b.s_ve)

            @block.sync
            def _(s, m=m):
                b.wait(s, f"{tag}_xa_{m}")
                i = s.dma_start(out=b.xdram.ap()[m * 128:(m + 1) * 128, :], in_=b.maske[:])
                b.mark(f"{tag}_xw_{m}", i, b.d_rw[m % 2], 16)

        b.ev[f"{tag}_done"] = b.ev[f"{tag}_xw_{C-1}"]
        b.ev["psfree"] = b.ev[f"{tag}_pg_{C-1}"]

    def emit(self):
        nc = self.nc
        with contextlib.ExitStack() as st:
            self.alloc(st)
            self.ptb = [st.enter_context(nc.sbuf_tensor(f"ptb_{i}", [128, 1024], BF16)) for i in range(2)]
            with nc.Block() as block:
                self.p_consts(block)
                self.p_ln(block, "ln0a", "x", "s2", deps=("consts_sy",))
                if self.upto == "y1":
                    self.p_dump(block, self.stream2, ("ln0a_done",))
                    return
                self.p_ln(block, "ln1a", "s2", "mm", affine=("ln1w", "ln1b"),
                          deps=("ln0a_done",))
                if self.upto == "ln1":
                    self.p_dump(block, self.mm_in, ("ln1a_done",))
                    return
                with contextlib.ExitStack() as msa_st:
                    if self.upto in ("k1", "cc1"):
                        self.p_kv(block, "kva", msa_st, partial="k" if self.upto == "k1" else "cc1")
                        self.p_dump(block, self.mm_in,
                                    ("kva_kd_6", "kva_kd_7") if self.upto == "k1" else ("kva_agm",))
                        return
                    self.p_kv(block, "kva", msa_st)
                    if self.upto == "k0":
                        self.p_dump(block, self.mm_in, ("kva_kd_7", "kva_md"))
                        return
                    if self.upto == "kv0":
                        self.p_dump(block, self.mm_in, ("kva_vd_15",))
                        return
                    if self.upto == "kv":
                        self.p_dump(block, self.mm_in, ("kva_agv", "kva_maxk2"))
                        return
                    if self.upto == "mh":
                        self.p_msa(block, "msaa", "kva", msa_st, pairs=8, stage="q")
                        b = self

                        @block.sync
                        def _(s):
                            b.wait(s, "msaa_ebv_7")
                            i = s.dma_start(out=b.dbg.ap()[0:1, 0:256], in_=b.hrow[:])
                            b.d_msc[0].inc(i, 16)
                            i = s.dma_start(out=b.dbg.ap()[100:228, 0:16], in_=b.ebias[:])
                            b.mark("mhdump", i, b.d_msc[0], 16)
                            b.wait(s, "mhdump")
                        return
                    if self.upto.startswith("msa_") or self.upto == "msa1":
                        pairs = 1 if self.upto.endswith("1") else 8
                        stage = {"msa_q": "q", "msa_sc": "sc", "msa_pv": "pv",
                                 "msa_q1": "q", "msa_sc1": "sc", "msa_pv1": "pv",
                                 "msa_ev1": "ev", "msa1": "all"}[self.upto]
                        self.p_msa(block, "msaa", "kva", msa_st, pairs=pairs, stage=stage)
                        self.p_dump(block, self.mm_in, ("msaa_pj_%d" % (pairs - 1),))
                        return
                    self.p_msa(block, "msaa", "kva", msa_st)
                if self.upto == "msa":
                    self.p_dump(block, self.stream2, ("msaa_done",))
                    return
                self.p_ln(block, "ln2a", "s2", "mm", affine=("ln2w", "ln2b"),
                          deps=("msaa_done",))
                self.p_gate(block, "gatea", deps=("ln2a_done",))
                if self.upto == "gate":
                    b = self

                    @block.sync
                    def _(s):
                        b.wait(s, "gatea_done")
                        i = s.dma_start(out=b.dbg.ap()[0:8, :], in_=b.maskfm[:])
                        b.d_msc[0].inc(i, 16)
                        i = s.dma_start(out=b.dbg.ap()[16:144, 0:128], in_=b.scr[0][:, 0:128])
                        b.mark("gdump", i, b.d_msc[0], 16)
                        b.wait(s, "gdump")
                    return
                with contextlib.ExitStack() as moe_st:
                    self.p_moe(block, "moea", moe_st, deps=("gatea_done", "ln2a_done"))
                if self.upto == "moe":
                    self.p_dump(block, self.stream2, ("moea_done",))
                    return
                self.p_resid(block, "r1", 0.5, deps=("moea_done",))
                self.p_ln(block, "ln0b", "x", "mm", deps=lambda c: (f"r1_w_{c}",))
                self.p_lin(block, "linb", deps=("ln0b_done",),
                           xdeps=lambda m: (f"r1_w_{m}",))
                if self.upto == "lin":
                    self.p_dump_dram(block, lambda c: (f"linb_xw_{c}",))
                    return
                self.p_ln(block, "ln0c", "x", "s2", deps=lambda c: (f"linb_xw_{c}",))
                self.p_ln(block, "ln1c", "s2", "mm", affine=("ln1w", "ln1b"),
                          deps=("ln0c_done",))
                with contextlib.ExitStack() as msa_st2:
                    self.p_kv(block, "kvb", msa_st2, deps=("linb_done", "moea_done"))
                    self.p_msa(block, "msab", "kvb", msa_st2, deps=("linb_done", "moea_done"))
                self.p_ln(block, "ln2b", "s2", "mm", affine=("ln2w", "ln2b"),
                          deps=("msab_done",))
                self.p_gate(block, "gateb", deps=("ln2b_done",))
                with contextlib.ExitStack() as moe_st2:
                    self.p_moe(block, "moeb", moe_st2, deps=("gateb_done", "ln2b_done"))
                self.p_resid(block, "r2", 0.5, final=True,
                             deps=lambda c: ("moeb_done", f"linb_xw_{c}"))

                @block.sync
                def _(s):
                    self.wait(s, "r2_w_6", "r2_w_7")

    def p_dump_dram(self, block, depf):
        b = self

        @block.sync
        def _(s):
            for c in range(C):
                b.wait(s, *depf(c))
            i = s.dma_start(out=b.dbg.ap(), in_=b.xdram.ap())
            b.mark("dmpd", i, b.d_msc[0], 16)
            b.wait(s, "dmpd")

    def p_dump(self, block, tensors, deps):
        b = self
        for c in range(C):
            i2 = c % 2

            @block.vector
            def _(v, c=c, i2=i2):
                b.wait(v, *deps)
                b.wait(v, f"dmp_w_{c-2}")
                i = v.tensor_copy(b.scr[i2][:], tensors[c][:])
                b.mark(f"dmp_v_{c}", i, b.s_ve)

            @block.sync
            def _(s, c=c, i2=i2):
                b.wait(s, f"dmp_v_{c}")
                i = s.dma_start(out=b.dbg.ap()[c * 128:(c + 1) * 128, :], in_=b.scr[i2][:])
                b.mark(f"dmp_w_{c}", i, b.d_rw[i2], 16)

        @block.sync
        def _(s):
            b.wait(s, f"dmp_w_{C-2}", f"dmp_w_{C-1}")


def build_nc(upto="full"):
    nc = bass.Bass("TRN2", target_bir_lowering=False, debug=False,
                   detect_race_conditions=False)
    bb = B(nc, upto)
    bb.declare()
    bb.emit()
    return nc


def make_core_inputs(inputs, core):
    b_, hf = core // 2, core % 2
    f32 = np.float32
    bfl = ml_dtypes.bfloat16
    x = np.asarray(inputs["L0_attn_x"])
    xs = np.ascontiguousarray(x[b_, hf * T:(hf + 1) * T, :].T).astype(f32)
    biascols = np.concatenate([
        np.asarray(inputs["L0_attn_bq"]).reshape(D, 1),
        np.asarray(inputs["L0_attn_bk"]).reshape(D, 1),
        np.asarray(inputs["L0_attn_bv"]).reshape(D, 1),
        np.asarray(inputs["L0_attn_bo"]).reshape(D, 1),
        np.asarray(inputs["L0_ffn_ln1_w"]).reshape(D, 1),
        np.asarray(inputs["L0_ffn_ln1_b"]).reshape(D, 1),
        np.asarray(inputs["L0_ffn_ln2_w"]).reshape(D, 1),
        np.asarray(inputs["L0_ffn_ln2_b"]).reshape(D, 1),
    ], axis=1).astype(f32)
    return {
        "x_in": xs,
        "awq": np.ascontiguousarray(np.asarray(inputs["L0_attn_wq"]).reshape(D, D)).astype(f32),
        "awk": np.ascontiguousarray(np.asarray(inputs["L0_attn_wk"]).reshape(D, D)).astype(f32),
        "awv": np.ascontiguousarray(np.asarray(inputs["L0_attn_wv"]).reshape(D, D)).astype(f32),
        "awo": np.ascontiguousarray(np.asarray(inputs["L0_attn_wo"]).reshape(D, D)).astype(f32),
        "wqkv": np.ascontiguousarray(np.asarray(inputs["L0_ffn_w_qkv"])).astype(f32),
        "wproj": np.ascontiguousarray(np.asarray(inputs["L0_ffn_w_proj"])).astype(f32),
        "biascols": np.ascontiguousarray(biascols),
        "gate_w": np.ascontiguousarray(np.asarray(inputs["L0_ffn_gate_w"])).astype(f32),
        "gate_b": np.ascontiguousarray(np.asarray(inputs["L0_ffn_gate_b"]).reshape(1, E)).astype(f32),
        "w1": np.ascontiguousarray(np.asarray(inputs["L0_ffn_w1"])).astype(f32),
        "w2": np.ascontiguousarray(np.asarray(inputs["L0_ffn_w2"])).astype(f32),
        "ones_f": np.ones((1, 128), f32),
        "negones_f": np.full((1, 128), -1.0, f32),
        "ones_bf": np.ones((1, 128), bfl),
        "ones65_f": np.concatenate([np.ones((32, 128), f32), -np.ones((32, 128), f32), np.ones((1, 128), f32)], axis=0),
        "onescol_bf": np.ones((128, 1), bfl),
        "onesm_bf": np.ones((128, 128), bfl),
        "ident": np.eye(128, dtype=f32),
        "ind8": np.kron(np.eye(E, dtype=f32), np.ones((1, 128), f32)),
    }


def kernel(**inputs):
    nc = build_nc("full")
    in_maps = [make_core_inputs(inputs, c) for c in range(8)]
    res = run_bass_kernel_spmd(nc, in_maps, core_ids=list(range(8)))
    out = np.zeros((4, SEQ, D), np.float32)
    for c in range(8):
        b_, hf = c // 2, c % 2
        out[b_, hf * T:(hf + 1) * T, :] = res.results[c]["x_out"].T
    return out


# version 02
# speedup vs baseline: 1.2575x; 1.2575x over previous
import sys
import contextlib
import numpy as np
import ml_dtypes

sys.path.insert(0, "/opt/trn_rl_repo")
import concourse.bass as bass
import concourse.mybir as mybir
from concourse.bass_utils import run_bass_kernel_spmd

F32 = mybir.dt.float32
BF16 = mybir.dt.bfloat16
AF = mybir.ActivationFunctionType
ALU = mybir.AluOpType
AX = mybir.AxisListType

D = 1024
T = 2048
C = 8
NT = 16
H = 16
DK = 64
E = 8
HID = 4096
SEQ = 4096
EPS_LN = 1e-5
EPS_LA = 1e-6
INV_D = 1.0 / D
LN_EIGHTH = float(np.log(0.125))


class Ctr:
    def __init__(self, sem):
        self.sem = sem
        self.n = 0

    def inc(self, instr, k=1):
        instr.then_inc(self.sem, k)
        self.n += k
        return self.n


class B:

    def __init__(self, nc, upto):
        self.nc = nc
        self.upto = upto
        self.ev = {}

    def mark(self, key, instr, ctr, k=1):
        ctr.inc(instr, k)
        self.ev[key] = (ctr, ctr.n)

    def wait(self, eng, *keys):
        for key in keys:
            if key is None:
                continue
            e = self.ev.get(key) if isinstance(key, str) else key
            if e is None:
                continue
            ctr, n = e
            if n > 0:
                eng.wait_ge(ctr.sem, n)

    def declare(self):
        nc = self.nc

        def din(name, shape, dt=F32):
            return nc.dram_tensor(name, shape, dt, kind="ExternalInput")

        self.x_in = din("x_in", [D, T])
        self.awq = din("awq", [D, D]); self.awk = din("awk", [D, D]); self.awv = din("awv", [D, D])
        self.awo = din("awo", [D, D])
        self.wqkv = din("wqkv", [D, 3 * D])
        self.wproj = din("wproj", [D, D])
        self.biascols = din("biascols", [D, 8])
        self.gate_w = din("gate_w", [D, E])
        self.gate_b = din("gate_b", [1, E])
        self.w1 = din("w1", [E, D, HID])
        self.w2 = din("w2", [E, HID, D])
        self.ones_f = din("ones_f", [1, 128])
        self.negones_f = din("negones_f", [1, 128])
        self.ones_bf = din("ones_bf", [1, 128], BF16)
        self.ones65_f = din("ones65_f", [65, 128])
        self.onescol_bf = din("onescol_bf", [128, 1], BF16)
        self.onesm_bf = din("onesm_bf", [128, 128], BF16)
        self.ident = din("ident", [128, 128])
        self.ind8 = din("ind8", [E, E * 128])

        self.x_out = nc.dram_tensor("x_out", [D, T], F32, kind="ExternalOutput")
        if self.upto != "full":
            self.dbg = nc.dram_tensor("dbg", [D, T], F32, kind="ExternalOutput")

        self.xdram = nc.dram_tensor("xdram", [D, T], F32)
        self.agk_in = nc.dram_tensor("agk_in", [D, T], BF16)
        self.agk_out = nc.dram_tensor("agk_out", [2 * D, T], BF16)
        self.agv_in = nc.dram_tensor("agv_in", [T, D], BF16)
        self.agv_out = nc.dram_tensor("agv_out", [2 * T, D], BF16)
        self.agm_in = nc.dram_tensor("agm_in", [1, H], F32)
        self.agm_out = nc.dram_tensor("agm_out", [2, H], F32)
        self.arl_in = nc.dram_tensor("arl_in", [128, 16], F32)
        self.arl_out = nc.dram_tensor("arl_out", [128, 16], F32)

    def alloc(self, st):
        nc = self.nc

        def sb(name, shape, dt):
            return st.enter_context(nc.sbuf_tensor(name, shape, dt))

        def psum(name, shape, dt=F32):
            return st.enter_context(nc.psum_tensor(name, shape, dt))

        def sem(name):
            return Ctr(st.enter_context(nc.semaphore(name)))

        self.sb_ = sb
        self.stream2 = [sb(f"stream2_{c}", [128, T], BF16) for c in range(C)]
        self.mm_in = [sb(f"mm_in_{c}", [128, T], BF16) for c in range(C)]
        self.scr = [sb(f"scr_{i}", [128, T], F32) for i in range(2)]
        self.xb = [sb(f"xb_{i}", [128, T], BF16) for i in range(2)]
        self.sqb = [sb(f"sqb_{i}", [128, T], BF16) for i in range(2)]
        self.rows = sb("rows", [65, T], F32)
        self.maskfm = sb("maskfm", [E, T], F32)
        self.maske = sb("maske", [128, T], F32)
        self.w1b = [sb(f"w1b_{i}", [128, 8 * 128], BF16) for i in range(2)]
        self.wsm = [sb(f"wsm_{i}", [128, D], BF16) for i in range(2)]
        self.cols8 = sb("cols8", [128, 64], F32)
        self.cols8n = sb("cols8n", [128, 24], F32)
        self.lincols = sb("lincols", [128, 32], F32)
        self.onesc = sb("onesc", [128, 1], BF16)
        self.onesf = sb("onesf", [1, 128], F32)
        self.negonesf = sb("negonesf", [1, 128], F32)
        self.onesbf = sb("onesbf", [1, 128], BF16)
        self.ones65 = sb("ones65", [65, 128], F32)
        self.onesm = sb("onesm", [128, 128], BF16)
        self.identt = sb("identt", [128, 128], F32)
        self.ind8t = sb("ind8t", [E, E * 128], F32)
        self.gwb = sb("gwb", [128, C * E], BF16)
        self.gbb = sb("gbb", [1, E], BF16)
        self.hrow = sb("hrow", [1, 256], F32)
        self.ebias = sb("ebias", [128, H], F32)

        self.pA = psum("pA", [128, T])
        self.pB = psum("pB", [128, T])

        self.s_sy = sem("s_sy")
        self.s_g = sem("s_g")
        self.s_pe = sem("s_pe")
        self.s_ac = sem("s_ac")
        self.s_ve = sem("s_ve")
        self.d_cs = sem("d_cs")
        self.d_cg = sem("d_cg")
        self.d_xb = [sem(f"d_xb{i}") for i in range(2)]
        self.d_w1b = [sem(f"d_w1b{i}") for i in range(2)]
        self.d_wsm = [sem(f"d_wsm{i}") for i in range(2)]
        self.d_kt = [sem(f"d_kt{i}") for i in range(2)]
        self.d_vt = [sem(f"d_vt{i}") for i in range(4)]
        self.d_kd = [sem(f"d_kd{i}") for i in range(2)]
        self.d_vd = [sem(f"d_vd{i}") for i in range(2)]
        self.d_msc = [sem(f"d_msc{i}") for i in range(2)]
        self.d_rr = [sem(f"d_rr{i}") for i in range(2)]
        self.d_rw = [sem(f"d_rw{i}") for i in range(2)]
        self.d_vsl = sem("d_vsl")
        self.d_w2 = sem("d_w2")
        self.c_k = sem("c_k")
        self.c_m = sem("c_m")
        self.c_v = sem("c_v")
        self.c_l = sem("c_l")

        self.colmap = dict(abq=0, abk=1, abv=2, abo=3, ln1w=4, ln1b=5, ln2w=6, ln2b=7)

    def col(self, name, c):
        o = c * 8 + self.colmap[name]
        return self.cols8[:, o:o + 1]

    def ncol(self, name, c):
        o = c * 3 + dict(abq=0, abk=1, abv=2)[name]
        return self.cols8n[:, o:o + 1]

    def p_consts(self, block):
        b = self

        @block.sync
        def _(s):
            for out, in_ in [
                (b.xdram.ap(), b.x_in.ap()),
                (b.onesf[:], b.ones_f.ap()),
                (b.negonesf[:], b.negones_f.ap()),
                (b.onesbf[:], b.ones_bf.ap()),
                (b.ones65[:], b.ones65_f.ap()),
                (b.onesc[:], b.onescol_bf.ap()),
                (b.onesm[:], b.onesm_bf.ap()),
                (b.identt[:], b.ident.ap()),
                (b.ind8t[:], b.ind8.ap()),
            ]:
                i = s.dma_start(out=out, in_=in_)
                b.d_cs.inc(i, 16)
            i = s.dma_start(
                out=b.cols8[:].rearrange("p (c x) -> p c x", c=C),
                in_=b.biascols.ap().rearrange("(c p) x -> p c x", p=128),
            )
            b.mark("consts_sy", i, b.d_cs, 16)

        @block.gpsimd
        def _(g):
            i = g.dma_start(
                out=b.gwb[:].rearrange("p (c e) -> p c e", c=C),
                in_=b.gate_w.ap().rearrange("(c p) e -> p c e", p=128),
            )
            b.d_cg.inc(i, 16)
            i = g.dma_start(out=b.gbb[:], in_=b.gate_b.ap())
            b.mark("consts_g", i, b.d_cg, 16)

        @block.vector
        def _(v):
            b.wait(v, "consts_sy")
            for c in range(C):
                for name in ("abq", "abk", "abv"):
                    i = v.tensor_scalar(b.ncol(name, c), b.col(name, c), -1.0, None, ALU.mult)
            b.mark("consts_v", i, b.s_ve)

    def p_ln(self, block, tag, src, dst, affine=None, deps=()):
        b = self
        psfree = b.ev.get("psfree")
        depf = deps if callable(deps) else (lambda c: deps)

        for c in range(C):
            i2 = c % 2
            if src == "x":
                @block.gpsimd
                def _(g, c=c, i2=i2):
                    b.wait(g, f"{tag}_st_{c-2}", f"{tag}_sq_{c-2}", *depf(c))
                    i = g.dma_start(out=b.xb[i2][:], in_=b.xdram.ap()[c * 128:(c + 1) * 128, :])
                    b.mark(f"{tag}_xb_{c}", i, b.d_xb[i2], 16)
                srcb = b.xb[i2]
                srcdep = f"{tag}_xb_{c}"
            else:
                srcb = b.stream2[c]
                srcdep = None

            @block.scalar
            def _(s, c=c, i2=i2, srcb=srcb, srcdep=srcdep):
                b.wait(s, srcdep, f"{tag}_st_{c-2}", *(depf(c) if src != "x" else ()))
                i = s.activation(b.sqb[i2][:], srcb[:], AF.Square)
                b.mark(f"{tag}_sq_{c}", i, b.s_ac)

            @block.tensor
            def _(t, c=c, i2=i2, srcb=srcb, srcdep=srcdep):
                b.wait(t, srcdep, f"{tag}_sq_{c}")
                if c == 0:
                    b.wait(t, psfree)
                for n in range(4):
                    t.matmul(b.pA[:, n * 512:(n + 1) * 512], b.onesm[:],
                             srcb[:, n * 512:(n + 1) * 512], start=(c == 0), stop=(c == C - 1))
                for n in range(4):
                    i = t.matmul(b.pB[:, n * 512:(n + 1) * 512], b.onesm[:],
                                 b.sqb[i2][:, n * 512:(n + 1) * 512], start=(c == 0), stop=(c == C - 1))
                b.mark(f"{tag}_st_{c}", i, b.s_pe)

        @block.vector
        def _(v):
            b.wait(v, f"{tag}_st_{C-1}")
            v.tensor_scalar(b.rows[0:1, :], b.pA[0:1, :], INV_D, None, ALU.mult)
            v.tensor_scalar(b.rows[32:33, :], b.pB[0:1, :], INV_D, EPS_LN, ALU.mult, ALU.add)
            v.tensor_tensor(b.pB[0:1, :], b.rows[0:1, :], b.rows[0:1, :], ALU.mult)
            i = v.tensor_tensor(b.rows[32:33, :], b.rows[32:33, :], b.pB[0:1, :], ALU.subtract)
            b.mark(f"{tag}_rows", i, b.s_ve)

        @block.scalar
        def _(s):
            b.wait(s, f"{tag}_rows")
            s.activation(b.rows[64:65, :], b.rows[32:33, :], AF.Ln)
            i = s.activation(b.rows[32:33, :], b.rows[64:65, :], AF.Exp, scale=-0.5)
            b.mark(f"{tag}_rstd", i, b.s_ac)

        @block.tensor
        def _(t):
            b.wait(t, f"{tag}_rstd")
            for n in range(4):
                i = t.matmul(b.pB[:, n * 512:(n + 1) * 512], b.ones65[32:33, :],
                             b.rows[32:33, n * 512:(n + 1) * 512], start=True, stop=True)
            b.mark(f"{tag}_bc", i, b.s_pe)

        for c in range(C):
            i2 = c % 2
            if src == "x":
                @block.gpsimd
                def _(g, c=c, i2=i2):
                    b.wait(g, f"{tag}_st_{C-1}" if c < 2 else f"{tag}_n_{c-2}")
                    i = g.dma_start(out=b.xb[i2][:], in_=b.xdram.ap()[c * 128:(c + 1) * 128, :])
                    b.mark(f"{tag}_xb2_{c}", i, b.d_xb[i2], 16)
                srcb = b.xb[i2]
                srcdep = f"{tag}_xb2_{c}"
            else:
                srcb = b.stream2[c]
                srcdep = None

            @block.vector
            def _(v, c=c, i2=i2, srcb=srcb, srcdep=srcdep):
                b.wait(v, f"{tag}_bc", srcdep)
                v.scalar_tensor_tensor(b.scr[i2][:], b.pA[:, :], INV_D, srcb[:],
                                       ALU.mult, ALU.subtract)
                if dst == "s2" and not affine:
                    i = v.tensor_tensor(b.stream2[c][:], b.scr[i2][:], b.pB[:, :], ALU.mult)
                elif affine:
                    v.tensor_tensor(b.scr[i2][:], b.scr[i2][:], b.pB[:, :], ALU.mult)
                    i = v.tensor_scalar(b.mm_in[c][:], b.scr[i2][:],
                                        self.col(affine[0], c), self.col(affine[1], c),
                                        ALU.mult, ALU.add)
                else:
                    i = v.tensor_tensor(b.mm_in[c][:], b.scr[i2][:], b.pB[:, :], ALU.mult)
                b.mark(f"{tag}_n_{c}", i, b.s_ve)

        b.ev["psfree"] = b.ev[f"{tag}_n_{C-1}"]
        b.ev[f"{tag}_done"] = b.ev[f"{tag}_n_{C-1}"]
        b.ev[f"{tag}_done_prev"] = b.ev[f"{tag}_n_{C-2}"]

    def p_resid(self, block, tag, scale, final=False, deps=()):
        b = self
        depf = deps if callable(deps) else (lambda c: deps)
        for c in range(C):
            i2 = c % 2

            @block.sync
            def _(s, c=c, i2=i2):
                b.wait(s, f"{tag}_w_{c-2}", *depf(c))
                i = s.dma_start(out=b.scr[i2][:], in_=b.xdram.ap()[c * 128:(c + 1) * 128, :])
                b.mark(f"{tag}_r_{c}", i, b.d_rr[i2], 16)

            @block.vector
            def _(v, c=c, i2=i2):
                b.wait(v, f"{tag}_r_{c}")
                i = v.scalar_tensor_tensor(b.scr[i2][:], b.stream2[c][:], scale,
                                           b.scr[i2][:], ALU.mult, ALU.add)
                b.mark(f"{tag}_v_{c}", i, b.s_ve)

            @block.sync
            def _(s, c=c, i2=i2):
                b.wait(s, f"{tag}_v_{c}")
                tgt = b.x_out.ap() if final else b.xdram.ap()
                i = s.dma_start(out=tgt[c * 128:(c + 1) * 128, :], in_=b.scr[i2][:])
                if not final:
                    b.mark(f"{tag}_w_{c}", i, b.d_rw[i2], 16)
                else:
                    b.d_rw[i2].inc(i, 16)
                    i2_ = s.dma_start(out=b.xdram.ap()[c * 128:(c + 1) * 128, :], in_=b.scr[i2][:])
                    b.mark(f"{tag}_w_{c}", i2_, b.d_rw[i2], 16)
        b.ev[f"{tag}_done"] = b.ev[f"{tag}_w_{C-1}"]

    def p_kv(self, block, tag, st, partial=None, deps=()):
        b = self
        wq = self.wqkv.ap()
        groups = [[0, 1], [2, 3], [4, 5], [6, 7]]

        for ck in range(C):
            j = ck % 2
            i2 = ck % 2

            @block.gpsimd
            def _(g, ck=ck, j=j):
                b.wait(g, f"{tag}_kmm_{ck-2}")
                if ck == 0:
                    b.wait(g, "consts_g", *deps)
                i = g.dma_start(
                    out=b.w1b[j][:].rearrange("p (kc m) -> p kc m", kc=C),
                    in_=wq[:, D + ck * 128: D + (ck + 1) * 128].rearrange("(kc p) m -> p kc m", p=128),
                )
                b.mark(f"{tag}_kw_{ck}", i, b.d_w1b[j], 16)

            @block.tensor
            def _(t, ck=ck, j=j):
                b.wait(t, f"{tag}_kw_{ck}", f"{tag}_ks_{ck-1}")
                if ck == 0:
                    b.wait(t, "psfree")
                for kc in range(C):
                    for n in range(4):
                        i = t.matmul(b.pA[:, n * 512:(n + 1) * 512],
                                     b.w1b[j][:, kc * 128:(kc + 1) * 128],
                                     b.mm_in[kc][:, n * 512:(n + 1) * 512],
                                     start=(kc == 0), stop=(kc == C - 1))
                b.mark(f"{tag}_kmm_{ck}", i, b.s_pe)

            @block.scalar
            def _(s, ck=ck, i2=i2):
                b.wait(s, f"{tag}_kmm_{ck}", f"{tag}_kd_{ck-2}", f"{tag}_ks2_{ck-2}")
                i = s.activation(b.xb[i2][:], b.pA[:, :], AF.Copy)
                b.mark(f"{tag}_kc_{ck}", i, b.s_ac)
                i = s.activation(b.sqb[i2][:], b.pA[:, :], AF.Square)
                b.mark(f"{tag}_ks_{ck}", i, b.s_ac)

            for h2 in range(2):
                @block.tensor
                def _(t, ck=ck, i2=i2, h2=h2):
                    b.wait(t, f"{tag}_ks_{ck}" if h2 == 0 else f"{tag}_kmh_{ck}_0",
                           f"{tag}_km_{ck-1}" if h2 == 0 else None)
                    for n in range(4):
                        i = t.matmul(b.pB[:, n * 512:(n + 1) * 512],
                                     b.onesm[h2 * 64:(h2 + 1) * 64, :],
                                     b.sqb[i2][h2 * 64:(h2 + 1) * 64, n * 512:(n + 1) * 512],
                                     start=True, stop=True)
                    b.mark(f"{tag}_ksum_{ck}_{h2}", i, b.s_pe)

                @block.vector
                def _(v, ck=ck, h2=h2):
                    b.wait(v, f"{tag}_ksum_{ck}_{h2}")
                    hh = ck * 2 + h2
                    i = v.reduce_max(b.hrow[0:1, hh:hh + 1], b.pB[0:1, :], axis=AX.X)
                    b.mark(f"{tag}_kmh_{ck}_{h2}", i, b.s_ve)
            b.ev[f"{tag}_ks2_{ck}"] = b.ev[f"{tag}_ksum_{ck}_1"]
            b.ev[f"{tag}_km_{ck}"] = b.ev[f"{tag}_kmh_{ck}_1"]

            @block.sync
            def _(s, ck=ck, i2=i2):
                b.wait(s, f"{tag}_kc_{ck}")
                i = s.dma_start(out=b.agk_in.ap()[ck * 128:(ck + 1) * 128, :], in_=b.xb[i2][:])
                b.mark(f"{tag}_kd_{ck}", i, b.d_kd[i2], 16)

        if partial == "k":
            return
        @block.sync
        def _(s):
            b.wait(s, f"{tag}_km_{C-1}")
            i = s.dma_start(out=b.agm_in.ap(), in_=b.hrow[0:1, 0:16])
            b.mark(f"{tag}_md", i, b.d_msc[0], 16)

        @block.gpsimd
        def _(g):
            b.wait(g, f"{tag}_kd_{C-2}", f"{tag}_kd_{C-1}", f"{tag}_md")
            i = g.collective_compute(
                "AllGather", ALU.bypass, replica_groups=groups,
                ins=[b.agk_in.ap().opt()], outs=[b.agk_out.ap().opt()])
            b.mark(f"{tag}_agk", i, b.c_k)
            i = g.collective_compute(
                "AllGather", ALU.bypass, replica_groups=groups,
                ins=[b.agm_in.ap().opt()], outs=[b.agm_out.ap().opt()])
            b.mark(f"{tag}_agm", i, b.c_m)

        if partial == "cc1":
            return
        vsl = [st.enter_context(self.nc.sbuf_tensor(f"{tag}_vsl_{kc}", [128, D], BF16))
               for kc in range(C)]

        @block.gpsimd
        def _(g):
            b.wait(g, *deps)
            for kc in range(C):
                i = g.dma_start(out=vsl[kc][:], in_=wq[kc * 128:(kc + 1) * 128, 2 * D:3 * D])
                b.d_vsl.inc(i, 16)
            b.ev[f"{tag}_vw"] = (b.d_vsl, b.d_vsl.n)

        for tt in range(NT):
            i2 = tt % 2
            pp = [b.pA, b.pB][tt % 2]

            @block.tensor
            def _(t, tt=tt, pp=pp):
                b.wait(t, f"{tag}_vw" if tt == 0 else None, f"{tag}_vev_{tt-2}")
                if tt == 0:
                    b.wait(t, f"{tag}_ks_{C-1}")
                if tt == 1:
                    b.wait(t, f"{tag}_km_{C-1}")
                for kc in range(C):
                    for n in range(2):
                        i = t.matmul(pp[:, n * 512:(n + 1) * 512],
                                     b.mm_in[kc][:, tt * 128:(tt + 1) * 128],
                                     vsl[kc][:, n * 512:(n + 1) * 512],
                                     start=(kc == 0), stop=(kc == C - 1))
                b.mark(f"{tag}_vmm_{tt}", i, b.s_pe)

            @block.scalar
            def _(s, tt=tt, i2=i2, pp=pp):
                b.wait(s, f"{tag}_vmm_{tt}", f"{tag}_vd_{tt-2}")
                if tt < 2:
                    b.wait(s, f"{tag}_ks2_{C-1}")
                i = s.activation(b.sqb[i2][:, 0:1024], pp[:, 0:1024], AF.Copy)
                b.mark(f"{tag}_vev_{tt}", i, b.s_ac)

            @block.sync
            def _(s, tt=tt, i2=i2):
                b.wait(s, f"{tag}_vev_{tt}")
                i = s.dma_start(out=b.agv_in.ap()[tt * 128:(tt + 1) * 128, :], in_=b.sqb[i2][:, 0:1024])
                b.mark(f"{tag}_vd_{tt}", i, b.d_vd[i2], 16)

        @block.gpsimd
        def _(g):
            b.wait(g, f"{tag}_vd_{NT-2}", f"{tag}_vd_{NT-1}")
            i = g.collective_compute(
                "AllGather", ALU.bypass, replica_groups=groups,
                ins=[b.agv_in.ap().opt()], outs=[b.agv_out.ap().opt()])
            b.mark(f"{tag}_agv", i, b.c_v)

        @block.sync
        def _(s):
            b.wait(s, f"{tag}_agm")
            i = s.dma_start(out=b.hrow[0:1, 16:32], in_=b.agm_out.ap()[0:1, :])
            b.d_msc[1].inc(i, 16)
            i = s.dma_start(out=b.hrow[0:1, 32:48], in_=b.agm_out.ap()[1:2, :])
            b.mark(f"{tag}_mg", i, b.d_msc[1], 16)

        @block.vector
        def _(v):
            b.wait(v, f"{tag}_mg")
            i = v.tensor_tensor(b.hrow[0:1, 16:32], b.hrow[0:1, 16:32], b.hrow[0:1, 32:48], ALU.max)
            b.mark(f"{tag}_maxk2", i, b.s_ve)

        b.ev["psfree"] = b.ev[f"{tag}_vev_{NT-1}"]

    def p_msa(self, block, tag, kvtag, st, pairs=8, stage="all", deps=()):
        b = self
        kt = [st.enter_context(self.nc.sbuf_tensor(f"{tag}_kt_{i}", [128, SEQ], BF16)) for i in range(2)]
        qp = [st.enter_context(self.nc.sbuf_tensor(f"{tag}_qp_{i}", [128, T], BF16)) for i in range(2)]
        ap_ = [st.enter_context(self.nc.sbuf_tensor(f"{tag}_ap_{i}", [128, T], BF16)) for i in range(2)]
        vt = [st.enter_context(self.nc.sbuf_tensor(f"{tag}_vt_{i}", [128, 65], BF16)) for i in range(4)]

        @block.vector
        def _(v):
            b.wait(v, "consts_sy", *deps)
            for i_ in range(4):
                i = v.tensor_copy(vt[i_][:, 64:65], b.onesc[:])
            b.mark(f"{tag}_vtones", i, b.s_ve)

        for hp in range(pairs):
            j = hp % 2

            @block.gpsimd
            def _(g, hp=hp, j=j):
                b.wait(g, f"{tag}_qmm_{hp-2}")
                if hp < 2:
                    b.wait(g, f"{kvtag}_kmm_{C-1}", *deps)
                i = g.dma_start(
                    out=b.w1b[j][:].rearrange("p (kc m) -> p kc m", kc=C),
                    in_=b.wqkv.ap()[:, hp * 128:(hp + 1) * 128].rearrange("(kc p) m -> p kc m", p=128),
                )
                b.mark(f"{tag}_qw_{hp}", i, b.d_w1b[j], 16)
                b.wait(g, f"{tag}_pj_{hp-2}")
                i = g.dma_start(out=b.wsm[j][:], in_=b.wproj.ap()[hp * 128:(hp + 1) * 128, :])
                b.mark(f"{tag}_pw_{hp}", i, b.d_wsm[j], 16)

            @block.sync
            def _(s, hp=hp, j=j):
                b.wait(s, f"{kvtag}_agk", f"{tag}_sc_done_{hp-2}")
                if hp < 2:
                    b.wait(s, *deps)
                i = s.dma_start(out=kt[j][:, 0:T], in_=b.agk_out.ap()[hp * 128:(hp + 1) * 128, :])
                b.d_kt[j].inc(i, 16)
                i = s.dma_start(out=kt[j][:, T:SEQ], in_=b.agk_out.ap()[D + hp * 128:D + (hp + 1) * 128, :])
                b.mark(f"{tag}_kt_{hp}", i, b.d_kt[j], 16)

            @block.tensor
            def _(t, hp=hp, j=j):
                b.wait(t, f"{tag}_qw_{hp}", f"{tag}_pj_{hp-1}")
                if hp == 0:
                    b.wait(t, "psfree")
                for kc in range(C):
                    for n in range(4):
                        i = t.matmul(b.pA[:, n * 512:(n + 1) * 512],
                                     b.w1b[j][:, kc * 128:(kc + 1) * 128],
                                     b.mm_in[kc][:, n * 512:(n + 1) * 512],
                                     start=(kc == 0), stop=(kc == C - 1))
                b.mark(f"{tag}_qmm_{hp}", i, b.s_pe)

            @block.scalar
            def _(s, hp=hp, j=j):
                b.wait(s, f"{tag}_qmm_{hp}")
                s.activation(qp[j][:], b.pA[:, :], AF.Copy)
                i = s.activation(b.sqb[0][:], b.pA[:, :], AF.Square)
                b.mark(f"{tag}_qe_{hp}", i, b.s_ac)

            for h2 in range(2):
                @block.tensor
                def _(t, hp=hp, h2=h2):
                    if h2 == 0:
                        b.wait(t, f"{tag}_qe_{hp}", f"{tag}_pvc_{hp-1}_1_1", f"{tag}_rc_{hp-1}_1_1")
                    else:
                        b.wait(t, f"{tag}_qmh_{hp}_0")
                    for n in range(4):
                        i = t.matmul(b.pB[:, n * 512:(n + 1) * 512],
                                     b.onesm[h2 * 64:(h2 + 1) * 64, :],
                                     b.sqb[0][h2 * 64:(h2 + 1) * 64, n * 512:(n + 1) * 512],
                                     start=True, stop=True)
                    b.mark(f"{tag}_qsum_{hp}_{h2}", i, b.s_pe)

                @block.vector
                def _(v, hp=hp, h2=h2):
                    h = hp * 2 + h2
                    b.wait(v, f"{tag}_qsum_{hp}_{h2}")
                    i = v.reduce_max(b.hrow[0:1, 64 + h:65 + h], b.pB[0:1, :], axis=AX.X)
                    b.mark(f"{tag}_qmh_{hp}_{h2}", i, b.s_ve)

            @block.vector
            def _(v, hp=hp):
                b.wait(v, f"{kvtag}_maxk2", f"{tag}_qmh_{hp}_0", f"{tag}_qmh_{hp}_1")
                i = v.tensor_tensor(b.hrow[0:1, 96 + hp * 2:98 + hp * 2],
                                    b.hrow[0:1, 64 + hp * 2:66 + hp * 2],
                                    b.hrow[0:1, 16 + hp * 2:18 + hp * 2], ALU.mult)
                b.mark(f"{tag}_qmt_{hp}", i, b.s_ve)
                b.wait(v, f"{tag}_qmt_{hp}")
                i = v.tensor_scalar(b.hrow[0:1, 96 + hp * 2:98 + hp * 2],
                                    b.hrow[0:1, 96 + hp * 2:98 + hp * 2], 1.0 / 64, None, ALU.mult)
                b.mark(f"{tag}_qm_{hp}", i, b.s_ve)

            @block.scalar
            def _(s, hp=hp):
                b.wait(s, f"{tag}_qm_{hp}")
                i = s.activation(b.hrow[0:1, 128 + hp * 2:130 + hp * 2],
                                 b.hrow[0:1, 96 + hp * 2:98 + hp * 2], AF.Ln)
                b.mark(f"{tag}_mhl_{hp}", i, b.s_ac)
                s.wait_ge(b.s_ac.sem, b.s_ac.n)
                i = s.activation(b.hrow[0:1, 128 + hp * 2:130 + hp * 2],
                                 b.hrow[0:1, 128 + hp * 2:130 + hp * 2], AF.Exp, scale=0.5)
                b.mark(f"{tag}_mh_{hp}", i, b.s_ac)

            @block.tensor
            def _(t, hp=hp):
                b.wait(t, f"{tag}_mh_{hp}", f"{tag}_ebv_{hp-1}")
                for h2 in range(2):
                    h = hp * 2 + h2
                    i = t.matmul(b.pB[:, 1024 + h2 * 512:1025 + h2 * 512],
                                 b.negonesf[:], b.hrow[0:1, 128 + h:129 + h],
                                 start=True, stop=True)
                b.mark(f"{tag}_ebmm_{hp}", i, b.s_pe)

            @block.vector
            def _(v, hp=hp):
                b.wait(v, f"{tag}_ebmm_{hp}")
                for h2 in range(2):
                    h = hp * 2 + h2
                    i = v.tensor_copy(b.ebias[:, h:h + 1], b.pB[:, 1024 + h2 * 512:1025 + h2 * 512])
                b.mark(f"{tag}_ebv_{hp}", i, b.s_ve)

            if stage == "q":
                b.ev[f"{tag}_sc_done_{hp}"] = b.ev[f"{tag}_ebv_{hp}"]
                b.ev[f"{tag}_pj_{hp}"] = b.ev[f"{tag}_ebv_{hp}"]
                b.ev[f"{tag}_pvc_{hp}_1_1"] = b.ev[f"{tag}_ebv_{hp}"]
                b.ev[f"{tag}_rc_{hp}_1_1"] = b.ev[f"{tag}_ebv_{hp}"]
                continue
            prev_grp = f"{tag}_pvev_{hp-1}_1_1"
            for h2 in range(2):
                h = hp * 2 + h2
                for qg in range(2):
                    pvoff = 1024 * qg
                    for kci in range(33):
                      if kci < 32:
                        kc = kci
                        sb_ = (kc % 2) * 1024
                        vb = kc % 4

                        @block.sync
                        def _(s, hp=hp, h=h, h2=h2, qg=qg, kc=kc, vb=vb):
                            b.wait(s, b.ev.get(f"{tag}_vtfree_{vb}"))
                            if kc == 0 and h2 == 0 and qg == 0 and hp == 0:
                                b.wait(s, f"{kvtag}_agv", f"{tag}_vtones")
                            i = s.dma_start(out=vt[vb][:, 0:64],
                                            in_=b.agv_out.ap()[kc * 128:(kc + 1) * 128, h * 64:(h + 1) * 64])
                            b.mark(f"{tag}_vl_{hp}_{h2}_{qg}_{kc}", i, b.d_vt[vb], 16)

                        @block.tensor
                        def _(t, hp=hp, j=j, h2=h2, qg=qg, kc=kc, sb_=sb_, prev_grp=prev_grp):
                            b.wait(t, b.ev.get(f"{tag}_ptbfree_{kc % 2}"))
                            if kc == 0:
                                b.wait(t, prev_grp)
                                if h2 == 0 and qg == 0:
                                    b.wait(t, f"{tag}_kt_{hp}", f"{tag}_ebv_{hp}")
                            for n in range(2):
                                i = t.matmul(b.pA[:, sb_ + n * 512: sb_ + (n + 1) * 512],
                                             kt[j][h2 * 64:(h2 + 1) * 64, kc * 128:(kc + 1) * 128],
                                             qp[j][h2 * 64:(h2 + 1) * 64, qg * 1024 + n * 512: qg * 1024 + (n + 1) * 512],
                                             start=True, stop=True)
                            b.mark(f"{tag}_s_{hp}_{h2}_{qg}_{kc}", i, b.s_pe)

                        @block.scalar
                        def _(s, hp=hp, h=h, h2=h2, qg=qg, kc=kc, sb_=sb_):
                            b.wait(s, f"{tag}_s_{hp}_{h2}_{qg}_{kc}", f"{tag}_pv_{hp}_{h2}_{qg}_{kc-2}")
                            i = s.activation(b.ptb[kc % 2][:], b.pA[:, sb_:sb_ + 1024], AF.Exp,
                                             scale=0.125, bias=b.ebias[:, h:h + 1])
                            b.mark(f"{tag}_exp_{hp}_{h2}_{qg}_{kc}", i, b.s_ac)

                      if kci >= 1:
                        kc = kci - 1
                        vb = kc % 4
                        if True and stage == "sc":
                            b.ev[f"{tag}_pv_{hp}_{h2}_{qg}_{kc}"] = b.ev[f"{tag}_exp_{hp}_{h2}_{qg}_{kc}"]
                            if kc == 31:
                                b.ev[f"{tag}_pvev_{hp}_{h2}_{qg}"] = b.ev[f"{tag}_exp_{hp}_{h2}_{qg}_31"]
                                prev_grp = f"{tag}_pvev_{hp}_{h2}_{qg}"
                                if h2 == 1 and qg == 1:
                                    b.ev[f"{tag}_sc_done_{hp}"] = b.ev[f"{tag}_exp_{hp}_1_1_31"]
                                    b.ev[f"{tag}_pj_{hp}"] = b.ev[f"{tag}_exp_{hp}_1_1_31"]
                                    b.ev[f"{tag}_pvc_{hp}_1_1"] = b.ev[f"{tag}_exp_{hp}_1_1_31"]
                                    b.ev[f"{tag}_rc_{hp}_1_1"] = b.ev[f"{tag}_exp_{hp}_1_1_31"]
                            continue
                        @block.tensor
                        def _(t, hp=hp, h2=h2, qg=qg, kc=kc, vb=vb, pvoff=pvoff):
                            assert f"{tag}_exp_{hp}_{h2}_{qg}_{kc}" in b.ev
                            b.wait(t, f"{tag}_exp_{hp}_{h2}_{qg}_{kc}", f"{tag}_vl_{hp}_{h2}_{qg}_{kc}")
                            for n in range(2):
                                i = t.matmul(b.pB[0:65, pvoff + n * 512: pvoff + (n + 1) * 512],
                                             vt[vb][:, :],
                                             b.ptb[kc % 2][:, n * 512:(n + 1) * 512],
                                             start=(kc == 0), stop=(kc == 31))
                            b.mark(f"{tag}_pv_{hp}_{h2}_{qg}_{kc}", i, b.s_pe)
                            b.ev[f"{tag}_vtfree_{vb}"] = (b.s_pe, b.s_pe.n)
                            b.ev[f"{tag}_ptbfree_{kc % 2}"] = (b.s_pe, b.s_pe.n)

                    @block.vector
                    def _(v, hp=hp, h2=h2, qg=qg, pvoff=pvoff):
                        b.wait(v, f"{tag}_pv_{hp}_{h2}_{qg}_31")
                        i = v.reciprocal(b.rows[64:65, 0:1024], b.pB[64:65, pvoff:pvoff + 1024])
                        b.mark(f"{tag}_rc_{hp}_{h2}_{qg}", i, b.s_ve)
                        if stage == "pv":
                            b.ev[f"{tag}_pvev_{hp}_{h2}_{qg}"] = b.ev[f"{tag}_rc_{hp}_{h2}_{qg}"]

                    if stage == "pv":
                        prev_grp = f"{tag}_pvev_{hp}_{h2}_{qg}"
                        if h2 == 1 and qg == 1:
                            b.ev[f"{tag}_sc_done_{hp}"] = b.ev[f"{tag}_pvev_{hp}_1_1"]
                            b.ev[f"{tag}_pj_{hp}"] = b.ev[f"{tag}_pvev_{hp}_1_1"]
                            b.ev[f"{tag}_pvc_{hp}_1_1"] = b.ev[f"{tag}_pvev_{hp}_1_1"]
                        continue
                    @block.tensor
                    def _(t, hp=hp, h2=h2, qg=qg):
                        b.wait(t, f"{tag}_rc_{hp}_{h2}_{qg}")
                        for n in range(2):
                            i = t.matmul(b.pA[:, n * 512:(n + 1) * 512], b.ones65[64:65, :],
                                         b.rows[64:65, n * 512:(n + 1) * 512], start=True, stop=True)
                        b.mark(f"{tag}_rcb_{hp}_{h2}_{qg}", i, b.s_pe)

                    @block.scalar
                    def _(s, hp=hp, h2=h2, qg=qg, pvoff=pvoff):
                        b.wait(s, f"{tag}_pv_{hp}_{h2}_{qg}_31")
                        i = s.activation(b.scr[0][0:64, qg * 1024:(qg + 1) * 1024],
                                         b.pB[0:64, pvoff:pvoff + 1024], AF.Copy)
                        b.mark(f"{tag}_pvc_{hp}_{h2}_{qg}", i, b.s_ac)

                    @block.vector
                    def _(v, hp=hp, j=j, h2=h2, qg=qg):
                        b.wait(v, f"{tag}_rcb_{hp}_{h2}_{qg}", f"{tag}_pvc_{hp}_{h2}_{qg}")
                        i = v.tensor_tensor(ap_[j][h2 * 64:(h2 + 1) * 64, qg * 1024:(qg + 1) * 1024],
                                            b.scr[0][0:64, qg * 1024:(qg + 1) * 1024],
                                            b.pA[0:64, 0:1024], ALU.mult)
                        b.mark(f"{tag}_pvev_{hp}_{h2}_{qg}", i, b.s_ve)
                    prev_grp = f"{tag}_pvev_{hp}_{h2}_{qg}"

            b.ev[f"{tag}_sc_done_{hp}"] = b.ev[f"{tag}_pvev_{hp}_1_1"]

            if stage == "ev":
                b.ev[f"{tag}_pj_{hp}"] = b.ev[f"{tag}_pvev_{hp}_1_1"]
            if stage in ("sc", "pv", "ev"):
                continue
            for m in range(C):
                for n in range(4):
                    @block.tensor
                    def _(t, hp=hp, j=j, m=m, n=n):
                        if m == 0 and n == 0:
                            b.wait(t, f"{tag}_pvev_{hp}_1_1", f"{tag}_pw_{hp}")
                        b.wait(t, b.ev.get(f"{tag}_pjfree_{n % 2}"))
                        i = t.matmul(b.pA[:, 1024 + (n % 2) * 512: 1024 + (n % 2 + 1) * 512],
                                     b.wsm[j][:, m * 128:(m + 1) * 128],
                                     ap_[j][:, n * 512:(n + 1) * 512], start=True, stop=True)
                        b.mark(f"{tag}_pjmm_{hp}_{m}_{n}", i, b.s_pe)

                    @block.vector
                    def _(v, hp=hp, m=m, n=n):
                        b.wait(v, f"{tag}_pjmm_{hp}_{m}_{n}")
                        i = v.scalar_tensor_tensor(
                            b.stream2[m][:, n * 512:(n + 1) * 512],
                            b.pA[:, 1024 + (n % 2) * 512: 1024 + (n % 2 + 1) * 512], 1.0,
                            b.stream2[m][:, n * 512:(n + 1) * 512], ALU.mult, ALU.add)
                        b.mark(f"{tag}_pjv_{hp}_{m}_{n}", i, b.s_ve)
                        b.ev[f"{tag}_pjfree_{n % 2}"] = (b.s_ve, b.s_ve.n)
                b.ev[f"{tag}_pjv_{hp}_{m}"] = b.ev[f"{tag}_pjv_{hp}_{m}_3"]
            b.ev[f"{tag}_pj_{hp}"] = b.ev[f"{tag}_pjv_{hp}_{C-1}"]

        b.ev[f"{tag}_done"] = b.ev[f"{tag}_pj_{pairs-1}"]
        b.ev["psfree"] = b.ev[f"{tag}_pj_{pairs-1}"]



    def p_gate(self, block, tag, deps):
        b = self
        for tt in range(NT):
            sl = (tt % 2) * 512

            @block.tensor
            def _(t, tt=tt, sl=sl):
                b.wait(t, f"{tag}_lgv_{tt-2}")
                if tt == 0:
                    b.wait(t, *deps, "psfree", "consts_g")
                for kc in range(C):
                    t.matmul(b.pA[:, sl:sl + 8],
                             b.mm_in[kc][:, tt * 128:(tt + 1) * 128],
                             b.gwb[:, kc * 8:(kc + 1) * 8],
                             start=(kc == 0), stop=False)
                i = t.matmul(b.pA[:, sl:sl + 8], b.onesbf[:], b.gbb[:],
                             start=False, stop=True)
                b.mark(f"{tag}_lg_{tt}", i, b.s_pe)

            @block.vector
            def _(v, tt=tt, sl=sl):
                b.wait(v, f"{tag}_lg_{tt}")
                i = v.tensor_copy(b.scr[0][:, tt * 8:(tt + 1) * 8], b.pA[:, sl:sl + 8])
                b.mark(f"{tag}_lgv_{tt}", i, b.s_ve)
                b.wait(v, f"{tag}_lgv_{tt}")
                i = v.reduce_max(b.scr[1][:, tt:tt + 1], b.scr[0][:, tt * 8:(tt + 1) * 8], axis=AX.X)
                b.mark(f"{tag}_lgm_{tt}", i, b.s_ve)
                b.wait(v, f"{tag}_lgm_{tt}")
                i = v.tensor_scalar(b.scr[1][:, 256 + tt * 8:256 + (tt + 1) * 8],
                                    b.scr[0][:, tt * 8:(tt + 1) * 8],
                                    b.scr[1][:, tt:tt + 1], None, ALU.is_equal)
                b.mark(f"{tag}_msk_{tt}", i, b.s_ve)

            @block.tensor
            def _(t, tt=tt, sl=sl):
                b.wait(t, f"{tag}_msk_{tt}", f"{tag}_mfm_{tt-2}")
                i = t.transpose(b.pB[0:8, sl:sl + 128],
                                b.scr[1][:, 256 + tt * 8:256 + (tt + 1) * 8], b.identt[:])
                b.mark(f"{tag}_tr_{tt}", i, b.s_pe)

            @block.vector
            def _(v, tt=tt, sl=sl):
                b.wait(v, f"{tag}_tr_{tt}")
                i = v.tensor_copy(b.maskfm[0:8, tt * 128:(tt + 1) * 128], b.pB[0:8, sl:sl + 128])
                b.mark(f"{tag}_mfm_{tt}", i, b.s_ve)

        b.ev[f"{tag}_done"] = b.ev[f"{tag}_mfm_{NT-1}"]
        b.ev["psfree"] = b.ev[f"{tag}_mfm_{NT-1}"]

    def p_moe(self, block, tag, st, deps):
        b = self
        hq = [st.enter_context(self.nc.sbuf_tensor(f"{tag}_hq_{i}", [128, T], BF16)) for i in range(8)]
        w2q = st.enter_context(self.nc.sbuf_tensor(f"{tag}_w2q", [128, 8 * 1024], BF16))

        for e in range(E):
            @block.tensor
            def _(t, e=e):
                b.wait(t, f"{tag}_ye_{e-1}_3")
                if e == 0:
                    b.wait(t, *deps, "psfree")
                for n in range(4):
                    i = t.matmul(b.pA[:, n * 512:(n + 1) * 512],
                                 b.ind8t[0:8, e * 128:(e + 1) * 128],
                                 b.maskfm[0:8, n * 512:(n + 1) * 512], start=True, stop=True)
                b.mark(f"{tag}_mb_{e}", i, b.s_pe)

            @block.vector
            def _(v, e=e):
                b.wait(v, f"{tag}_mb_{e}")
                i = v.tensor_copy(b.maske[:], b.pA[:, :])
                b.mark(f"{tag}_me_{e}", i, b.s_ve)

            for q in range(4):
                @block.gpsimd
                def _(g, e=e, q=q):
                    if e == 0 and q == 0:
                        b.wait(g, *deps)
                    b.wait(g, f"{tag}_ye_{e}_{q-1}" if q > 0 else (f"{tag}_ye_{e-1}_3" if e > 0 else None))
                    i = g.dma_start(
                        out=w2q[:].rearrange("p (kc m) -> p kc m", kc=8),
                        in_=b.w2.ap()[e, q * 1024:(q + 1) * 1024, :].rearrange("(kc p) m -> p kc m", p=128),
                    )
                    b.mark(f"{tag}_w2_{e}_{q}", i, b.d_w2, 16)

                for hs in range(8):
                    j = hs % 2
                    pp = [b.pA, b.pB][hs % 2]

                    @block.gpsimd
                    def _(g, e=e, q=q, hs=hs, j=j):
                        b.wait(g, f"{tag}_h_{e}_{q}_{hs-2}")
                        if hs < 2:
                            pe_, pq_ = (e, q - 1) if q > 0 else (e - 1, 3)
                            b.wait(g, f"{tag}_hmm_{pe_}_{pq_}_{hs + 6}")
                            if q == 0 and e == 0:
                                b.wait(g, *deps)
                        i = g.dma_start(
                            out=b.w1b[j][:].rearrange("p (kc m) -> p kc m", kc=C),
                            in_=b.w1.ap()[e, :, q * 1024 + hs * 128: q * 1024 + (hs + 1) * 128]
                                .rearrange("(kc p) m -> p kc m", p=128),
                        )
                        b.mark(f"{tag}_w1_{e}_{q}_{hs}", i, b.d_w1b[j], 16)

                    @block.tensor
                    def _(t, e=e, q=q, hs=hs, j=j, pp=pp):
                        b.wait(t, f"{tag}_w1_{e}_{q}_{hs}", f"{tag}_h_{e}_{q}_{hs-2}")
                        if hs < 2:
                            pe_, pq_ = (e, q - 1) if q > 0 else (e - 1, 3)
                            b.wait(t, f"{tag}_h_{pe_}_{pq_}_{hs + 6}")
                            b.wait(t, f"{tag}_me_{e}" if hs == 0 else None,
                                   f"{tag}_yv_{e}_{q-1}" if q > 0 else (f"{tag}_yv_{e-1}_3" if e > 0 else None))
                        for kc in range(C):
                            for n in range(4):
                                i = t.matmul(pp[:, n * 512:(n + 1) * 512],
                                             b.w1b[j][:, kc * 128:(kc + 1) * 128],
                                             b.mm_in[kc][:, n * 512:(n + 1) * 512],
                                             start=(kc == 0), stop=(kc == C - 1))
                        b.mark(f"{tag}_hmm_{e}_{q}_{hs}", i, b.s_pe)

                    @block.scalar
                    def _(s, e=e, q=q, hs=hs, pp=pp):
                        b.wait(s, f"{tag}_hmm_{e}_{q}_{hs}",
                               f"{tag}_ymm_{e}_{q-1}" if q > 0 else (f"{tag}_ymm_{e-1}_3" if e > 0 else None))
                        i = s.activation(hq[hs][:], pp[:, :], AF.Gelu_apprx_tanh)
                        b.mark(f"{tag}_h_{e}_{q}_{hs}", i, b.s_ac)

                for m in range(C):
                    for tg in range(4):
                        slot = ((m * 4 + tg) % 2) * 512

                        @block.tensor
                        def _(t, e=e, q=q, m=m, tg=tg, slot=slot):
                            if m == 0 and tg == 0:
                                b.wait(t, f"{tag}_h_{e}_{q}_6", f"{tag}_h_{e}_{q}_7", f"{tag}_w2_{e}_{q}")
                            b.wait(t, b.ev.get(f"{tag}_yfree_{slot}"))
                            for hs in range(8):
                                i = t.matmul(b.pA[:, slot:slot + 512],
                                             w2q[:, hs * 1024 + m * 128: hs * 1024 + (m + 1) * 128],
                                             hq[hs][:, tg * 512:(tg + 1) * 512],
                                             start=(hs == 0), stop=(hs == 7))
                            b.mark(f"{tag}_ymm_{e}_{q}_{m}_{tg}", i, b.s_pe)
                            if m == C - 1 and tg == 3:
                                b.ev[f"{tag}_ymm_{e}_{q}"] = (b.s_pe, b.s_pe.n)

                        @block.vector
                        def _(v, e=e, q=q, m=m, tg=tg, slot=slot):
                            b.wait(v, f"{tag}_ymm_{e}_{q}_{m}_{tg}")
                            v.tensor_tensor(b.scr[0][:, slot:slot + 512], b.pA[:, slot:slot + 512],
                                            b.maske[:, tg * 512:(tg + 1) * 512], ALU.mult)
                            i = v.tensor_tensor(b.stream2[m][:, tg * 512:(tg + 1) * 512],
                                                b.scr[0][:, slot:slot + 512],
                                                b.stream2[m][:, tg * 512:(tg + 1) * 512], ALU.add)
                            b.mark(f"{tag}_yv_{e}_{q}_{m}_{tg}", i, b.s_ve)
                            b.ev[f"{tag}_yfree_{slot}"] = (b.s_ve, b.s_ve.n)
                            if m == C - 1 and tg == 3:
                                b.ev[f"{tag}_yv_{e}_{q}"] = (b.s_ve, b.s_ve.n)
                                b.ev[f"{tag}_ye_{e}_{q}"] = (b.s_ve, b.s_ve.n)

            b.ev[f"{tag}_ye_{e}_3"] = b.ev[f"{tag}_ye_{e}_3"]
        b.ev[f"{tag}_done"] = b.ev[f"{tag}_ye_{E-1}_3"]
        b.ev["psfree"] = b.ev[f"{tag}_done"]

    def p_lin(self, block, tag, deps, xdeps=lambda m: ()):
        b = self
        groups = [[0, 1], [2, 3], [4, 5], [6, 7]]

        for ck in range(C):
            j = ck % 2

            @block.gpsimd
            def _(g, ck=ck, j=j):
                b.wait(g, f"{tag}_qmm_{ck-2}")
                if ck == 0:
                    b.wait(g, *deps)
                i = g.dma_start(
                    out=b.w1b[j][:].rearrange("p (kc m) -> p kc m", kc=C),
                    in_=b.awk.ap()[:, ck * 128:(ck + 1) * 128].rearrange("(kc p) m -> p kc m", p=128))
                b.mark(f"{tag}_wk_{ck}", i, b.d_w1b[j], 16)
                i = g.dma_start(
                    out=b.wsm[j][:].rearrange("p (kc m) -> p kc m", kc=C),
                    in_=b.awv.ap()[:, ck * 128:(ck + 1) * 128].rearrange("(kc p) m -> p kc m", p=128))
                b.mark(f"{tag}_wv_{ck}", i, b.d_wsm[j], 16)
                b.wait(g, f"{tag}_qs_{ck-2}")
                i = g.dma_start(
                    out=b.xb[j][:, 0:1024].rearrange("p (kc m) -> p kc m", kc=C),
                    in_=b.awq.ap()[:, ck * 128:(ck + 1) * 128].rearrange("(kc p) m -> p kc m", p=128))
                b.mark(f"{tag}_wq_{ck}", i, b.d_xb[j], 16)

            @block.tensor
            def _(t, ck=ck, j=j):
                b.wait(t, f"{tag}_wk_{ck}", f"{tag}_qS_{ck-1}")
                if ck == 0:
                    b.wait(t, *deps, "psfree")
                for kc in range(C):
                    for n in range(4):
                        i = t.matmul(b.pA[:, n * 512:(n + 1) * 512],
                                     b.w1b[j][:, kc * 128:(kc + 1) * 128],
                                     b.mm_in[kc][:, n * 512:(n + 1) * 512],
                                     start=(kc == 0), stop=(kc == C - 1))
                b.mark(f"{tag}_kmm_{ck}", i, b.s_pe)
                b.wait(t, f"{tag}_wv_{ck}", f"{tag}_vS_{ck-1}")
                for kc in range(C):
                    for n in range(4):
                        i = t.matmul(b.pB[:, n * 512:(n + 1) * 512],
                                     b.wsm[j][:, kc * 128:(kc + 1) * 128],
                                     b.mm_in[kc][:, n * 512:(n + 1) * 512],
                                     start=(kc == 0), stop=(kc == C - 1))
                b.mark(f"{tag}_vmm_{ck}", i, b.s_pe)

            @block.scalar
            def _(s, ck=ck, j=j):
                b.wait(s, f"{tag}_kmm_{ck}", f"{tag}_kv_{ck-1}")
                s.activation(b.scr[0][:], b.pA[:, :], AF.Relu, scale=-1.0, bias=b.ncol("abk", ck))
                s.activation(b.scr[1][:], b.scr[0][:], AF.Exp, scale=-1.0)
                i = s.activation(b.scr[0][:], b.pA[:, :], AF.Relu, bias=self.col("abk", ck))
                b.mark(f"{tag}_kS_{ck}", i, b.s_ac)

            @block.vector
            def _(v, ck=ck):
                b.wait(v, f"{tag}_vmm_{ck}", f"{tag}_kv_{ck-1}")
                i = v.tensor_scalar(b.sqb[1][:], b.pB[:, :], self.col("abv", ck), None, ALU.add)
                b.mark(f"{tag}_vS_{ck}", i, b.s_ve)
                b.wait(v, f"{tag}_kS_{ck}")
                i = v.scalar_tensor_tensor(b.sqb[0][:], b.scr[1][:], 1.0, b.scr[0][:],
                                           ALU.mult, ALU.add,
                                           accum_out=b.lincols[:, 8 + ck:9 + ck])
                b.mark(f"{tag}_kA_{ck}", i, b.s_ve)
                b.wait(v, f"{tag}_kA_{ck}")
                i = v.scalar_tensor_tensor(b.scr[0][:], b.sqb[0][:], 1.0,
                                           b.sqb[1][:], ALU.mult, ALU.mult,
                                           accum_out=b.lincols[:, ck:ck + 1])
                b.mark(f"{tag}_kv0_{ck}", i, b.s_ve)

            @block.tensor
            def _(t, ck=ck, j=j):
                b.wait(t, f"{tag}_wq_{ck}", f"{tag}_kS_{ck}")
                for kc in range(C):
                    for n in range(4):
                        i = t.matmul(b.pA[:, n * 512:(n + 1) * 512],
                                     b.xb[j][:, kc * 128:(kc + 1) * 128],
                                     b.mm_in[kc][:, n * 512:(n + 1) * 512],
                                     start=(kc == 0), stop=(kc == C - 1))
                b.mark(f"{tag}_qmm_{ck}", i, b.s_pe)

            @block.scalar
            def _(s, ck=ck):
                b.wait(s, f"{tag}_qmm_{ck}", f"{tag}_qA_{ck-1}", f"{tag}_kv0_{ck}")
                s.activation(b.scr[0][:], b.pA[:, :], AF.Relu, scale=-1.0, bias=b.ncol("abq", ck))
                s.activation(b.scr[1][:], b.scr[0][:], AF.Exp, scale=-1.0)
                i = s.activation(b.scr[0][:], b.pA[:, :], AF.Relu, bias=self.col("abq", ck))
                b.mark(f"{tag}_qS_{ck}", i, b.s_ac)

            @block.vector
            def _(v, ck=ck):
                b.wait(v, f"{tag}_qS_{ck}")
                i = v.scalar_tensor_tensor(b.stream2[ck][:], b.scr[1][:], 1.0, b.scr[0][:],
                                           ALU.mult, ALU.add)
                b.mark(f"{tag}_qA_{ck}", i, b.s_ve)
                b.ev[f"{tag}_qs_{ck}"] = b.ev[f"{tag}_qA_{ck}"]
                b.ev[f"{tag}_kv_{ck}"] = b.ev[f"{tag}_qA_{ck}"]

        @block.sync
        def _(s):
            b.wait(s, f"{tag}_kv0_{C-1}")
            i = s.dma_start(out=b.arl_in.ap(), in_=b.lincols[:, 0:16])
            b.mark(f"{tag}_ard", i, b.d_msc[0], 16)

        @block.gpsimd
        def _(g):
            b.wait(g, f"{tag}_ard")
            i = g.collective_compute("AllReduce", ALU.add, replica_groups=groups,
                                     ins=[b.arl_in.ap().opt()], outs=[b.arl_out.ap().opt()])
            b.mark(f"{tag}_ar", i, b.c_l)

        @block.sync
        def _(s):
            b.wait(s, f"{tag}_ar")
            i = s.dma_start(out=b.lincols[:, 16:32], in_=b.arl_out.ap())
            b.mark(f"{tag}_art", i, b.d_msc[1], 16)

        for ck in range(C):
            @block.vector
            def _(v, ck=ck):
                b.wait(v, f"{tag}_art" if ck == 0 else None)
                v.tensor_scalar(b.scr[0][:], b.stream2[ck][:],
                                b.lincols[:, 24 + ck:25 + ck], EPS_LA, ALU.mult, ALU.add)
                v.reciprocal(b.scr[1][:], b.scr[0][:])
                v.tensor_tensor(b.scr[0][:], b.stream2[ck][:], b.scr[1][:], ALU.mult)
                i = v.tensor_scalar(b.mm_in[ck][:], b.scr[0][:],
                                    b.lincols[:, 16 + ck:17 + ck], None, ALU.mult)
                b.mark(f"{tag}_cb_{ck}", i, b.s_ve)

        for m in range(C):
            j = m % 2

            @block.gpsimd
            def _(g, m=m, j=j):
                b.wait(g, f"{tag}_pmm_{m-2}")
                i = g.dma_start(
                    out=b.w1b[j][:].rearrange("p (kc m) -> p kc m", kc=C),
                    in_=b.awo.ap()[:, m * 128:(m + 1) * 128].rearrange("(kc p) m -> p kc m", p=128))
                b.mark(f"{tag}_wo_{m}", i, b.d_w1b[j], 16)

            @block.sync
            def _(s, m=m):
                b.wait(s, f"{tag}_xw_{m-1}", f"{tag}_xa_{m-1}", *xdeps(m))
                i = s.dma_start(out=b.maske[:], in_=b.xdram.ap()[m * 128:(m + 1) * 128, :])
                b.mark(f"{tag}_xr_{m}", i, b.d_rr[m % 2], 16)

            @block.tensor
            def _(t, m=m, j=j):
                b.wait(t, f"{tag}_wo_{m}", f"{tag}_pg_{m-2}")
                if m == 0:
                    b.wait(t, f"{tag}_cb_{C-1}")
                pp = [b.pA, b.pB][m % 2]
                for kc in range(C):
                    for n in range(4):
                        i = t.matmul(pp[:, n * 512:(n + 1) * 512],
                                     b.w1b[j][:, kc * 128:(kc + 1) * 128],
                                     b.mm_in[kc][:, n * 512:(n + 1) * 512],
                                     start=(kc == 0), stop=(kc == C - 1))
                b.mark(f"{tag}_pmm_{m}", i, b.s_pe)

            @block.scalar
            def _(s, m=m):
                b.wait(s, f"{tag}_pmm_{m}", f"{tag}_xa_{m-2}")
                pp = [b.pA, b.pB][m % 2]
                i = s.activation(b.scr[m % 2][:], pp[:, :], AF.Gelu_apprx_tanh,
                                 bias=self.col("abo", m))
                b.mark(f"{tag}_pg_{m}", i, b.s_ac)

            @block.vector
            def _(v, m=m):
                b.wait(v, f"{tag}_pg_{m}", f"{tag}_xr_{m}")
                i = v.tensor_tensor(b.maske[:], b.scr[m % 2][:], b.maske[:], ALU.add)
                b.mark(f"{tag}_xa_{m}", i, b.s_ve)

            @block.sync
            def _(s, m=m):
                b.wait(s, f"{tag}_xa_{m}")
                i = s.dma_start(out=b.xdram.ap()[m * 128:(m + 1) * 128, :], in_=b.maske[:])
                b.mark(f"{tag}_xw_{m}", i, b.d_rw[m % 2], 16)

        b.ev[f"{tag}_done"] = b.ev[f"{tag}_xw_{C-1}"]
        b.ev["psfree"] = b.ev[f"{tag}_pg_{C-1}"]

    def emit(self):
        nc = self.nc
        with contextlib.ExitStack() as st:
            self.alloc(st)
            self.ptb = [st.enter_context(nc.sbuf_tensor(f"ptb_{i}", [128, 1024], BF16)) for i in range(2)]
            with nc.Block() as block:
                self.p_consts(block)
                self.p_ln(block, "ln0a", "x", "s2", deps=("consts_sy",))
                if self.upto == "y1":
                    self.p_dump(block, self.stream2, ("ln0a_done",))
                    return
                self.p_ln(block, "ln1a", "s2", "mm", affine=("ln1w", "ln1b"),
                          deps=("ln0a_done",))
                if self.upto == "ln1":
                    self.p_dump(block, self.mm_in, ("ln1a_done",))
                    return
                with contextlib.ExitStack() as msa_st:
                    if self.upto in ("k1", "cc1"):
                        self.p_kv(block, "kva", msa_st, partial="k" if self.upto == "k1" else "cc1")
                        self.p_dump(block, self.mm_in,
                                    ("kva_kd_6", "kva_kd_7") if self.upto == "k1" else ("kva_agm",))
                        return
                    self.p_kv(block, "kva", msa_st)
                    if self.upto == "k0":
                        self.p_dump(block, self.mm_in, ("kva_kd_7", "kva_md"))
                        return
                    if self.upto == "kv0":
                        self.p_dump(block, self.mm_in, ("kva_vd_15",))
                        return
                    if self.upto == "kv":
                        self.p_dump(block, self.mm_in, ("kva_agv", "kva_maxk2"))
                        return
                    if self.upto == "mh":
                        self.p_msa(block, "msaa", "kva", msa_st, pairs=8, stage="q")
                        b = self

                        @block.sync
                        def _(s):
                            b.wait(s, "msaa_ebv_7")
                            i = s.dma_start(out=b.dbg.ap()[0:1, 0:256], in_=b.hrow[:])
                            b.d_msc[0].inc(i, 16)
                            i = s.dma_start(out=b.dbg.ap()[100:228, 0:16], in_=b.ebias[:])
                            b.mark("mhdump", i, b.d_msc[0], 16)
                            b.wait(s, "mhdump")
                        return
                    if self.upto.startswith("msa_") or self.upto == "msa1":
                        pairs = 1 if self.upto.endswith("1") else 8
                        stage = {"msa_q": "q", "msa_sc": "sc", "msa_pv": "pv",
                                 "msa_q1": "q", "msa_sc1": "sc", "msa_pv1": "pv",
                                 "msa_ev1": "ev", "msa1": "all"}[self.upto]
                        self.p_msa(block, "msaa", "kva", msa_st, pairs=pairs, stage=stage)
                        self.p_dump(block, self.mm_in, ("msaa_pj_%d" % (pairs - 1),))
                        return
                    self.p_msa(block, "msaa", "kva", msa_st)
                if self.upto == "msa":
                    self.p_dump(block, self.stream2, ("msaa_done",))
                    return
                self.p_ln(block, "ln2a", "s2", "mm", affine=("ln2w", "ln2b"),
                          deps=("msaa_done",))
                self.p_gate(block, "gatea", deps=("ln2a_done",))
                if self.upto == "gate":
                    b = self

                    @block.sync
                    def _(s):
                        b.wait(s, "gatea_done")
                        i = s.dma_start(out=b.dbg.ap()[0:8, :], in_=b.maskfm[:])
                        b.d_msc[0].inc(i, 16)
                        i = s.dma_start(out=b.dbg.ap()[16:144, 0:128], in_=b.scr[0][:, 0:128])
                        b.mark("gdump", i, b.d_msc[0], 16)
                        b.wait(s, "gdump")
                    return
                with contextlib.ExitStack() as moe_st:
                    self.p_moe(block, "moea", moe_st, deps=("gatea_done", "ln2a_done"))
                if self.upto == "moe":
                    self.p_dump(block, self.stream2, ("moea_done",))
                    return
                self.p_resid(block, "r1", 0.5, deps=("moea_done",))
                self.p_ln(block, "ln0b", "x", "mm", deps=lambda c: (f"r1_w_{c}",))
                self.p_lin(block, "linb", deps=("ln0b_done",),
                           xdeps=lambda m: (f"r1_w_{m}",))
                if self.upto == "lin":
                    self.p_dump_dram(block, lambda c: (f"linb_xw_{c}",))
                    return
                self.p_ln(block, "ln0c", "x", "s2", deps=lambda c: (f"linb_xw_{c}",))
                self.p_ln(block, "ln1c", "s2", "mm", affine=("ln1w", "ln1b"),
                          deps=("ln0c_done",))
                with contextlib.ExitStack() as msa_st2:
                    self.p_kv(block, "kvb", msa_st2, deps=("linb_done", "moea_done"))
                    self.p_msa(block, "msab", "kvb", msa_st2, deps=("linb_done", "moea_done"))
                self.p_ln(block, "ln2b", "s2", "mm", affine=("ln2w", "ln2b"),
                          deps=("msab_done",))
                self.p_gate(block, "gateb", deps=("ln2b_done",))
                with contextlib.ExitStack() as moe_st2:
                    self.p_moe(block, "moeb", moe_st2, deps=("gateb_done", "ln2b_done"))
                self.p_resid(block, "r2", 0.5, final=True,
                             deps=lambda c: ("moeb_done", f"linb_xw_{c}"))

                @block.sync
                def _(s):
                    self.wait(s, "r2_w_6", "r2_w_7")

    def p_dump_dram(self, block, depf):
        b = self

        @block.sync
        def _(s):
            for c in range(C):
                b.wait(s, *depf(c))
            i = s.dma_start(out=b.dbg.ap(), in_=b.xdram.ap())
            b.mark("dmpd", i, b.d_msc[0], 16)
            b.wait(s, "dmpd")

    def p_dump(self, block, tensors, deps):
        b = self
        for c in range(C):
            i2 = c % 2

            @block.vector
            def _(v, c=c, i2=i2):
                b.wait(v, *deps)
                b.wait(v, f"dmp_w_{c-2}")
                i = v.tensor_copy(b.scr[i2][:], tensors[c][:])
                b.mark(f"dmp_v_{c}", i, b.s_ve)

            @block.sync
            def _(s, c=c, i2=i2):
                b.wait(s, f"dmp_v_{c}")
                i = s.dma_start(out=b.dbg.ap()[c * 128:(c + 1) * 128, :], in_=b.scr[i2][:])
                b.mark(f"dmp_w_{c}", i, b.d_rw[i2], 16)

        @block.sync
        def _(s):
            b.wait(s, f"dmp_w_{C-2}", f"dmp_w_{C-1}")


def build_nc(upto="full"):
    nc = bass.Bass("TRN2", target_bir_lowering=False, debug=False,
                   detect_race_conditions=False)
    bb = B(nc, upto)
    bb.declare()
    bb.emit()
    return nc


def make_core_inputs(inputs, core):
    b_, hf = core // 2, core % 2
    f32 = np.float32
    bfl = ml_dtypes.bfloat16
    x = np.asarray(inputs["L0_attn_x"])
    xs = np.ascontiguousarray(x[b_, hf * T:(hf + 1) * T, :].T).astype(f32)
    biascols = np.concatenate([
        np.asarray(inputs["L0_attn_bq"]).reshape(D, 1),
        np.asarray(inputs["L0_attn_bk"]).reshape(D, 1),
        np.asarray(inputs["L0_attn_bv"]).reshape(D, 1),
        np.asarray(inputs["L0_attn_bo"]).reshape(D, 1),
        np.asarray(inputs["L0_ffn_ln1_w"]).reshape(D, 1),
        np.asarray(inputs["L0_ffn_ln1_b"]).reshape(D, 1),
        np.asarray(inputs["L0_ffn_ln2_w"]).reshape(D, 1),
        np.asarray(inputs["L0_ffn_ln2_b"]).reshape(D, 1),
    ], axis=1).astype(f32)
    return {
        "x_in": xs,
        "awq": np.ascontiguousarray(np.asarray(inputs["L0_attn_wq"]).reshape(D, D)).astype(f32),
        "awk": np.ascontiguousarray(np.asarray(inputs["L0_attn_wk"]).reshape(D, D)).astype(f32),
        "awv": np.ascontiguousarray(np.asarray(inputs["L0_attn_wv"]).reshape(D, D)).astype(f32),
        "awo": np.ascontiguousarray(np.asarray(inputs["L0_attn_wo"]).reshape(D, D)).astype(f32),
        "wqkv": np.ascontiguousarray(np.asarray(inputs["L0_ffn_w_qkv"])).astype(f32),
        "wproj": np.ascontiguousarray(np.asarray(inputs["L0_ffn_w_proj"])).astype(f32),
        "biascols": np.ascontiguousarray(biascols),
        "gate_w": np.ascontiguousarray(np.asarray(inputs["L0_ffn_gate_w"])).astype(f32),
        "gate_b": np.ascontiguousarray(np.asarray(inputs["L0_ffn_gate_b"]).reshape(1, E)).astype(f32),
        "w1": np.ascontiguousarray(np.asarray(inputs["L0_ffn_w1"])).astype(f32),
        "w2": np.ascontiguousarray(np.asarray(inputs["L0_ffn_w2"])).astype(f32),
        "ones_f": np.ones((1, 128), f32),
        "negones_f": np.full((1, 128), -1.0, f32),
        "ones_bf": np.ones((1, 128), bfl),
        "ones65_f": np.concatenate([np.ones((32, 128), f32), -np.ones((32, 128), f32), np.ones((1, 128), f32)], axis=0),
        "onescol_bf": np.ones((128, 1), bfl),
        "onesm_bf": np.ones((128, 128), bfl),
        "ident": np.eye(128, dtype=f32),
        "ind8": np.kron(np.eye(E, dtype=f32), np.ones((1, 128), f32)),
    }


def kernel(**inputs):
    nc = build_nc("full")
    in_maps = [make_core_inputs(inputs, c) for c in range(8)]
    res = run_bass_kernel_spmd(nc, in_maps, core_ids=list(range(8)))
    out = np.zeros((4, SEQ, D), np.float32)
    for c in range(8):
        b_, hf = c // 2, c % 2
        out[b_, hf * T:(hf + 1) * T, :] = res.results[c]["x_out"].T
    return out


# version 03 (final)
# speedup vs baseline: 1.2873x; 1.0237x over previous
import sys
import contextlib
import numpy as np
import ml_dtypes

sys.path.insert(0, "/opt/trn_rl_repo")
import concourse.bass as bass
import concourse.mybir as mybir
from concourse.bass_utils import run_bass_kernel_spmd

F32 = mybir.dt.float32
BF16 = mybir.dt.bfloat16
AF = mybir.ActivationFunctionType
ALU = mybir.AluOpType
AX = mybir.AxisListType

D = 1024
T = 2048
C = 8
NT = 16
H = 16
DK = 64
E = 8
HID = 4096
SEQ = 4096
EPS_LN = 1e-5
EPS_LA = 1e-6
INV_D = 1.0 / D
LN_EIGHTH = float(np.log(0.125))


class Ctr:
    def __init__(self, sem):
        self.sem = sem
        self.n = 0

    def inc(self, instr, k=1):
        instr.then_inc(self.sem, k)
        self.n += k
        return self.n


class B:

    def __init__(self, nc, upto):
        self.nc = nc
        self.upto = upto
        self.ev = {}

    def mark(self, key, instr, ctr, k=1):
        ctr.inc(instr, k)
        self.ev[key] = (ctr, ctr.n)

    def wait(self, eng, *keys):
        for key in keys:
            if key is None:
                continue
            e = self.ev.get(key) if isinstance(key, str) else key
            if e is None:
                continue
            ctr, n = e
            if n > 0:
                eng.wait_ge(ctr.sem, n)

    def declare(self):
        nc = self.nc

        def din(name, shape, dt=F32):
            return nc.dram_tensor(name, shape, dt, kind="ExternalInput")

        self.x_in = din("x_in", [D, T])
        self.awq = din("awq", [D, D]); self.awk = din("awk", [D, D]); self.awv = din("awv", [D, D])
        self.awo = din("awo", [D, D])
        self.wqkv = din("wqkv", [D, 3 * D])
        self.wproj = din("wproj", [D, D])
        self.biascols = din("biascols", [D, 8])
        self.gate_w = din("gate_w", [D, E])
        self.gate_b = din("gate_b", [1, E])
        self.w1 = din("w1", [E, D, HID])
        self.w2 = din("w2", [E, HID, D])
        self.ones_f = din("ones_f", [1, 128])
        self.negones_f = din("negones_f", [1, 128])
        self.ones_bf = din("ones_bf", [1, 128], BF16)
        self.ones65_f = din("ones65_f", [65, 128])
        self.onescol_bf = din("onescol_bf", [128, 1], BF16)
        self.onesm_bf = din("onesm_bf", [128, 128], BF16)
        self.ident = din("ident", [128, 128])
        self.ind8 = din("ind8", [E, E * 128])

        self.x_out = nc.dram_tensor("x_out", [D, T], F32, kind="ExternalOutput")
        if self.upto != "full":
            self.dbg = nc.dram_tensor("dbg", [D, T], F32, kind="ExternalOutput")

        self.xdram = nc.dram_tensor("xdram", [D, T], F32)
        self.agk_in = nc.dram_tensor("agk_in", [D, T], BF16)
        self.agk_out = nc.dram_tensor("agk_out", [2 * D, T], BF16)
        self.agv_in = nc.dram_tensor("agv_in", [T, D], BF16)
        self.agv_out = nc.dram_tensor("agv_out", [2 * T, D], BF16)
        self.agm_in = nc.dram_tensor("agm_in", [1, H], F32)
        self.agm_out = nc.dram_tensor("agm_out", [2, H], F32)
        self.arl_in = nc.dram_tensor("arl_in", [128, 16], F32)
        self.arl_out = nc.dram_tensor("arl_out", [128, 16], F32)

    def alloc(self, st):
        nc = self.nc

        def sb(name, shape, dt):
            return st.enter_context(nc.sbuf_tensor(name, shape, dt))

        def psum(name, shape, dt=F32):
            return st.enter_context(nc.psum_tensor(name, shape, dt))

        def sem(name):
            return Ctr(st.enter_context(nc.semaphore(name)))

        self.sb_ = sb
        self.stream2 = [sb(f"stream2_{c}", [128, T], BF16) for c in range(C)]
        self.mm_in = [sb(f"mm_in_{c}", [128, T], BF16) for c in range(C)]
        self.scr = [sb(f"scr_{i}", [128, T], F32) for i in range(2)]
        self.xb = [sb(f"xb_{i}", [128, T], BF16) for i in range(2)]
        self.sqb = [sb(f"sqb_{i}", [128, T], BF16) for i in range(2)]
        self.rows = sb("rows", [65, T], F32)
        self.maskfm = sb("maskfm", [E, T], F32)
        self.maske = sb("maske", [128, T], F32)
        self.w1b = [sb(f"w1b_{i}", [128, 8 * 128], BF16) for i in range(2)]
        self.wsm = [sb(f"wsm_{i}", [128, D], BF16) for i in range(2)]
        self.cols8 = sb("cols8", [128, 64], F32)
        self.cols8n = sb("cols8n", [128, 24], F32)
        self.lincols = sb("lincols", [128, 32], F32)
        self.onesc = sb("onesc", [128, 1], BF16)
        self.onesf = sb("onesf", [1, 128], F32)
        self.negonesf = sb("negonesf", [1, 128], F32)
        self.onesbf = sb("onesbf", [1, 128], BF16)
        self.ones65 = sb("ones65", [65, 128], F32)
        self.onesm = sb("onesm", [128, 128], BF16)
        self.identt = sb("identt", [128, 128], F32)
        self.ind8t = sb("ind8t", [E, E * 128], F32)
        self.gwb = sb("gwb", [128, C * E], BF16)
        self.gbb = sb("gbb", [1, E], BF16)
        self.hrow = sb("hrow", [1, 256], F32)
        self.ebias = sb("ebias", [128, H], F32)

        self.pA = psum("pA", [128, T])
        self.pB = psum("pB", [128, T])

        self.s_sy = sem("s_sy")
        self.s_g = sem("s_g")
        self.s_pe = sem("s_pe")
        self.s_ac = sem("s_ac")
        self.s_ve = sem("s_ve")
        self.d_cs = sem("d_cs")
        self.d_cg = sem("d_cg")
        self.d_xb = [sem(f"d_xb{i}") for i in range(2)]
        self.d_w1b = [sem(f"d_w1b{i}") for i in range(2)]
        self.d_wsm = [sem(f"d_wsm{i}") for i in range(2)]
        self.d_kt = [sem(f"d_kt{i}") for i in range(2)]
        self.d_vt = [sem(f"d_vt{i}") for i in range(4)]
        self.d_kd = [sem(f"d_kd{i}") for i in range(2)]
        self.d_vd = [sem(f"d_vd{i}") for i in range(2)]
        self.d_msc = [sem(f"d_msc{i}") for i in range(2)]
        self.d_rr = [sem(f"d_rr{i}") for i in range(2)]
        self.d_rw = [sem(f"d_rw{i}") for i in range(2)]
        self.d_vsl = sem("d_vsl")
        self.d_w2 = sem("d_w2")
        self.c_k = sem("c_k")
        self.c_m = sem("c_m")
        self.c_v = sem("c_v")
        self.c_l = sem("c_l")

        self.colmap = dict(abq=0, abk=1, abv=2, abo=3, ln1w=4, ln1b=5, ln2w=6, ln2b=7)

    def col(self, name, c):
        o = c * 8 + self.colmap[name]
        return self.cols8[:, o:o + 1]

    def ncol(self, name, c):
        o = c * 3 + dict(abq=0, abk=1, abv=2)[name]
        return self.cols8n[:, o:o + 1]

    def p_consts(self, block):
        b = self

        @block.sync
        def _(s):
            for out, in_ in [
                (b.xdram.ap(), b.x_in.ap()),
                (b.onesf[:], b.ones_f.ap()),
                (b.negonesf[:], b.negones_f.ap()),
                (b.onesbf[:], b.ones_bf.ap()),
                (b.ones65[:], b.ones65_f.ap()),
                (b.onesc[:], b.onescol_bf.ap()),
                (b.onesm[:], b.onesm_bf.ap()),
                (b.identt[:], b.ident.ap()),
                (b.ind8t[:], b.ind8.ap()),
            ]:
                i = s.dma_start(out=out, in_=in_)
                b.d_cs.inc(i, 16)
            i = s.dma_start(
                out=b.cols8[:].rearrange("p (c x) -> p c x", c=C),
                in_=b.biascols.ap().rearrange("(c p) x -> p c x", p=128),
            )
            b.mark("consts_sy", i, b.d_cs, 16)

        @block.gpsimd
        def _(g):
            i = g.dma_start(
                out=b.gwb[:].rearrange("p (c e) -> p c e", c=C),
                in_=b.gate_w.ap().rearrange("(c p) e -> p c e", p=128),
            )
            b.d_cg.inc(i, 16)
            i = g.dma_start(out=b.gbb[:], in_=b.gate_b.ap())
            b.mark("consts_g", i, b.d_cg, 16)

        @block.vector
        def _(v):
            b.wait(v, "consts_sy")
            for c in range(C):
                for name in ("abq", "abk", "abv"):
                    i = v.tensor_scalar(b.ncol(name, c), b.col(name, c), -1.0, None, ALU.mult)
            b.mark("consts_v", i, b.s_ve)

    def p_ln(self, block, tag, src, dst, affine=None, deps=()):
        b = self
        psfree = b.ev.get("psfree")
        depf = deps if callable(deps) else (lambda c: deps)

        for c in range(C):
            i2 = c % 2
            if src == "x":
                @block.gpsimd
                def _(g, c=c, i2=i2):
                    b.wait(g, f"{tag}_st_{c-2}", f"{tag}_sq_{c-2}", *depf(c))
                    i = g.dma_start(out=b.xb[i2][:], in_=b.xdram.ap()[c * 128:(c + 1) * 128, :])
                    b.mark(f"{tag}_xb_{c}", i, b.d_xb[i2], 16)
                srcb = b.xb[i2]
                srcdep = f"{tag}_xb_{c}"
            else:
                srcb = b.stream2[c]
                srcdep = None

            @block.scalar
            def _(s, c=c, i2=i2, srcb=srcb, srcdep=srcdep):
                b.wait(s, srcdep, f"{tag}_st_{c-2}", *(depf(c) if src != "x" else ()))
                i = s.activation(b.sqb[i2][:], srcb[:], AF.Square)
                b.mark(f"{tag}_sq_{c}", i, b.s_ac)

            @block.tensor
            def _(t, c=c, i2=i2, srcb=srcb, srcdep=srcdep):
                b.wait(t, srcdep, f"{tag}_sq_{c}")
                if c == 0:
                    b.wait(t, psfree)
                for n in range(4):
                    t.matmul(b.pA[:, n * 512:(n + 1) * 512], b.onesm[:],
                             srcb[:, n * 512:(n + 1) * 512], start=(c == 0), stop=(c == C - 1))
                for n in range(4):
                    i = t.matmul(b.pB[:, n * 512:(n + 1) * 512], b.onesm[:],
                                 b.sqb[i2][:, n * 512:(n + 1) * 512], start=(c == 0), stop=(c == C - 1))
                b.mark(f"{tag}_st_{c}", i, b.s_pe)

        @block.vector
        def _(v):
            b.wait(v, f"{tag}_st_{C-1}")
            v.tensor_scalar(b.rows[0:1, :], b.pA[0:1, :], INV_D, None, ALU.mult)
            v.tensor_scalar(b.rows[32:33, :], b.pB[0:1, :], INV_D, EPS_LN, ALU.mult, ALU.add)
            v.tensor_tensor(b.pB[0:1, :], b.rows[0:1, :], b.rows[0:1, :], ALU.mult)
            i = v.tensor_tensor(b.rows[32:33, :], b.rows[32:33, :], b.pB[0:1, :], ALU.subtract)
            b.mark(f"{tag}_rows", i, b.s_ve)

        @block.scalar
        def _(s):
            b.wait(s, f"{tag}_rows")
            s.activation(b.rows[64:65, :], b.rows[32:33, :], AF.Ln)
            i = s.activation(b.rows[32:33, :], b.rows[64:65, :], AF.Exp, scale=-0.5)
            b.mark(f"{tag}_rstd", i, b.s_ac)

        @block.tensor
        def _(t):
            b.wait(t, f"{tag}_rstd")
            for n in range(4):
                i = t.matmul(b.pB[:, n * 512:(n + 1) * 512], b.ones65[32:33, :],
                             b.rows[32:33, n * 512:(n + 1) * 512], start=True, stop=True)
            b.mark(f"{tag}_bc", i, b.s_pe)

        for c in range(C):
            i2 = c % 2
            if src == "x":
                @block.gpsimd
                def _(g, c=c, i2=i2):
                    b.wait(g, f"{tag}_st_{C-1}" if c < 2 else f"{tag}_n_{c-2}")
                    i = g.dma_start(out=b.xb[i2][:], in_=b.xdram.ap()[c * 128:(c + 1) * 128, :])
                    b.mark(f"{tag}_xb2_{c}", i, b.d_xb[i2], 16)
                srcb = b.xb[i2]
                srcdep = f"{tag}_xb2_{c}"
            else:
                srcb = b.stream2[c]
                srcdep = None

            @block.vector
            def _(v, c=c, i2=i2, srcb=srcb, srcdep=srcdep):
                b.wait(v, f"{tag}_bc", srcdep)
                v.scalar_tensor_tensor(b.scr[i2][:], b.pA[:, :], INV_D, srcb[:],
                                       ALU.mult, ALU.subtract)
                if dst == "s2" and not affine:
                    i = v.tensor_tensor(b.stream2[c][:], b.scr[i2][:], b.pB[:, :], ALU.mult)
                elif affine:
                    v.tensor_tensor(b.scr[i2][:], b.scr[i2][:], b.pB[:, :], ALU.mult)
                    i = v.tensor_scalar(b.mm_in[c][:], b.scr[i2][:],
                                        self.col(affine[0], c), self.col(affine[1], c),
                                        ALU.mult, ALU.add)
                else:
                    i = v.tensor_tensor(b.mm_in[c][:], b.scr[i2][:], b.pB[:, :], ALU.mult)
                b.mark(f"{tag}_n_{c}", i, b.s_ve)

        b.ev["psfree"] = b.ev[f"{tag}_n_{C-1}"]
        b.ev[f"{tag}_done"] = b.ev[f"{tag}_n_{C-1}"]
        b.ev[f"{tag}_done_prev"] = b.ev[f"{tag}_n_{C-2}"]

    def p_resid(self, block, tag, scale, final=False, deps=()):
        b = self
        depf = deps if callable(deps) else (lambda c: deps)
        for c in range(C):
            i2 = c % 2

            @block.sync
            def _(s, c=c, i2=i2):
                b.wait(s, f"{tag}_w_{c-2}", *depf(c))
                i = s.dma_start(out=b.scr[i2][:], in_=b.xdram.ap()[c * 128:(c + 1) * 128, :])
                b.mark(f"{tag}_r_{c}", i, b.d_rr[i2], 16)

            @block.vector
            def _(v, c=c, i2=i2):
                b.wait(v, f"{tag}_r_{c}")
                i = v.scalar_tensor_tensor(b.scr[i2][:], b.stream2[c][:], scale,
                                           b.scr[i2][:], ALU.mult, ALU.add)
                b.mark(f"{tag}_v_{c}", i, b.s_ve)

            @block.sync
            def _(s, c=c, i2=i2):
                b.wait(s, f"{tag}_v_{c}")
                tgt = b.x_out.ap() if final else b.xdram.ap()
                i = s.dma_start(out=tgt[c * 128:(c + 1) * 128, :], in_=b.scr[i2][:])
                if not final:
                    b.mark(f"{tag}_w_{c}", i, b.d_rw[i2], 16)
                else:
                    b.d_rw[i2].inc(i, 16)
                    i2_ = s.dma_start(out=b.xdram.ap()[c * 128:(c + 1) * 128, :], in_=b.scr[i2][:])
                    b.mark(f"{tag}_w_{c}", i2_, b.d_rw[i2], 16)
        b.ev[f"{tag}_done"] = b.ev[f"{tag}_w_{C-1}"]

    def p_kv(self, block, tag, st, partial=None, deps=()):
        b = self
        wq = self.wqkv.ap()
        groups = [[0, 1], [2, 3], [4, 5], [6, 7]]

        for ck in range(C):
            j = ck % 2
            i2 = ck % 2

            @block.gpsimd
            def _(g, ck=ck, j=j):
                b.wait(g, f"{tag}_kmm_{ck-2}")
                if ck == 0:
                    b.wait(g, "consts_g", *deps)
                i = g.dma_start(
                    out=b.w1b[j][:].rearrange("p (kc m) -> p kc m", kc=C),
                    in_=wq[:, D + ck * 128: D + (ck + 1) * 128].rearrange("(kc p) m -> p kc m", p=128),
                )
                b.mark(f"{tag}_kw_{ck}", i, b.d_w1b[j], 16)

            @block.tensor
            def _(t, ck=ck, j=j):
                b.wait(t, f"{tag}_kw_{ck}", f"{tag}_ks_{ck-1}")
                if ck == 0:
                    b.wait(t, "psfree")
                for kc in range(C):
                    for n in range(4):
                        i = t.matmul(b.pA[:, n * 512:(n + 1) * 512],
                                     b.w1b[j][:, kc * 128:(kc + 1) * 128],
                                     b.mm_in[kc][:, n * 512:(n + 1) * 512],
                                     start=(kc == 0), stop=(kc == C - 1))
                b.mark(f"{tag}_kmm_{ck}", i, b.s_pe)

            @block.scalar
            def _(s, ck=ck, i2=i2):
                b.wait(s, f"{tag}_kmm_{ck}", f"{tag}_kd_{ck-2}", f"{tag}_ks2_{ck-2}")
                i = s.activation(b.xb[i2][:], b.pA[:, :], AF.Copy)
                b.mark(f"{tag}_kc_{ck}", i, b.s_ac)
                i = s.activation(b.sqb[i2][:], b.pA[:, :], AF.Square)
                b.mark(f"{tag}_ks_{ck}", i, b.s_ac)

            for h2 in range(2):
                @block.tensor
                def _(t, ck=ck, i2=i2, h2=h2):
                    b.wait(t, f"{tag}_ks_{ck}" if h2 == 0 else f"{tag}_kmh_{ck}_0",
                           f"{tag}_km_{ck-1}" if h2 == 0 else None)
                    for n in range(4):
                        i = t.matmul(b.pB[:, n * 512:(n + 1) * 512],
                                     b.onesm[h2 * 64:(h2 + 1) * 64, :],
                                     b.sqb[i2][h2 * 64:(h2 + 1) * 64, n * 512:(n + 1) * 512],
                                     start=True, stop=True)
                    b.mark(f"{tag}_ksum_{ck}_{h2}", i, b.s_pe)

                @block.vector
                def _(v, ck=ck, h2=h2):
                    b.wait(v, f"{tag}_ksum_{ck}_{h2}")
                    hh = ck * 2 + h2
                    i = v.reduce_max(b.hrow[0:1, hh:hh + 1], b.pB[0:1, :], axis=AX.X)
                    b.mark(f"{tag}_kmh_{ck}_{h2}", i, b.s_ve)
            b.ev[f"{tag}_ks2_{ck}"] = b.ev[f"{tag}_ksum_{ck}_1"]
            b.ev[f"{tag}_km_{ck}"] = b.ev[f"{tag}_kmh_{ck}_1"]

            @block.sync
            def _(s, ck=ck, i2=i2):
                b.wait(s, f"{tag}_kc_{ck}")
                i = s.dma_start(out=b.agk_in.ap()[ck * 128:(ck + 1) * 128, :], in_=b.xb[i2][:])
                b.mark(f"{tag}_kd_{ck}", i, b.d_kd[i2], 16)

        if partial == "k":
            return
        @block.sync
        def _(s):
            b.wait(s, f"{tag}_km_{C-1}")
            i = s.dma_start(out=b.agm_in.ap(), in_=b.hrow[0:1, 0:16])
            b.mark(f"{tag}_md", i, b.d_msc[0], 16)

        @block.gpsimd
        def _(g):
            b.wait(g, f"{tag}_kd_{C-2}", f"{tag}_kd_{C-1}", f"{tag}_md")
            i = g.collective_compute(
                "AllGather", ALU.bypass, replica_groups=groups,
                ins=[b.agk_in.ap().opt()], outs=[b.agk_out.ap().opt()])
            b.mark(f"{tag}_agk", i, b.c_k)
            i = g.collective_compute(
                "AllGather", ALU.bypass, replica_groups=groups,
                ins=[b.agm_in.ap().opt()], outs=[b.agm_out.ap().opt()])
            b.mark(f"{tag}_agm", i, b.c_m)

        if partial == "cc1":
            return
        vsl = [st.enter_context(self.nc.sbuf_tensor(f"{tag}_vsl_{kc}", [128, D], BF16))
               for kc in range(C)]

        @block.gpsimd
        def _(g):
            b.wait(g, *deps)
            for kc in range(C):
                i = g.dma_start(out=vsl[kc][:], in_=wq[kc * 128:(kc + 1) * 128, 2 * D:3 * D])
                b.d_vsl.inc(i, 16)
            b.ev[f"{tag}_vw"] = (b.d_vsl, b.d_vsl.n)

        for tt in range(NT):
            i2 = tt % 2
            pp = [b.pA, b.pB][tt % 2]

            @block.tensor
            def _(t, tt=tt, pp=pp):
                b.wait(t, f"{tag}_vw" if tt == 0 else None, f"{tag}_vev_{tt-2}")
                if tt == 0:
                    b.wait(t, f"{tag}_ks_{C-1}")
                if tt == 1:
                    b.wait(t, f"{tag}_km_{C-1}")
                for kc in range(C):
                    for n in range(2):
                        i = t.matmul(pp[:, n * 512:(n + 1) * 512],
                                     b.mm_in[kc][:, tt * 128:(tt + 1) * 128],
                                     vsl[kc][:, n * 512:(n + 1) * 512],
                                     start=(kc == 0), stop=(kc == C - 1))
                b.mark(f"{tag}_vmm_{tt}", i, b.s_pe)

            @block.scalar
            def _(s, tt=tt, i2=i2, pp=pp):
                b.wait(s, f"{tag}_vmm_{tt}", f"{tag}_vd_{tt-2}")
                if tt < 2:
                    b.wait(s, f"{tag}_ks2_{C-1}")
                i = s.activation(b.sqb[i2][:, 0:1024], pp[:, 0:1024], AF.Copy)
                b.mark(f"{tag}_vev_{tt}", i, b.s_ac)

            @block.sync
            def _(s, tt=tt, i2=i2):
                b.wait(s, f"{tag}_vev_{tt}")
                i = s.dma_start(out=b.agv_in.ap()[tt * 128:(tt + 1) * 128, :], in_=b.sqb[i2][:, 0:1024])
                b.mark(f"{tag}_vd_{tt}", i, b.d_vd[i2], 16)

        @block.gpsimd
        def _(g):
            b.wait(g, f"{tag}_vd_{NT-2}", f"{tag}_vd_{NT-1}")
            i = g.collective_compute(
                "AllGather", ALU.bypass, replica_groups=groups,
                ins=[b.agv_in.ap().opt()], outs=[b.agv_out.ap().opt()])
            b.mark(f"{tag}_agv", i, b.c_v)

        @block.sync
        def _(s):
            b.wait(s, f"{tag}_agm")
            i = s.dma_start(out=b.hrow[0:1, 16:32], in_=b.agm_out.ap()[0:1, :])
            b.d_msc[1].inc(i, 16)
            i = s.dma_start(out=b.hrow[0:1, 32:48], in_=b.agm_out.ap()[1:2, :])
            b.mark(f"{tag}_mg", i, b.d_msc[1], 16)

        @block.vector
        def _(v):
            b.wait(v, f"{tag}_mg")
            i = v.tensor_tensor(b.hrow[0:1, 16:32], b.hrow[0:1, 16:32], b.hrow[0:1, 32:48], ALU.max)
            b.mark(f"{tag}_maxk2", i, b.s_ve)

        b.ev["psfree"] = b.ev[f"{tag}_vev_{NT-1}"]

    def p_msa(self, block, tag, kvtag, st, pairs=8, stage="all", deps=()):
        b = self
        kt = [st.enter_context(self.nc.sbuf_tensor(f"{tag}_kt_{i}", [128, SEQ], BF16)) for i in range(2)]
        qp = [st.enter_context(self.nc.sbuf_tensor(f"{tag}_qp_{i}", [128, T], BF16)) for i in range(2)]
        ap_ = [st.enter_context(self.nc.sbuf_tensor(f"{tag}_ap_{i}", [128, T], BF16)) for i in range(2)]
        vt = [st.enter_context(self.nc.sbuf_tensor(f"{tag}_vt_{i}", [128, 65], BF16)) for i in range(4)]

        @block.vector
        def _(v):
            b.wait(v, "consts_sy", *deps)
            for i_ in range(4):
                i = v.tensor_copy(vt[i_][:, 64:65], b.onesc[:])
            b.mark(f"{tag}_vtones", i, b.s_ve)

        for hp in range(pairs):
            j = hp % 2

            @block.gpsimd
            def _(g, hp=hp, j=j):
                b.wait(g, f"{tag}_qmm_{hp-2}")
                if hp < 2:
                    b.wait(g, f"{kvtag}_kmm_{C-1}", *deps)
                i = g.dma_start(
                    out=b.w1b[j][:].rearrange("p (kc m) -> p kc m", kc=C),
                    in_=b.wqkv.ap()[:, hp * 128:(hp + 1) * 128].rearrange("(kc p) m -> p kc m", p=128),
                )
                b.mark(f"{tag}_qw_{hp}", i, b.d_w1b[j], 16)
                b.wait(g, f"{tag}_pj_{hp-2}")
                i = g.dma_start(out=b.wsm[j][:], in_=b.wproj.ap()[hp * 128:(hp + 1) * 128, :])
                b.mark(f"{tag}_pw_{hp}", i, b.d_wsm[j], 16)

            @block.sync
            def _(s, hp=hp, j=j):
                b.wait(s, f"{kvtag}_agk", f"{tag}_sc_done_{hp-2}")
                if hp < 2:
                    b.wait(s, *deps)
                i = s.dma_start(out=kt[j][:, 0:T], in_=b.agk_out.ap()[hp * 128:(hp + 1) * 128, :])
                b.d_kt[j].inc(i, 16)
                i = s.dma_start(out=kt[j][:, T:SEQ], in_=b.agk_out.ap()[D + hp * 128:D + (hp + 1) * 128, :])
                b.mark(f"{tag}_kt_{hp}", i, b.d_kt[j], 16)

            @block.tensor
            def _(t, hp=hp, j=j):
                b.wait(t, f"{tag}_qw_{hp}", f"{tag}_pj_{hp-1}")
                if hp == 0:
                    b.wait(t, "psfree")
                for kc in range(C):
                    for n in range(4):
                        i = t.matmul(b.pA[:, n * 512:(n + 1) * 512],
                                     b.w1b[j][:, kc * 128:(kc + 1) * 128],
                                     b.mm_in[kc][:, n * 512:(n + 1) * 512],
                                     start=(kc == 0), stop=(kc == C - 1))
                b.mark(f"{tag}_qmm_{hp}", i, b.s_pe)

            @block.scalar
            def _(s, hp=hp, j=j):
                b.wait(s, f"{tag}_qmm_{hp}")
                s.activation(qp[j][:], b.pA[:, :], AF.Copy)
                i = s.activation(b.sqb[0][:], b.pA[:, :], AF.Square)
                b.mark(f"{tag}_qe_{hp}", i, b.s_ac)

            for h2 in range(2):
                @block.tensor
                def _(t, hp=hp, h2=h2):
                    if h2 == 0:
                        b.wait(t, f"{tag}_qe_{hp}", f"{tag}_pvc_{hp-1}_1_1", f"{tag}_rc_{hp-1}_1_1")
                    else:
                        b.wait(t, f"{tag}_qmh_{hp}_0")
                    for n in range(4):
                        i = t.matmul(b.pB[:, n * 512:(n + 1) * 512],
                                     b.onesm[h2 * 64:(h2 + 1) * 64, :],
                                     b.sqb[0][h2 * 64:(h2 + 1) * 64, n * 512:(n + 1) * 512],
                                     start=True, stop=True)
                    b.mark(f"{tag}_qsum_{hp}_{h2}", i, b.s_pe)

                @block.vector
                def _(v, hp=hp, h2=h2):
                    h = hp * 2 + h2
                    b.wait(v, f"{tag}_qsum_{hp}_{h2}")
                    i = v.reduce_max(b.hrow[0:1, 64 + h:65 + h], b.pB[0:1, :], axis=AX.X)
                    b.mark(f"{tag}_qmh_{hp}_{h2}", i, b.s_ve)

            @block.vector
            def _(v, hp=hp):
                b.wait(v, f"{kvtag}_maxk2", f"{tag}_qmh_{hp}_0", f"{tag}_qmh_{hp}_1")
                i = v.tensor_tensor(b.hrow[0:1, 96 + hp * 2:98 + hp * 2],
                                    b.hrow[0:1, 64 + hp * 2:66 + hp * 2],
                                    b.hrow[0:1, 16 + hp * 2:18 + hp * 2], ALU.mult)
                b.mark(f"{tag}_qmt_{hp}", i, b.s_ve)
                b.wait(v, f"{tag}_qmt_{hp}")
                i = v.tensor_scalar(b.hrow[0:1, 96 + hp * 2:98 + hp * 2],
                                    b.hrow[0:1, 96 + hp * 2:98 + hp * 2], 1.0 / 64, None, ALU.mult)
                b.mark(f"{tag}_qm_{hp}", i, b.s_ve)

            @block.scalar
            def _(s, hp=hp):
                b.wait(s, f"{tag}_qm_{hp}")
                i = s.activation(b.hrow[0:1, 128 + hp * 2:130 + hp * 2],
                                 b.hrow[0:1, 96 + hp * 2:98 + hp * 2], AF.Ln)
                b.mark(f"{tag}_mhl_{hp}", i, b.s_ac)
                s.wait_ge(b.s_ac.sem, b.s_ac.n)
                i = s.activation(b.hrow[0:1, 128 + hp * 2:130 + hp * 2],
                                 b.hrow[0:1, 128 + hp * 2:130 + hp * 2], AF.Exp, scale=0.5)
                b.mark(f"{tag}_mh_{hp}", i, b.s_ac)

            @block.tensor
            def _(t, hp=hp):
                b.wait(t, f"{tag}_mh_{hp}", f"{tag}_ebv_{hp-1}")
                for h2 in range(2):
                    h = hp * 2 + h2
                    i = t.matmul(b.pB[:, 1024 + h2 * 512:1025 + h2 * 512],
                                 b.negonesf[:], b.hrow[0:1, 128 + h:129 + h],
                                 start=True, stop=True)
                b.mark(f"{tag}_ebmm_{hp}", i, b.s_pe)

            @block.vector
            def _(v, hp=hp):
                b.wait(v, f"{tag}_ebmm_{hp}")
                for h2 in range(2):
                    h = hp * 2 + h2
                    i = v.tensor_copy(b.ebias[:, h:h + 1], b.pB[:, 1024 + h2 * 512:1025 + h2 * 512])
                b.mark(f"{tag}_ebv_{hp}", i, b.s_ve)

            if stage == "q":
                b.ev[f"{tag}_sc_done_{hp}"] = b.ev[f"{tag}_ebv_{hp}"]
                b.ev[f"{tag}_pj_{hp}"] = b.ev[f"{tag}_ebv_{hp}"]
                b.ev[f"{tag}_pvc_{hp}_1_1"] = b.ev[f"{tag}_ebv_{hp}"]
                b.ev[f"{tag}_rc_{hp}_1_1"] = b.ev[f"{tag}_ebv_{hp}"]
                continue
            prev_grp = f"{tag}_pvev_{hp-1}_1_1"
            for h2 in range(2):
                h = hp * 2 + h2
                for qg in range(2):
                    pvoff = 1024 * qg
                    for kci in range(33):
                      if kci < 32:
                        kc = kci
                        sb_ = (kc % 2) * 1024
                        vb = kc % 4

                        @block.sync
                        def _(s, hp=hp, h=h, h2=h2, qg=qg, kc=kc, vb=vb):
                            b.wait(s, b.ev.get(f"{tag}_vtfree_{vb}"))
                            if kc == 0 and h2 == 0 and qg == 0 and hp == 0:
                                b.wait(s, f"{kvtag}_agv", f"{tag}_vtones")
                            i = s.dma_start(out=vt[vb][:, 0:64],
                                            in_=b.agv_out.ap()[kc * 128:(kc + 1) * 128, h * 64:(h + 1) * 64])
                            b.mark(f"{tag}_vl_{hp}_{h2}_{qg}_{kc}", i, b.d_vt[vb], 16)

                        @block.tensor
                        def _(t, hp=hp, j=j, h2=h2, qg=qg, kc=kc, sb_=sb_, prev_grp=prev_grp):
                            b.wait(t, b.ev.get(f"{tag}_ptbfree_{kc % 2}"))
                            if kc == 0:
                                b.wait(t, prev_grp)
                                if h2 == 0 and qg == 0:
                                    b.wait(t, f"{tag}_kt_{hp}")
                            for n in range(2):
                                i = t.matmul(b.pA[:, sb_ + n * 512: sb_ + (n + 1) * 512],
                                             kt[j][h2 * 64:(h2 + 1) * 64, kc * 128:(kc + 1) * 128],
                                             qp[j][h2 * 64:(h2 + 1) * 64, qg * 1024 + n * 512: qg * 1024 + (n + 1) * 512],
                                             start=True, stop=True)
                            b.mark(f"{tag}_s_{hp}_{h2}_{qg}_{kc}", i, b.s_pe)

                        @block.scalar
                        def _(s, hp=hp, h=h, h2=h2, qg=qg, kc=kc, sb_=sb_):
                            if kc == 0 and h2 == 0 and qg == 0:
                                b.wait(s, f"{tag}_ebv_{hp}")
                            b.wait(s, f"{tag}_s_{hp}_{h2}_{qg}_{kc}", f"{tag}_pv_{hp}_{h2}_{qg}_{kc-2}")
                            i = s.activation(b.ptb[kc % 2][:], b.pA[:, sb_:sb_ + 1024], AF.Exp,
                                             scale=0.125, bias=b.ebias[:, h:h + 1])
                            b.mark(f"{tag}_exp_{hp}_{h2}_{qg}_{kc}", i, b.s_ac)

                      if kci >= 1:
                        kc = kci - 1
                        vb = kc % 4
                        if True and stage == "sc":
                            b.ev[f"{tag}_pv_{hp}_{h2}_{qg}_{kc}"] = b.ev[f"{tag}_exp_{hp}_{h2}_{qg}_{kc}"]
                            if kc == 31:
                                b.ev[f"{tag}_pvev_{hp}_{h2}_{qg}"] = b.ev[f"{tag}_exp_{hp}_{h2}_{qg}_31"]
                                prev_grp = f"{tag}_pvev_{hp}_{h2}_{qg}"
                                if h2 == 1 and qg == 1:
                                    b.ev[f"{tag}_sc_done_{hp}"] = b.ev[f"{tag}_exp_{hp}_1_1_31"]
                                    b.ev[f"{tag}_pj_{hp}"] = b.ev[f"{tag}_exp_{hp}_1_1_31"]
                                    b.ev[f"{tag}_pvc_{hp}_1_1"] = b.ev[f"{tag}_exp_{hp}_1_1_31"]
                                    b.ev[f"{tag}_rc_{hp}_1_1"] = b.ev[f"{tag}_exp_{hp}_1_1_31"]
                            continue
                        @block.tensor
                        def _(t, hp=hp, h2=h2, qg=qg, kc=kc, vb=vb, pvoff=pvoff):
                            assert f"{tag}_exp_{hp}_{h2}_{qg}_{kc}" in b.ev
                            b.wait(t, f"{tag}_exp_{hp}_{h2}_{qg}_{kc}", f"{tag}_vl_{hp}_{h2}_{qg}_{kc}")
                            for n in range(2):
                                i = t.matmul(b.pB[0:65, pvoff + n * 512: pvoff + (n + 1) * 512],
                                             vt[vb][:, :],
                                             b.ptb[kc % 2][:, n * 512:(n + 1) * 512],
                                             start=(kc == 0), stop=(kc == 31))
                            b.mark(f"{tag}_pv_{hp}_{h2}_{qg}_{kc}", i, b.s_pe)
                            b.ev[f"{tag}_vtfree_{vb}"] = (b.s_pe, b.s_pe.n)
                            b.ev[f"{tag}_ptbfree_{kc % 2}"] = (b.s_pe, b.s_pe.n)

                    @block.vector
                    def _(v, hp=hp, h2=h2, qg=qg, pvoff=pvoff):
                        b.wait(v, f"{tag}_pv_{hp}_{h2}_{qg}_31")
                        i = v.reciprocal(b.rows[64:65, 0:1024], b.pB[64:65, pvoff:pvoff + 1024])
                        b.mark(f"{tag}_rc_{hp}_{h2}_{qg}", i, b.s_ve)
                        if stage == "pv":
                            b.ev[f"{tag}_pvev_{hp}_{h2}_{qg}"] = b.ev[f"{tag}_rc_{hp}_{h2}_{qg}"]

                    if stage == "pv":
                        prev_grp = f"{tag}_pvev_{hp}_{h2}_{qg}"
                        if h2 == 1 and qg == 1:
                            b.ev[f"{tag}_sc_done_{hp}"] = b.ev[f"{tag}_pvev_{hp}_1_1"]
                            b.ev[f"{tag}_pj_{hp}"] = b.ev[f"{tag}_pvev_{hp}_1_1"]
                            b.ev[f"{tag}_pvc_{hp}_1_1"] = b.ev[f"{tag}_pvev_{hp}_1_1"]
                        continue
                    @block.tensor
                    def _(t, hp=hp, h2=h2, qg=qg):
                        b.wait(t, f"{tag}_rc_{hp}_{h2}_{qg}")
                        for n in range(2):
                            i = t.matmul(b.pA[:, n * 512:(n + 1) * 512], b.ones65[64:65, :],
                                         b.rows[64:65, n * 512:(n + 1) * 512], start=True, stop=True)
                        b.mark(f"{tag}_rcb_{hp}_{h2}_{qg}", i, b.s_pe)

                    @block.scalar
                    def _(s, hp=hp, h2=h2, qg=qg, pvoff=pvoff):
                        b.wait(s, f"{tag}_pv_{hp}_{h2}_{qg}_31")
                        i = s.activation(b.scr[0][0:64, qg * 1024:(qg + 1) * 1024],
                                         b.pB[0:64, pvoff:pvoff + 1024], AF.Copy)
                        b.mark(f"{tag}_pvc_{hp}_{h2}_{qg}", i, b.s_ac)

                    @block.vector
                    def _(v, hp=hp, j=j, h2=h2, qg=qg):
                        b.wait(v, f"{tag}_rcb_{hp}_{h2}_{qg}", f"{tag}_pvc_{hp}_{h2}_{qg}")
                        i = v.tensor_tensor(ap_[j][h2 * 64:(h2 + 1) * 64, qg * 1024:(qg + 1) * 1024],
                                            b.scr[0][0:64, qg * 1024:(qg + 1) * 1024],
                                            b.pA[0:64, 0:1024], ALU.mult)
                        b.mark(f"{tag}_pvev_{hp}_{h2}_{qg}", i, b.s_ve)
                    prev_grp = f"{tag}_pvev_{hp}_{h2}_{qg}"

            b.ev[f"{tag}_sc_done_{hp}"] = b.ev[f"{tag}_pvev_{hp}_1_1"]

            if stage == "ev":
                b.ev[f"{tag}_pj_{hp}"] = b.ev[f"{tag}_pvev_{hp}_1_1"]
            if stage in ("sc", "pv", "ev"):
                continue
            for m in range(C):
                for n in range(4):
                    @block.tensor
                    def _(t, hp=hp, j=j, m=m, n=n):
                        if m == 0 and n == 0:
                            b.wait(t, f"{tag}_pvev_{hp}_1_1", f"{tag}_pw_{hp}")
                        b.wait(t, b.ev.get(f"{tag}_pjfree_{n % 2}"))
                        i = t.matmul(b.pA[:, 1024 + (n % 2) * 512: 1024 + (n % 2 + 1) * 512],
                                     b.wsm[j][:, m * 128:(m + 1) * 128],
                                     ap_[j][:, n * 512:(n + 1) * 512], start=True, stop=True)
                        b.mark(f"{tag}_pjmm_{hp}_{m}_{n}", i, b.s_pe)

                    @block.vector
                    def _(v, hp=hp, m=m, n=n):
                        b.wait(v, f"{tag}_pjmm_{hp}_{m}_{n}")
                        i = v.scalar_tensor_tensor(
                            b.stream2[m][:, n * 512:(n + 1) * 512],
                            b.pA[:, 1024 + (n % 2) * 512: 1024 + (n % 2 + 1) * 512], 1.0,
                            b.stream2[m][:, n * 512:(n + 1) * 512], ALU.mult, ALU.add)
                        b.mark(f"{tag}_pjv_{hp}_{m}_{n}", i, b.s_ve)
                        b.ev[f"{tag}_pjfree_{n % 2}"] = (b.s_ve, b.s_ve.n)
                b.ev[f"{tag}_pjv_{hp}_{m}"] = b.ev[f"{tag}_pjv_{hp}_{m}_3"]
            b.ev[f"{tag}_pj_{hp}"] = b.ev[f"{tag}_pjv_{hp}_{C-1}"]

        b.ev[f"{tag}_done"] = b.ev[f"{tag}_pj_{pairs-1}"]
        b.ev["psfree"] = b.ev[f"{tag}_pj_{pairs-1}"]



    def p_gate(self, block, tag, deps):
        b = self
        for tt in range(NT):
            sl = (tt % 2) * 512

            @block.tensor
            def _(t, tt=tt, sl=sl):
                b.wait(t, f"{tag}_lgv_{tt-2}")
                if tt == 0:
                    b.wait(t, *deps, "psfree", "consts_g")
                for kc in range(C):
                    t.matmul(b.pA[:, sl:sl + 8],
                             b.mm_in[kc][:, tt * 128:(tt + 1) * 128],
                             b.gwb[:, kc * 8:(kc + 1) * 8],
                             start=(kc == 0), stop=False)
                i = t.matmul(b.pA[:, sl:sl + 8], b.onesbf[:], b.gbb[:],
                             start=False, stop=True)
                b.mark(f"{tag}_lg_{tt}", i, b.s_pe)

            @block.vector
            def _(v, tt=tt, sl=sl):
                b.wait(v, f"{tag}_lg_{tt}")
                i = v.tensor_copy(b.scr[0][:, tt * 8:(tt + 1) * 8], b.pA[:, sl:sl + 8])
                b.mark(f"{tag}_lgv_{tt}", i, b.s_ve)
                b.wait(v, f"{tag}_lgv_{tt}")
                i = v.reduce_max(b.scr[1][:, tt:tt + 1], b.scr[0][:, tt * 8:(tt + 1) * 8], axis=AX.X)
                b.mark(f"{tag}_lgm_{tt}", i, b.s_ve)
                b.wait(v, f"{tag}_lgm_{tt}")
                i = v.tensor_scalar(b.scr[1][:, 256 + tt * 8:256 + (tt + 1) * 8],
                                    b.scr[0][:, tt * 8:(tt + 1) * 8],
                                    b.scr[1][:, tt:tt + 1], None, ALU.is_equal)
                b.mark(f"{tag}_msk_{tt}", i, b.s_ve)

            @block.tensor
            def _(t, tt=tt, sl=sl):
                b.wait(t, f"{tag}_msk_{tt}", f"{tag}_mfm_{tt-2}")
                i = t.transpose(b.pB[0:8, sl:sl + 128],
                                b.scr[1][:, 256 + tt * 8:256 + (tt + 1) * 8], b.identt[:])
                b.mark(f"{tag}_tr_{tt}", i, b.s_pe)

            @block.vector
            def _(v, tt=tt, sl=sl):
                b.wait(v, f"{tag}_tr_{tt}")
                i = v.tensor_copy(b.maskfm[0:8, tt * 128:(tt + 1) * 128], b.pB[0:8, sl:sl + 128])
                b.mark(f"{tag}_mfm_{tt}", i, b.s_ve)

        b.ev[f"{tag}_done"] = b.ev[f"{tag}_mfm_{NT-1}"]
        b.ev["psfree"] = b.ev[f"{tag}_mfm_{NT-1}"]

    def p_moe(self, block, tag, st, deps):
        b = self
        hq = [st.enter_context(self.nc.sbuf_tensor(f"{tag}_hq_{i}", [128, T], BF16)) for i in range(8)]
        w2q = st.enter_context(self.nc.sbuf_tensor(f"{tag}_w2q", [128, 8 * 1024], BF16))

        for e in range(E):
            @block.tensor
            def _(t, e=e):
                b.wait(t, f"{tag}_ye_{e-1}_3")
                if e == 0:
                    b.wait(t, *deps, "psfree")
                for n in range(4):
                    i = t.matmul(b.pA[:, n * 512:(n + 1) * 512],
                                 b.ind8t[0:8, e * 128:(e + 1) * 128],
                                 b.maskfm[0:8, n * 512:(n + 1) * 512], start=True, stop=True)
                b.mark(f"{tag}_mb_{e}", i, b.s_pe)

            @block.vector
            def _(v, e=e):
                b.wait(v, f"{tag}_mb_{e}")
                i = v.tensor_copy(b.maske[:], b.pA[:, :])
                b.mark(f"{tag}_me_{e}", i, b.s_ve)

            for q in range(4):
                @block.gpsimd
                def _(g, e=e, q=q):
                    if e == 0 and q == 0:
                        b.wait(g, *deps)
                    b.wait(g, f"{tag}_ye_{e}_{q-1}" if q > 0 else (f"{tag}_ye_{e-1}_3" if e > 0 else None))
                    i = g.dma_start(
                        out=w2q[:].rearrange("p (kc m) -> p kc m", kc=8),
                        in_=b.w2.ap()[e, q * 1024:(q + 1) * 1024, :].rearrange("(kc p) m -> p kc m", p=128),
                    )
                    b.mark(f"{tag}_w2_{e}_{q}", i, b.d_w2, 16)

                for hs in range(8):
                    j = hs % 2
                    pp = [b.pA, b.pB][hs % 2]

                    @block.gpsimd
                    def _(g, e=e, q=q, hs=hs, j=j):
                        b.wait(g, f"{tag}_h_{e}_{q}_{hs-2}")
                        if hs < 2:
                            pe_, pq_ = (e, q - 1) if q > 0 else (e - 1, 3)
                            b.wait(g, f"{tag}_hmm_{pe_}_{pq_}_{hs + 6}")
                            if q == 0 and e == 0:
                                b.wait(g, *deps)
                        i = g.dma_start(
                            out=b.w1b[j][:].rearrange("p (kc m) -> p kc m", kc=C),
                            in_=b.w1.ap()[e, :, q * 1024 + hs * 128: q * 1024 + (hs + 1) * 128]
                                .rearrange("(kc p) m -> p kc m", p=128),
                        )
                        b.mark(f"{tag}_w1_{e}_{q}_{hs}", i, b.d_w1b[j], 16)

                    @block.tensor
                    def _(t, e=e, q=q, hs=hs, j=j, pp=pp):
                        b.wait(t, f"{tag}_w1_{e}_{q}_{hs}", f"{tag}_h_{e}_{q}_{hs-2}")
                        if hs < 2:
                            pe_, pq_ = (e, q - 1) if q > 0 else (e - 1, 3)
                            b.wait(t, f"{tag}_h_{pe_}_{pq_}_{hs + 6}")
                            b.wait(t, f"{tag}_me_{e}" if hs == 0 else None,
                                   f"{tag}_yv_{e}_{q-1}" if q > 0 else (f"{tag}_yv_{e-1}_3" if e > 0 else None))
                        for kc in range(C):
                            for n in range(4):
                                i = t.matmul(pp[:, n * 512:(n + 1) * 512],
                                             b.w1b[j][:, kc * 128:(kc + 1) * 128],
                                             b.mm_in[kc][:, n * 512:(n + 1) * 512],
                                             start=(kc == 0), stop=(kc == C - 1))
                        b.mark(f"{tag}_hmm_{e}_{q}_{hs}", i, b.s_pe)

                    @block.scalar
                    def _(s, e=e, q=q, hs=hs, pp=pp):
                        b.wait(s, f"{tag}_hmm_{e}_{q}_{hs}",
                               f"{tag}_ymm_{e}_{q-1}" if q > 0 else (f"{tag}_ymm_{e-1}_3" if e > 0 else None))
                        i = s.activation(hq[hs][:], pp[:, :], AF.Gelu_apprx_tanh)
                        b.mark(f"{tag}_h_{e}_{q}_{hs}", i, b.s_ac)

                for m in range(C):
                    for tg in range(4):
                        slot = ((m * 4 + tg) % 2) * 512

                        @block.tensor
                        def _(t, e=e, q=q, m=m, tg=tg, slot=slot):
                            if m == 0 and tg == 0:
                                b.wait(t, f"{tag}_w2_{e}_{q}")
                            b.wait(t, b.ev.get(f"{tag}_yfree_{slot}"))
                            for hs in range(8):
                                if m == 0 and tg == 0:
                                    b.wait(t, f"{tag}_h_{e}_{q}_{hs}")
                                i = t.matmul(b.pA[:, slot:slot + 512],
                                             w2q[:, hs * 1024 + m * 128: hs * 1024 + (m + 1) * 128],
                                             hq[hs][:, tg * 512:(tg + 1) * 512],
                                             start=(hs == 0), stop=(hs == 7))
                            b.mark(f"{tag}_ymm_{e}_{q}_{m}_{tg}", i, b.s_pe)
                            if m == C - 1 and tg == 3:
                                b.ev[f"{tag}_ymm_{e}_{q}"] = (b.s_pe, b.s_pe.n)

                        @block.vector
                        def _(v, e=e, q=q, m=m, tg=tg, slot=slot):
                            b.wait(v, f"{tag}_ymm_{e}_{q}_{m}_{tg}")
                            v.tensor_tensor(b.scr[0][:, slot:slot + 512], b.pA[:, slot:slot + 512],
                                            b.maske[:, tg * 512:(tg + 1) * 512], ALU.mult)
                            i = v.tensor_tensor(b.stream2[m][:, tg * 512:(tg + 1) * 512],
                                                b.scr[0][:, slot:slot + 512],
                                                b.stream2[m][:, tg * 512:(tg + 1) * 512], ALU.add)
                            b.mark(f"{tag}_yv_{e}_{q}_{m}_{tg}", i, b.s_ve)
                            b.ev[f"{tag}_yfree_{slot}"] = (b.s_ve, b.s_ve.n)
                            if m == C - 1 and tg == 3:
                                b.ev[f"{tag}_yv_{e}_{q}"] = (b.s_ve, b.s_ve.n)
                                b.ev[f"{tag}_ye_{e}_{q}"] = (b.s_ve, b.s_ve.n)

            b.ev[f"{tag}_ye_{e}_3"] = b.ev[f"{tag}_ye_{e}_3"]
        b.ev[f"{tag}_done"] = b.ev[f"{tag}_ye_{E-1}_3"]
        b.ev["psfree"] = b.ev[f"{tag}_done"]

    def p_lin(self, block, tag, deps, xdeps=lambda m: ()):
        b = self
        groups = [[0, 1], [2, 3], [4, 5], [6, 7]]

        for ck in range(C):
            j = ck % 2

            @block.gpsimd
            def _(g, ck=ck, j=j):
                b.wait(g, f"{tag}_qmm_{ck-2}")
                if ck == 0:
                    b.wait(g, *deps)
                i = g.dma_start(
                    out=b.w1b[j][:].rearrange("p (kc m) -> p kc m", kc=C),
                    in_=b.awk.ap()[:, ck * 128:(ck + 1) * 128].rearrange("(kc p) m -> p kc m", p=128))
                b.mark(f"{tag}_wk_{ck}", i, b.d_w1b[j], 16)
                i = g.dma_start(
                    out=b.wsm[j][:].rearrange("p (kc m) -> p kc m", kc=C),
                    in_=b.awv.ap()[:, ck * 128:(ck + 1) * 128].rearrange("(kc p) m -> p kc m", p=128))
                b.mark(f"{tag}_wv_{ck}", i, b.d_wsm[j], 16)
                b.wait(g, f"{tag}_qs_{ck-2}")
                i = g.dma_start(
                    out=b.xb[j][:, 0:1024].rearrange("p (kc m) -> p kc m", kc=C),
                    in_=b.awq.ap()[:, ck * 128:(ck + 1) * 128].rearrange("(kc p) m -> p kc m", p=128))
                b.mark(f"{tag}_wq_{ck}", i, b.d_xb[j], 16)

            @block.tensor
            def _(t, ck=ck, j=j):
                b.wait(t, f"{tag}_wk_{ck}", f"{tag}_qS_{ck-1}")
                if ck == 0:
                    b.wait(t, *deps, "psfree")
                for kc in range(C):
                    for n in range(4):
                        i = t.matmul(b.pA[:, n * 512:(n + 1) * 512],
                                     b.w1b[j][:, kc * 128:(kc + 1) * 128],
                                     b.mm_in[kc][:, n * 512:(n + 1) * 512],
                                     start=(kc == 0), stop=(kc == C - 1))
                b.mark(f"{tag}_kmm_{ck}", i, b.s_pe)
                b.wait(t, f"{tag}_wv_{ck}", f"{tag}_vS_{ck-1}")
                for kc in range(C):
                    for n in range(4):
                        i = t.matmul(b.pB[:, n * 512:(n + 1) * 512],
                                     b.wsm[j][:, kc * 128:(kc + 1) * 128],
                                     b.mm_in[kc][:, n * 512:(n + 1) * 512],
                                     start=(kc == 0), stop=(kc == C - 1))
                b.mark(f"{tag}_vmm_{ck}", i, b.s_pe)

            @block.scalar
            def _(s, ck=ck, j=j):
                b.wait(s, f"{tag}_kmm_{ck}", f"{tag}_kv_{ck-1}")
                s.activation(b.scr[0][:], b.pA[:, :], AF.Relu, scale=-1.0, bias=b.ncol("abk", ck))
                s.activation(b.scr[1][:], b.scr[0][:], AF.Exp, scale=-1.0)
                i = s.activation(b.scr[0][:], b.pA[:, :], AF.Relu, bias=self.col("abk", ck))
                b.mark(f"{tag}_kS_{ck}", i, b.s_ac)

            @block.vector
            def _(v, ck=ck):
                b.wait(v, f"{tag}_vmm_{ck}", f"{tag}_kv_{ck-1}")
                i = v.tensor_scalar(b.sqb[1][:], b.pB[:, :], self.col("abv", ck), None, ALU.add)
                b.mark(f"{tag}_vS_{ck}", i, b.s_ve)
                b.wait(v, f"{tag}_kS_{ck}")
                i = v.scalar_tensor_tensor(b.sqb[0][:], b.scr[1][:], 1.0, b.scr[0][:],
                                           ALU.mult, ALU.add,
                                           accum_out=b.lincols[:, 8 + ck:9 + ck])
                b.mark(f"{tag}_kA_{ck}", i, b.s_ve)
                b.wait(v, f"{tag}_kA_{ck}")
                i = v.scalar_tensor_tensor(b.scr[0][:], b.sqb[0][:], 1.0,
                                           b.sqb[1][:], ALU.mult, ALU.mult,
                                           accum_out=b.lincols[:, ck:ck + 1])
                b.mark(f"{tag}_kv0_{ck}", i, b.s_ve)

            @block.tensor
            def _(t, ck=ck, j=j):
                b.wait(t, f"{tag}_wq_{ck}", f"{tag}_kS_{ck}")
                for kc in range(C):
                    for n in range(4):
                        i = t.matmul(b.pA[:, n * 512:(n + 1) * 512],
                                     b.xb[j][:, kc * 128:(kc + 1) * 128],
                                     b.mm_in[kc][:, n * 512:(n + 1) * 512],
                                     start=(kc == 0), stop=(kc == C - 1))
                b.mark(f"{tag}_qmm_{ck}", i, b.s_pe)

            @block.scalar
            def _(s, ck=ck):
                b.wait(s, f"{tag}_qmm_{ck}", f"{tag}_qA_{ck-1}", f"{tag}_kv0_{ck}")
                s.activation(b.scr[0][:], b.pA[:, :], AF.Relu, scale=-1.0, bias=b.ncol("abq", ck))
                s.activation(b.scr[1][:], b.scr[0][:], AF.Exp, scale=-1.0)
                i = s.activation(b.scr[0][:], b.pA[:, :], AF.Relu, bias=self.col("abq", ck))
                b.mark(f"{tag}_qS_{ck}", i, b.s_ac)

            @block.vector
            def _(v, ck=ck):
                b.wait(v, f"{tag}_qS_{ck}")
                i = v.scalar_tensor_tensor(b.stream2[ck][:], b.scr[1][:], 1.0, b.scr[0][:],
                                           ALU.mult, ALU.add)
                b.mark(f"{tag}_qA_{ck}", i, b.s_ve)
                b.ev[f"{tag}_qs_{ck}"] = b.ev[f"{tag}_qA_{ck}"]
                b.ev[f"{tag}_kv_{ck}"] = b.ev[f"{tag}_qA_{ck}"]

        @block.sync
        def _(s):
            b.wait(s, f"{tag}_kv0_{C-1}")
            i = s.dma_start(out=b.arl_in.ap(), in_=b.lincols[:, 0:16])
            b.mark(f"{tag}_ard", i, b.d_msc[0], 16)

        @block.gpsimd
        def _(g):
            b.wait(g, f"{tag}_ard")
            i = g.collective_compute("AllReduce", ALU.add, replica_groups=groups,
                                     ins=[b.arl_in.ap().opt()], outs=[b.arl_out.ap().opt()])
            b.mark(f"{tag}_ar", i, b.c_l)

        @block.sync
        def _(s):
            b.wait(s, f"{tag}_ar")
            i = s.dma_start(out=b.lincols[:, 16:32], in_=b.arl_out.ap())
            b.mark(f"{tag}_art", i, b.d_msc[1], 16)

        for ck in range(C):
            @block.vector
            def _(v, ck=ck):
                b.wait(v, f"{tag}_art" if ck == 0 else None)
                v.tensor_scalar(b.scr[0][:], b.stream2[ck][:],
                                b.lincols[:, 24 + ck:25 + ck], EPS_LA, ALU.mult, ALU.add)
                v.reciprocal(b.scr[1][:], b.scr[0][:])
                v.tensor_tensor(b.scr[0][:], b.stream2[ck][:], b.scr[1][:], ALU.mult)
                i = v.tensor_scalar(b.mm_in[ck][:], b.scr[0][:],
                                    b.lincols[:, 16 + ck:17 + ck], None, ALU.mult)
                b.mark(f"{tag}_cb_{ck}", i, b.s_ve)

        for m in range(C):
            j = m % 2

            @block.gpsimd
            def _(g, m=m, j=j):
                b.wait(g, f"{tag}_pmm_{m-2}")
                i = g.dma_start(
                    out=b.w1b[j][:].rearrange("p (kc m) -> p kc m", kc=C),
                    in_=b.awo.ap()[:, m * 128:(m + 1) * 128].rearrange("(kc p) m -> p kc m", p=128))
                b.mark(f"{tag}_wo_{m}", i, b.d_w1b[j], 16)

            @block.sync
            def _(s, m=m):
                b.wait(s, f"{tag}_xw_{m-1}", f"{tag}_xa_{m-1}", *xdeps(m))
                i = s.dma_start(out=b.maske[:], in_=b.xdram.ap()[m * 128:(m + 1) * 128, :])
                b.mark(f"{tag}_xr_{m}", i, b.d_rr[m % 2], 16)

            @block.tensor
            def _(t, m=m, j=j):
                b.wait(t, f"{tag}_wo_{m}", f"{tag}_pg_{m-2}")
                if m == 0:
                    b.wait(t, f"{tag}_cb_{C-1}")
                pp = [b.pA, b.pB][m % 2]
                for kc in range(C):
                    for n in range(4):
                        i = t.matmul(pp[:, n * 512:(n + 1) * 512],
                                     b.w1b[j][:, kc * 128:(kc + 1) * 128],
                                     b.mm_in[kc][:, n * 512:(n + 1) * 512],
                                     start=(kc == 0), stop=(kc == C - 1))
                b.mark(f"{tag}_pmm_{m}", i, b.s_pe)

            @block.scalar
            def _(s, m=m):
                b.wait(s, f"{tag}_pmm_{m}", f"{tag}_xa_{m-2}")
                pp = [b.pA, b.pB][m % 2]
                i = s.activation(b.scr[m % 2][:], pp[:, :], AF.Gelu_apprx_tanh,
                                 bias=self.col("abo", m))
                b.mark(f"{tag}_pg_{m}", i, b.s_ac)

            @block.vector
            def _(v, m=m):
                b.wait(v, f"{tag}_pg_{m}", f"{tag}_xr_{m}")
                i = v.tensor_tensor(b.maske[:], b.scr[m % 2][:], b.maske[:], ALU.add)
                b.mark(f"{tag}_xa_{m}", i, b.s_ve)

            @block.sync
            def _(s, m=m):
                b.wait(s, f"{tag}_xa_{m}")
                i = s.dma_start(out=b.xdram.ap()[m * 128:(m + 1) * 128, :], in_=b.maske[:])
                b.mark(f"{tag}_xw_{m}", i, b.d_rw[m % 2], 16)

        b.ev[f"{tag}_done"] = b.ev[f"{tag}_xw_{C-1}"]
        b.ev["psfree"] = b.ev[f"{tag}_pg_{C-1}"]

    def emit(self):
        nc = self.nc
        with contextlib.ExitStack() as st:
            self.alloc(st)
            self.ptb = [st.enter_context(nc.sbuf_tensor(f"ptb_{i}", [128, 1024], BF16)) for i in range(2)]
            with nc.Block() as block:
                self.p_consts(block)
                self.p_ln(block, "ln0a", "x", "s2", deps=("consts_sy",))
                if self.upto == "y1":
                    self.p_dump(block, self.stream2, ("ln0a_done",))
                    return
                self.p_ln(block, "ln1a", "s2", "mm", affine=("ln1w", "ln1b"),
                          deps=("ln0a_done",))
                if self.upto == "ln1":
                    self.p_dump(block, self.mm_in, ("ln1a_done",))
                    return
                with contextlib.ExitStack() as msa_st:
                    if self.upto in ("k1", "cc1"):
                        self.p_kv(block, "kva", msa_st, partial="k" if self.upto == "k1" else "cc1")
                        self.p_dump(block, self.mm_in,
                                    ("kva_kd_6", "kva_kd_7") if self.upto == "k1" else ("kva_agm",))
                        return
                    self.p_kv(block, "kva", msa_st)
                    if self.upto == "k0":
                        self.p_dump(block, self.mm_in, ("kva_kd_7", "kva_md"))
                        return
                    if self.upto == "kv0":
                        self.p_dump(block, self.mm_in, ("kva_vd_15",))
                        return
                    if self.upto == "kv":
                        self.p_dump(block, self.mm_in, ("kva_agv", "kva_maxk2"))
                        return
                    if self.upto == "mh":
                        self.p_msa(block, "msaa", "kva", msa_st, pairs=8, stage="q")
                        b = self

                        @block.sync
                        def _(s):
                            b.wait(s, "msaa_ebv_7")
                            i = s.dma_start(out=b.dbg.ap()[0:1, 0:256], in_=b.hrow[:])
                            b.d_msc[0].inc(i, 16)
                            i = s.dma_start(out=b.dbg.ap()[100:228, 0:16], in_=b.ebias[:])
                            b.mark("mhdump", i, b.d_msc[0], 16)
                            b.wait(s, "mhdump")
                        return
                    if self.upto.startswith("msa_") or self.upto == "msa1":
                        pairs = 1 if self.upto.endswith("1") else 8
                        stage = {"msa_q": "q", "msa_sc": "sc", "msa_pv": "pv",
                                 "msa_q1": "q", "msa_sc1": "sc", "msa_pv1": "pv",
                                 "msa_ev1": "ev", "msa1": "all"}[self.upto]
                        self.p_msa(block, "msaa", "kva", msa_st, pairs=pairs, stage=stage)
                        self.p_dump(block, self.mm_in, ("msaa_pj_%d" % (pairs - 1),))
                        return
                    self.p_msa(block, "msaa", "kva", msa_st)
                if self.upto == "msa":
                    self.p_dump(block, self.stream2, ("msaa_done",))
                    return
                self.p_ln(block, "ln2a", "s2", "mm", affine=("ln2w", "ln2b"),
                          deps=("msaa_done",))
                self.p_gate(block, "gatea", deps=("ln2a_done",))
                if self.upto == "gate":
                    b = self

                    @block.sync
                    def _(s):
                        b.wait(s, "gatea_done")
                        i = s.dma_start(out=b.dbg.ap()[0:8, :], in_=b.maskfm[:])
                        b.d_msc[0].inc(i, 16)
                        i = s.dma_start(out=b.dbg.ap()[16:144, 0:128], in_=b.scr[0][:, 0:128])
                        b.mark("gdump", i, b.d_msc[0], 16)
                        b.wait(s, "gdump")
                    return
                with contextlib.ExitStack() as moe_st:
                    self.p_moe(block, "moea", moe_st, deps=("gatea_done", "ln2a_done"))
                if self.upto == "moe":
                    self.p_dump(block, self.stream2, ("moea_done",))
                    return
                self.p_resid(block, "r1", 0.5, deps=("moea_done",))
                self.p_ln(block, "ln0b", "x", "mm", deps=lambda c: (f"r1_w_{c}",))
                self.p_lin(block, "linb", deps=("ln0b_done",),
                           xdeps=lambda m: (f"r1_w_{m}",))
                if self.upto == "lin":
                    self.p_dump_dram(block, lambda c: (f"linb_xw_{c}",))
                    return
                self.p_ln(block, "ln0c", "x", "s2", deps=lambda c: (f"linb_xw_{c}",))
                self.p_ln(block, "ln1c", "s2", "mm", affine=("ln1w", "ln1b"),
                          deps=("ln0c_done",))
                with contextlib.ExitStack() as msa_st2:
                    self.p_kv(block, "kvb", msa_st2, deps=("linb_done", "moea_done"))
                    self.p_msa(block, "msab", "kvb", msa_st2, deps=("linb_done", "moea_done"))
                self.p_ln(block, "ln2b", "s2", "mm", affine=("ln2w", "ln2b"),
                          deps=("msab_done",))
                self.p_gate(block, "gateb", deps=("ln2b_done",))
                with contextlib.ExitStack() as moe_st2:
                    self.p_moe(block, "moeb", moe_st2, deps=("gateb_done", "ln2b_done"))
                self.p_resid(block, "r2", 0.5, final=True,
                             deps=lambda c: ("moeb_done", f"linb_xw_{c}"))

                @block.sync
                def _(s):
                    self.wait(s, "r2_w_6", "r2_w_7")

    def p_dump_dram(self, block, depf):
        b = self

        @block.sync
        def _(s):
            for c in range(C):
                b.wait(s, *depf(c))
            i = s.dma_start(out=b.dbg.ap(), in_=b.xdram.ap())
            b.mark("dmpd", i, b.d_msc[0], 16)
            b.wait(s, "dmpd")

    def p_dump(self, block, tensors, deps):
        b = self
        for c in range(C):
            i2 = c % 2

            @block.vector
            def _(v, c=c, i2=i2):
                b.wait(v, *deps)
                b.wait(v, f"dmp_w_{c-2}")
                i = v.tensor_copy(b.scr[i2][:], tensors[c][:])
                b.mark(f"dmp_v_{c}", i, b.s_ve)

            @block.sync
            def _(s, c=c, i2=i2):
                b.wait(s, f"dmp_v_{c}")
                i = s.dma_start(out=b.dbg.ap()[c * 128:(c + 1) * 128, :], in_=b.scr[i2][:])
                b.mark(f"dmp_w_{c}", i, b.d_rw[i2], 16)

        @block.sync
        def _(s):
            b.wait(s, f"dmp_w_{C-2}", f"dmp_w_{C-1}")


def build_nc(upto="full"):
    nc = bass.Bass("TRN2", target_bir_lowering=False, debug=False,
                   detect_race_conditions=False)
    bb = B(nc, upto)
    bb.declare()
    bb.emit()
    return nc


def make_core_inputs(inputs, core):
    b_, hf = core // 2, core % 2
    f32 = np.float32
    bfl = ml_dtypes.bfloat16
    x = np.asarray(inputs["L0_attn_x"])
    xs = np.ascontiguousarray(x[b_, hf * T:(hf + 1) * T, :].T).astype(f32)
    biascols = np.concatenate([
        np.asarray(inputs["L0_attn_bq"]).reshape(D, 1),
        np.asarray(inputs["L0_attn_bk"]).reshape(D, 1),
        np.asarray(inputs["L0_attn_bv"]).reshape(D, 1),
        np.asarray(inputs["L0_attn_bo"]).reshape(D, 1),
        np.asarray(inputs["L0_ffn_ln1_w"]).reshape(D, 1),
        np.asarray(inputs["L0_ffn_ln1_b"]).reshape(D, 1),
        np.asarray(inputs["L0_ffn_ln2_w"]).reshape(D, 1),
        np.asarray(inputs["L0_ffn_ln2_b"]).reshape(D, 1),
    ], axis=1).astype(f32)
    return {
        "x_in": xs,
        "awq": np.ascontiguousarray(np.asarray(inputs["L0_attn_wq"]).reshape(D, D)).astype(f32),
        "awk": np.ascontiguousarray(np.asarray(inputs["L0_attn_wk"]).reshape(D, D)).astype(f32),
        "awv": np.ascontiguousarray(np.asarray(inputs["L0_attn_wv"]).reshape(D, D)).astype(f32),
        "awo": np.ascontiguousarray(np.asarray(inputs["L0_attn_wo"]).reshape(D, D)).astype(f32),
        "wqkv": np.ascontiguousarray(np.asarray(inputs["L0_ffn_w_qkv"])).astype(f32),
        "wproj": np.ascontiguousarray(np.asarray(inputs["L0_ffn_w_proj"])).astype(f32),
        "biascols": np.ascontiguousarray(biascols),
        "gate_w": np.ascontiguousarray(np.asarray(inputs["L0_ffn_gate_w"])).astype(f32),
        "gate_b": np.ascontiguousarray(np.asarray(inputs["L0_ffn_gate_b"]).reshape(1, E)).astype(f32),
        "w1": np.ascontiguousarray(np.asarray(inputs["L0_ffn_w1"])).astype(f32),
        "w2": np.ascontiguousarray(np.asarray(inputs["L0_ffn_w2"])).astype(f32),
        "ones_f": np.ones((1, 128), f32),
        "negones_f": np.full((1, 128), -1.0, f32),
        "ones_bf": np.ones((1, 128), bfl),
        "ones65_f": np.concatenate([np.ones((32, 128), f32), -np.ones((32, 128), f32), np.ones((1, 128), f32)], axis=0),
        "onescol_bf": np.ones((128, 1), bfl),
        "onesm_bf": np.ones((128, 128), bfl),
        "ident": np.eye(128, dtype=f32),
        "ind8": np.kron(np.eye(E, dtype=f32), np.ones((1, 128), f32)),
    }


def kernel(**inputs):
    nc = build_nc("full")
    in_maps = [make_core_inputs(inputs, c) for c in range(8)]
    res = run_bass_kernel_spmd(nc, in_maps, core_ids=list(range(8)))
    out = np.zeros((4, SEQ, D), np.float32)
    for c in range(8):
        b_, hf = c // 2, c % 2
        out[b_, hf * T:(hf + 1) * T, :] = res.results[c]["x_out"].T
    return out
